# version 09
# baseline (speedup 1.0000x reference)
import numpy as np
import ml_dtypes
import concourse.bass as bass
import concourse.mybir as mybir
from concourse.bass_utils import run_bass_kernel_spmd
from concourse.alu_op_type import AluOpType as ALU

F32 = mybir.dt.float32
BF16 = mybir.dt.bfloat16
I32 = mybir.dt.int32
AF = mybir.ActivationFunctionType
AX = mybir.AxisListType


STREAMS = ("t", "v", "s", "p", "y")


class _Op:
    __slots__ = ("idx", "stream", "emit", "reads", "writes", "dma", "tag",
                 "deps", "need_inc", "cmp_seq", "dma_seq", "snapshot",
                 "ch", "batch")

    def __init__(self, idx, stream, emit, reads, writes, dma, tag):
        self.idx = idx
        self.stream = stream
        self.emit = emit
        self.reads = tuple(reads)
        self.writes = tuple(writes)
        self.dma = dma
        self.tag = tag
        self.deps = None
        self.need_inc = False
        self.cmp_seq = None
        self.dma_seq = None
        self.snapshot = None
        self.ch = None
        self.batch = False


class Prog:
    def __init__(self):
        self.ops = []
        self.last_write = {}
        self.readers = {}

    def op(self, stream, emit, reads=(), writes=(), dma=False, tag="",
           ch=None, batch=False):
        i = len(self.ops)
        o = _Op(i, stream, emit, reads, writes, dma, tag)
        o.ch = ch
        o.batch = batch
        deps = set()
        for k in o.reads:
            w = self.last_write.get(k)
            if w is not None:
                deps.add(w)
        for k in o.writes:
            w = self.last_write.get(k)
            if w is not None:
                deps.add(w)
            for r in self.readers.get(k, ()):
                deps.add(r)
        deps.discard(i)
        o.deps = deps
        self.ops.append(o)
        for k in o.reads:
            self.readers.setdefault(k, []).append(i)
        for k in o.writes:
            self.last_write[k] = i
            self.readers[k] = []
        return i

    def t(self, emit, reads=(), writes=(), tag=""):
        return self.op("t", emit, reads, writes, False, tag)

    def v(self, emit, reads=(), writes=(), tag=""):
        return self.op("v", emit, reads, writes, False, tag)

    def s(self, emit, reads=(), writes=(), tag=""):
        return self.op("s", emit, reads, writes, False, tag)

    def p(self, emit, reads=(), writes=(), tag=""):
        return self.op("p", emit, reads, writes, False, tag)

    def dma(self, stream, emit, reads=(), writes=(), tag="", ch=None,
            batch=False):
        assert ch is not None, f"dma {tag} needs a channel"
        return self.op(stream, emit, reads, writes, True, tag, ch=ch,
                       batch=batch)

    def build(self, nc, final_stream="y"):
        ops = self.ops

        def skip_edge(prod, cons):
            return (prod.stream == cons.stream == "t"
                    and not prod.dma and not cons.dma)

        for o in ops:
            for d in o.deps:
                p_ = ops[d]
                if not skip_edge(p_, o):
                    p_.need_inc = True

        cmp_count = {st: 0 for st in STREAMS}
        ch_count = {}
        ch_stream = {}
        ch_batch = {}
        for o in ops:
            if o.dma:
                ch_count[o.ch] = ch_count.get(o.ch, 0) + 1
                o.dma_seq = ch_count[o.ch]
                if o.ch in ch_stream:
                    assert ch_stream[o.ch] == o.stream, \
                        f"channel {o.ch} used from two queues"
                ch_stream[o.ch] = o.stream
                ch_batch[o.ch] = ch_batch.get(o.ch, False) or o.batch
            elif o.need_inc:
                cmp_count[o.stream] += 1
                o.cmp_seq = cmp_count[o.stream]

        def target(o):
            if o.dma:
                if ch_batch[o.ch]:
                    return ("ch", o.ch), 16 * ch_count[o.ch]
                return ("ch", o.ch), 16 * o.dma_seq
            if o.cmp_seq is not None:
                return (o.stream, "c"), o.cmp_seq
            return None, None

        watermark = {st: {} for st in STREAMS}
        waits = {o.idx: [] for o in ops}
        for o in ops:
            wm = watermark[o.stream]
            req = {}
            snaps = {}
            if o.dma and not ch_batch[o.ch] and o.dma_seq > 1:
                req[("ch", o.ch)] = 16 * (o.dma_seq - 1)
                snaps[("ch", o.ch)] = None
            for d in sorted(o.deps):
                p_ = ops[d]
                if skip_edge(p_, o):
                    continue
                key, cnt = target(p_)
                if key is None:
                    raise AssertionError(f"dep without inc: {p_.tag} -> {o.tag}")
                if req.get(key, 0) < cnt:
                    req[key] = cnt
                    snaps[key] = p_.snapshot
            for key, cnt in sorted(req.items()):
                if wm.get(key, 0) >= cnt:
                    continue
                waits[o.idx].append((key, cnt))
                wm[key] = cnt
                sn = snaps[key]
                if sn:
                    for k2, c2 in sn.items():
                        if wm.get(k2, 0) < c2:
                            wm[k2] = c2
            if o.need_inc or o.dma:
                key, cnt = target(o)
                sn = dict(wm)
                sn[key] = cnt
                o.snapshot = sn

        sems = {}
        ctxs = []
        for st in STREAMS:
            if cmp_count[st]:
                cm = nc.semaphore(f"sem_{st}c")
                sems[(st, "c")] = cm.__enter__()
                ctxs.append(cm)
        for ch in ch_count:
            cm = nc.semaphore(f"sem_ch_{ch}")
            sems[("ch", ch)] = cm.__enter__()
            ctxs.append(cm)

        stream_ops = {st: [o for o in ops if o.stream == st] for st in STREAMS}
        final_counts = []
        for st in STREAMS:
            if cmp_count[st]:
                final_counts.append(((st, "c"), cmp_count[st]))
        for ch, n in ch_count.items():
            final_counts.append((("ch", ch), 16 * n))

        block_cm = nc.Block()
        block = block_cm.__enter__()
        ctxs.append(block_cm)

        def emit_stream(eng, st):
            for o in stream_ops[st]:
                for key, cnt in waits[o.idx]:
                    eng.wait_ge(sems[key], cnt)
                inst = o.emit(eng)
                if o.dma:
                    inst.then_inc(sems[("ch", o.ch)], 16)
                elif o.need_inc:
                    if inst is None:
                        raise AssertionError(f"op {o.tag} returned no inst")
                    inst.then_inc(sems[(o.stream, "c")], 1)
            if st == final_stream:
                for key, cnt in final_counts:
                    eng.wait_ge(sems[key], cnt)

        if stream_ops["y"] or final_stream == "y":
            @block.sync
            def _(sy):
                emit_stream(sy, "y")
        if stream_ops["t"]:
            @block.tensor
            def _(t):
                emit_stream(t, "t")
        if stream_ops["v"]:
            @block.vector
            def _(v):
                emit_stream(v, "v")
        if stream_ops["s"]:
            @block.scalar
            def _(s):
                emit_stream(s, "s")
        if stream_ops["p"]:
            @block.gpsimd
            def _(g):
                emit_stream(g, "p")
        return ctxs


S = 2048
D = 2048
NT = S // 128
ND = D // 128
EPS = 1e-5


def _tiled_table(x):
    w = x.shape[1]
    return np.ascontiguousarray(
        x.reshape(NT, 128, w).transpose(1, 0, 2).reshape(128, NT * w)
    ).astype(np.float32)


def host_constants():
    c = {}
    c["c_ident"] = np.eye(128, dtype=np.float32)
    tril = np.tril(np.ones((128, 128), np.float32))
    c["c_mask9"] = ((1.0 - tril) * (-1e9)).astype(np.float32)
    c["c_maskinf"] = ((1.0 - tril) * (-1e30)).astype(np.float32)
    c["c_ones"] = np.ones((1, 512), np.float32)
    c["c_bias"] = np.tile(np.array([[0.0, EPS, 1e-38]], np.float32), (128, 1))
    c["c_onescol"] = np.ones((128, 4), np.float32)

    t = np.arange(S, dtype=np.float64)
    f = (8000.0 / 200.0) ** np.linspace(0.0, 1.0, 64) * 200.0 / 1000.0
    fr = 200.0 * f / 1000.0
    ang = np.outer(t, fr)
    c["c_l0a_cos"] = _tiled_table(np.cos(ang))
    c["c_l0a_sin"] = _tiled_table(np.sin(ang))
    freqs = 1.0 / 10000.0 ** (np.arange(32, dtype=np.float64) / 32.0)
    ang1 = np.outer(t, freqs)
    c["c_l1a_cos"] = _tiled_table(np.repeat(np.cos(ang1), 2, axis=-1))
    c["c_l1a_sin"] = _tiled_table(np.repeat(np.sin(ang1), 2, axis=-1))
    freqsP = np.repeat(np.outer(np.arange(S, dtype=np.float64), freqs), 2, axis=-1)
    c["c_l1a_cosP"] = np.ascontiguousarray(np.cos(0 * freqsP[:1]))
    c["c_l1a_cosP"] = np.ascontiguousarray(
        np.repeat(np.cos(ang1), 2, axis=-1).T).astype(np.float32)
    c["c_l1a_sinP"] = np.ascontiguousarray(
        np.repeat(np.sin(ang1), 2, axis=-1).T).astype(np.float32)
    inv = 1.0 / 10000.0 ** (np.arange(0, 64, 2, dtype=np.float64) / 64.0)
    angb = np.outer(t, inv)
    c["c_l1b_cos"] = _tiled_table(np.concatenate([np.cos(angb), np.cos(angb)], -1))
    c["c_l1b_sin"] = _tiled_table(np.concatenate([np.sin(angb), np.sin(angb)], -1))
    c["c_l1b_cosP"] = np.ascontiguousarray(
        np.concatenate([np.cos(angb), np.cos(angb)], -1).T).astype(np.float32)
    c["c_l1b_sinP"] = np.ascontiguousarray(
        np.concatenate([np.sin(angb), np.sin(angb)], -1).T).astype(np.float32)
    return c


class Ctx:
    def __init__(self, nc, prog):
        self.nc = nc
        self.P = prog
        self.ctxs = []
        self.dram = {}

    def _enter(self, cm):
        obj = cm.__enter__()
        self.ctxs.append(cm)
        return obj

    def din(self, name, shape, dtype):
        if name in self.dram:
            return self.dram[name]
        ap = self.nc.dram_tensor(name, list(shape), dtype, kind="ExternalInput").ap()
        self.dram[name] = ap
        return ap

    def dout(self, name, shape, dtype):
        ap = self.nc.dram_tensor(name, list(shape), dtype, kind="ExternalOutput").ap()
        self.dram[name] = ap
        return ap

    def dtmp(self, name, shape, dtype):
        ap = self.nc.dram_tensor(name, list(shape), dtype, kind="Internal").ap()
        self.dram[name] = ap
        return ap

    def sbuf(self, name, shape, dtype):
        return self._enter(self.nc.sbuf_tensor(name, list(shape), dtype)).ap()

    def psum(self, name, shape, dtype):
        return self._enter(self.nc.psum_tensor(name, list(shape), dtype)).ap()

    def close(self):
        for cm in reversed(self.ctxs):
            cm.__exit__(None, None, None)


def alloc_shared(cx):
    cx.xlnT = cx.sbuf("xlnT", [128, NT * S], BF16)
    cx.lnbuf = [cx.sbuf(f"lnbuf{i}", [128, D], F32) for i in range(2)]
    cx.xnat = [cx.sbuf(f"xnat{i}", [128, D], BF16) for i in range(2)]
    cx.stats = [cx.sbuf(f"stats{i}", [128, 32], F32) for i in range(2)]
    cx.wq_sb = [cx.sbuf(f"wq_sb{i}", [128, ND * 128], BF16) for i in range(2)]
    cx.wk_sb = [cx.sbuf(f"wk_sb{i}", [128, ND * 128], BF16) for i in range(2)]
    cx.wv_sb = [cx.sbuf(f"wv_sb{i}", [128, ND * 128], BF16) for i in range(2)]
    cx.qt = [cx.sbuf(f"qt{i}", [128, S], BF16) for i in range(2)]
    cx.kt = [cx.sbuf(f"kt{i}", [128, S], BF16) for i in range(2)]
    cx.vt = [cx.sbuf(f"vt{i}", [128, NT * 128], BF16) for i in range(2)]
    cx.ot = [cx.sbuf(f"ot{i}", [128, S], BF16) for i in range(2)]
    cx.pbuf = [cx.sbuf(f"pbuf{i}", [128, S], BF16) for i in range(2)]
    cx.ptb = [cx.sbuf("ptb0", [128, S], BF16)]
    cx.sm = [cx.sbuf(f"sm{i}", [128, 16], F32) for i in range(2)]
    cx.colstg = cx.sbuf("colstg", [128, 128], BF16)
    cx.colbf = cx.sbuf("colbf", [128, 64], BF16)
    cx.b1col = cx.sbuf("b1col", [128, 64], F32)
    cx.ident = cx.sbuf("ident", [128, 128], BF16)
    cx.maskinf = cx.sbuf("maskinf", [128, 128], F32)
    cx.ones_bf = cx.sbuf("ones_bf", [1, 512], BF16)
    cx.ones_f32 = cx.sbuf("ones_f32", [1, 128], F32)
    cx.cbias = cx.sbuf("cbias", [128, 3], F32)
    cx.wo_sb = cx.sbuf("wo_sb", [128, ND * 512], BF16)
    cx.at_sb = [cx.sbuf(f"at_sb{i}", [128, ND * 128], BF16) for i in range(2)]
    cx.res_sb = [cx.sbuf(f"res_sb{i}", [128, 512], F32) for i in range(2)]
    cx.vtt = [cx.res_sb[i].bitcast(BF16)[:, 0:512] for i in range(2)]
    cx.wdb = [cx.wo_sb, cx.sbuf("wdb1", [128, ND * 512], BF16)]
    cx.wut = cx.wq_sb
    cx.x2t_sb = cx.kt
    cx.dqi = [b.bitcast(I32) for b in cx.lnbuf]
    cx.dqo = cx.xnat
    cx.dqs = cx.stats
    cx.wtstage = cx.at_sb
    cx.aup = cx.sbuf("aup", [128, 16 * 16], BF16)
    cx.mnorm = cx.sbuf("mnorm", [128, 16], F32)
    cx.acol = cx.sbuf("acol", [128, 40], F32)
    cx.bqcol = cx.sbuf("bqcol", [128, 32], BF16)
    cx.mvb = cx.sbuf("mvb", [128, 64], F32)
    cx.wkcol = cx.sbuf("wkcol", [128, 16], F32)
    cx.onescol = cx.sbuf("onescol", [128, 4], BF16)
    cx.qstat = cx.sbuf("qstat", [128, 128], F32)
    cx.identf = cx.sbuf("identf", [128, 128], F32)
    cx.wvcol = cx.sbuf("wvcol", [128, 16], F32)
    cx.adn = cx.sbuf("adn", [128, 64 * 16], BF16)
    cx.colbuf = cx.sbuf("colbuf", [128, 64], F32)
    cx.bsl = [cx.vtt[0][0:16, 0:128], cx.vtt[1][0:16, 0:128]]
    cx.bsl_keys = [("res_sb", 0), ("res_sb", 1)]
    cx.bdn_sb = cx.vt[0][0:16, :]
    cx.tT = cx.ptb[0][0:16, :]
    cx.t2T = cx.pbuf[0][0:16, 0:512]
    cx.rowbuf = cx.res_sb[0][0:1, :]
    cx.rowbf = cx.ot[0][0:1, :]
    cx.ps_sc = cx.psum("ps_sc", [128, 2048], F32)
    cx.ps_tr = cx.psum("ps_tr", [128, 2048], BF16)
    cx.ps_o = cx.psum("ps_o", [128, 512], F32)
    cx.ps_mm = cx.psum("ps_mm", [128, 512], F32)
    cx.consts = host_constants()


def load_constants(cx):
    P = cx.P
    for name, dst, stream in (("c_ident", cx.ident, "p"),
                              ("c_maskinf", cx.maskinf, "y"),
                              ("c_ones", cx.ones_bf, "p"),
                              ("c_ones", cx.ones_f32, "y"),
                              ("c_bias", cx.cbias, "y"),
                              ("c_onescol", cx.onescol, "p"),
                              ("c_ident", cx.identf, "y")):
        src = cx.din(name, cx.consts[name].shape, F32)
        key = dst.tensor.name
        r, c = dst.shape
        P.dma(stream, lambda e, d=dst, s=src, r=r, c=c: e.dma_start(
            d[:, :], s[0:r, 0:c]),
            reads=(), writes=((key,),), tag=f"ld_{key}",
            ch=f"const_{stream}", batch=True)



def emit_ln_phase(cx, xres, pref, spill_xln=None, want_mnorm=None,
                  inner_ln=False, quant=False):
    P = cx.P
    for n in range(NT):
        par = n % 2
        lnb, xnat, st = cx.lnbuf[par], cx.xnat[par], cx.stats[par]
        kl, kx, ks = ("lnbuf", par), ("xnat", par), ("stats", par)
        P.dma("y", lambda e, d=lnb, s=xres, n=n: e.dma_start(
            d[:, :], s[n * 128:(n + 1) * 128, :]),
            reads=(("X", n),), writes=(kl,), tag=f"{pref}_lnld{n}",
            ch=f"lnbuf{par}")
        for i in range(4):
            P.v(lambda e, o=st, x=lnb, i=i: e.bn_stats(
                o[:, 6 * i:6 * i + 6], x[:, 512 * i:512 * (i + 1)]),
                reads=(kl,), writes=(ks,), tag=f"{pref}_bnst{n}_{i}")
        P.v(lambda e, o=st: e.bn_aggr(o[:, 24:26], o[:, 0:24]),
            reads=(ks,), writes=(ks,), tag=f"{pref}_bnag{n}")
        P.s(lambda e, o=st: e.activation(o[:, 26:27], o[:, 25:26], AF.Ln,
            bias=cx.cbias[:, 1:2]),
            reads=(ks, ("cbias",)), writes=(ks,), tag=f"{pref}_lnv{n}")
        P.s(lambda e, o=st: e.activation(o[:, 27:28], o[:, 26:27], AF.Exp, scale=-0.5, bias=cx.cbias[:, 0:1]),
            reads=(ks, ("cbias",)), writes=(ks,), tag=f"{pref}_rstd{n}")
        if want_mnorm is not None:
            P.s(lambda e, o=st: e.activation(o[:, 28:29], o[:, 25:26], AF.Ln,
                bias=cx.cbias[:, 2:3]),
                reads=(ks, ("cbias",)), writes=(ks,), tag=f"{pref}_lnv2{n}")
            P.s(lambda e, o=st: e.activation(o[:, 29:30], o[:, 28:29], AF.Exp, scale=0.5, bias=cx.cbias[:, 0:1]),
                reads=(ks, ("cbias",)), writes=(ks,), tag=f"{pref}_sqv{n}")
            P.v(lambda e, o=want_mnorm, st=st, n=n: e.tensor_scalar(
                o[:, n:n + 1], st[:, 29:30], st[:, 27:28], float(np.sqrt(D)),
                ALU.mult, ALU.mult),
                reads=(ks,), writes=(("mnorm",),), tag=f"{pref}_mn{n}")
        if inner_ln:
            P.v(lambda e, o=st: e.tensor_scalar(
                o[:, 30:31], o[:, 25:26], o[:, 27:28], None, ALU.mult),
                reads=(ks,), writes=(ks,), tag=f"{pref}_uf0{n}")
            P.v(lambda e, o=st: e.tensor_scalar(
                o[:, 31:32], o[:, 30:31], o[:, 27:28], None, ALU.mult),
                reads=(ks,), writes=(ks,), tag=f"{pref}_uf1{n}")
            P.s(lambda e, o=st: e.activation(o[:, 30:31], o[:, 31:32], AF.Ln,
                bias=cx.cbias[:, 1:2]),
                reads=(ks, ("cbias",)), writes=(ks,), tag=f"{pref}_uf2{n}")
            P.s(lambda e, o=st: e.activation(o[:, 31:32], o[:, 30:31], AF.Exp, scale=-0.5, bias=cx.cbias[:, 0:1]),
                reads=(ks, ("cbias",)), writes=(ks,), tag=f"{pref}_uf3{n}")
            P.v(lambda e, o=st: e.tensor_scalar(
                o[:, 29:30], o[:, 31:32], o[:, 27:28], None, ALU.mult),
                reads=(ks,), writes=(ks,), tag=f"{pref}_uf5{n}")
            rs = 29
        else:
            rs = 27
        P.v(lambda e, o=xnat, x=lnb, st=st, rs=rs: e.tensor_scalar(
            o[:, :], x[:, :], st[:, 24:25], st[:, rs:rs + 1], ALU.subtract, ALU.mult),
            reads=(kl, ks), writes=(kx,), tag=f"{pref}_xln{n}")
        if spill_xln is not None:
            P.dma("s", lambda e, d=spill_xln, s=xnat, n=n: e.dma_start(
                d[n * 128:(n + 1) * 128, :], s[:, :]),
                reads=(kx,), writes=(("XLN", n),), tag=f"{pref}_spill{n}",
                ch=f"spill{par}")
        if quant:
            P.v(lambda e, o=lnb, x=lnb, st=st, rs=rs: e.tensor_scalar(
                o[:, :], x[:, :], st[:, 24:25], st[:, rs:rs + 1],
                ALU.subtract, ALU.mult),
                reads=(kl, ks), writes=(kl,), tag=f"{pref}_xf32{n}")
            for hf in range(2):
                fp4_quant_half(cx, lnb[:, hf * 1024:(hf + 1) * 1024], kl,
                               xnat[:, hf * 1024:(hf + 1) * 1024], kx,
                               pref, f"xq{n}_{hf}")
        for k in range(ND):
            P.t(lambda e, o=cx.ps_tr, i_=xnat, k=k: e.transpose(
                o[:, 128 * k:128 * (k + 1)], i_[:, 128 * k:128 * (k + 1)],
                cx.ident[:, :]),
                reads=(kx, ("ident",)), writes=(("ps_tr",),), tag=f"{pref}_tr{n}_{k}")
        xlt = cx.xlnT.rearrange("p (dt s) -> p dt s", dt=ND)
        pst = cx.ps_tr.rearrange("p (dt s) -> p dt s", dt=ND)
        P.s(lambda e, o=xlt, i_=pst, n=n: e.copy(
            out=o[:, :, n * 128:(n + 1) * 128], in_=i_[:, :, :]),
            reads=(("ps_tr",),), writes=(("xlnT", n),), tag=f"{pref}_xlte{n}")


XLN_KEYS = tuple(("xlnT", n) for n in range(NT))



def load_w_colslice(cx, stream, w_dram, col0, ncol, dst, dstkey, tag, rows=D,
                    ch=None, batch=False, reads=()):
    nt = rows // 128
    src = w_dram.rearrange("(dt p) c -> p dt c", p=128)[:, :, col0:col0 + ncol]
    d = dst.rearrange("p (dt c) -> p dt c", dt=nt)
    cx.P.dma(stream, lambda e, d=d, s=src: e.dma_start(d[:, :, :], s),
             reads=reads, writes=(dstkey,), tag=tag, ch=ch or tag, batch=batch)


def attn_softmax_exp(cx, pref, h, qi, nk, par2, ncs, mid=None):
    P = cx.P
    nv = nk * 128
    sm = cx.sm[par2]
    ksm = ("sm", par2)
    kps = ("ps_sc",)
    kpb = ("pbuf", par2)
    pb = cx.pbuf[par2]
    P.v(lambda e, o=sm, ncs=ncs: e.reduce_max(
        o[:, 0:1], o[:, 4:4 + ncs], axis=AX.X, negate=True),
        reads=(ksm,), writes=(ksm,), tag=f"{pref}_cmb{h}_{qi}")
    P.s(lambda e, o=pb, i_=cx.ps_sc, sm=sm, nv=nv: e.activation(
        o[:, 0:nv], i_[:, 0:nv], AF.Exp, bias=sm[:, 0:1],
        accum_out=sm[:, 2:3]),
        reads=(kps, ksm), writes=(kpb, ksm), tag=f"{pref}_exp1_{h}_{qi}")
    if mid is not None:
        mid()
    P.v(lambda e, o=sm: e.reciprocal(o[:, 3:4], o[:, 2:3]),
        reads=(ksm,), writes=(ksm,), tag=f"{pref}_rcp{h}_{qi}")
    P.v(lambda e, o=pb, sm=sm, nv=nv: e.tensor_scalar(
        o[:, 0:nv], o[:, 0:nv], sm[:, 3:4], None, ALU.mult),
        reads=(kpb, ksm), writes=(kpb,), tag=f"{pref}_pnm{h}_{qi}")


def attn_core(cx, pref, h, par, causal, mask_sb, dh=128,
              bufs=None, oscale=None, qoff=0, obuf=None):
    P = cx.P
    if bufs is None:
        qt, kt, vt, ot = cx.qt[par], cx.kt[par], cx.vt[par], cx.ot[par]
        kq, kk, kv, ko = ("qt", par), ("kt", par), ("vt", par), ("ot", par)
    else:
        (qt, kq), (kt, kk), (vt, kv), (ot, ko) = bufs
    if obuf is not None:
        ot, ko = obuf

    def emit_ptr(qi):
        par2 = qi % 2
        nk = (qi + 1) if causal else NT
        kpb = ("pbuf", par2)
        for j in range(nk):
            P.t(lambda e, o=cx.ps_tr, i_=cx.pbuf[par2], j=j: e.transpose(
                o[:, j * 128:(j + 1) * 128], i_[:, j * 128:(j + 1) * 128],
                cx.ident[:, :]),
                reads=(kpb, ("ident",)), writes=(("ps_tr",),),
                tag=f"{pref}_ptr{h}_{qi}_{j}")

    def emit_pte_av(qi):
        nk = (qi + 1) if causal else NT
        nv = nk * 128
        kpt = ("ptb", 0)
        P.v(lambda e, o=cx.ptb[0], i_=cx.ps_tr, nv=nv: e.tensor_copy(
            o[:, 0:nv], i_[:, 0:nv]),
            reads=(("ps_tr",),), writes=(kpt,), tag=f"{pref}_pte{h}_{qi}")
        for j in range(nk):
            P.t(lambda e, o=cx.ps_o, v=vt, p_=cx.ptb[0], j=j,
                st=(j == 0), sp=(j == nk - 1): e.matmul(
                o[0:dh, 0:128], v[:, j * 128 + qoff:j * 128 + qoff + dh],
                p_[:, j * 128:(j + 1) * 128], start=st, stop=sp),
                reads=(kv, kpt), writes=(("ps_o",),),
                tag=f"{pref}_av{h}_{qi}_{j}")

    def emit_oev(qi):
        if oscale is None:
            P.s(lambda e, o=ot, i_=cx.ps_o, qi=qi: e.copy(
                out=o[qoff:qoff + dh, qi * 128:(qi + 1) * 128],
                in_=i_[0:dh, 0:128]),
                reads=(("ps_o",),), writes=(ko,), tag=f"{pref}_oev{h}_{qi}")
        else:
            osc, okey = oscale
            P.v(lambda e, o=ot, i_=cx.ps_o, qi=qi, osc=osc: e.tensor_scalar(
                o[0:dh, qi * 128:(qi + 1) * 128], i_[0:dh, 0:128], osc, None,
                ALU.mult),
                reads=(("ps_o",), okey), writes=(ko,), tag=f"{pref}_oev{h}_{qi}")

    for qi in range(NT):
        par2 = qi % 2
        nk = (qi + 1) if causal else NT
        nv = nk * 128
        sm = cx.sm[par2]
        ksm = ("sm", par2)
        ncs = (nv + 511) // 512
        for c in range(ncs):
            w = min(512, nv - c * 512)
            dg = causal and (qi // 4 == c)
            P.t(lambda e, o=cx.ps_sc, q=qt, k=kt, qi=qi, c=c, w=w, dg=dg:
                e.matmul(
                o[:, c * 512:c * 512 + w],
                q[qoff:qoff + dh, qi * 128:(qi + 1) * 128],
                k[qoff:qoff + dh, c * 512:c * 512 + w], start=True,
                stop=not dg),
                reads=(kq, kk), writes=(("ps_sc",),), tag=f"{pref}_sc{h}_{qi}_{c}")
            if dg:
                P.t(lambda e, o=cx.ps_sc, qi=qi: e.matmul(
                    o[:, qi * 128:(qi + 1) * 128], cx.identf[:, :],
                    cx.maskinf[:, :], start=False, stop=True),
                    reads=(("identf",), ("maskinf",)), writes=(("ps_sc",),),
                    tag=f"{pref}_msk{h}_{qi}")
            P.v(lambda e, o=sm, i_=cx.ps_sc, c=c, w=w: e.reduce_max(
                o[:, 4 + c:5 + c], i_[:, c * 512:c * 512 + w], axis=AX.X),
                reads=(("ps_sc",), ksm), writes=(ksm,),
                tag=f"{pref}_mx{h}_{qi}_{c}")
        if qi > 0:
            emit_ptr(qi - 1)
        attn_softmax_exp(cx, pref, h, qi, nk, par2, ncs,
                         mid=(lambda: emit_pte_av(qi - 1)) if qi else None)
        if qi > 0:
            emit_oev(qi - 1)
    emit_ptr(NT - 1)
    emit_pte_av(NT - 1)
    emit_oev(NT - 1)


def emit_out_proj(cx, pref, att_dram, w_dram, xres, wT=False, bias_row=None,
                  att_keys=(), bias_key=None):
    P = cx.P
    for fc in range(4):
        if not wT:
            load_w_colslice(cx, "p", w_dram, fc * 512, 512, cx.wo_sb, ("wo_sb",),
                            f"{pref}_wo{fc}", ch="wo")
        elif wT == "pe":
            wsl = cx.wo_sb.rearrange("p (dt c) -> p dt c", dt=ND)
            for i in range(4):
                stg = cx.qt[i % 2]
                skey = ("qt", i % 2)
                P.dma("p", lambda e, d=stg, s=w_dram, fc=fc, i=i: e.dma_start(
                    d[:, :], s[fc * 512 + i * 128:fc * 512 + (i + 1) * 128, :]),
                    reads=(), writes=(skey,), tag=f"{pref}_wol{fc}_{i}",
                    ch=f"wol{i % 2}")
                for k in range(ND):
                    P.t(lambda e, o=cx.ps_tr, i_=stg, k=k: e.transpose(
                        o[:, (k % 8) * 128:(k % 8 + 1) * 128],
                        i_[:, 128 * k:128 * (k + 1)], cx.ident[:, :]),
                        reads=(skey, ("ident",)), writes=(("ps_tr",),),
                        tag=f"{pref}_wot{fc}_{i}_{k}")
                    P.v(lambda e, o=wsl, i_=cx.ps_tr, k=k, i=i: e.tensor_copy(
                        o[:, k, i * 128:(i + 1) * 128],
                        i_[:, (k % 8) * 128:(k % 8 + 1) * 128]),
                        reads=(("ps_tr",),), writes=(("wo_sb",),),
                        tag=f"{pref}_woe{fc}_{i}_{k}")
        else:
            wsl = cx.wo_sb.rearrange("p (dt c) -> p dt c", dt=ND)
            for dt in range(ND):
                P.dma("y", lambda e, o=wsl, s=w_dram, fc=fc, dt=dt: e.dma_start(
                    o[:, dt, :], s[fc * 512:(fc + 1) * 512, dt * 128:(dt + 1) * 128],
                    transpose=True),
                    reads=(), writes=(("wo_sb",),), tag=f"{pref}_woT{fc}_{dt}",
                    ch=f"woT{dt % 2}")
        for st in range(NT):
            par = st % 2
            atb = cx.at_sb[par]
            katb = ("at_sb", par)
            src = att_dram.rearrange("(dt p) s -> p dt s", p=128)[
                :, :, st * 128:(st + 1) * 128]
            P.dma("y", lambda e, d=atb, s=src: e.dma_start(
                d.rearrange("p (dt c) -> p dt c", dt=ND)[:, :, :], s),
                reads=att_keys, writes=(katb,), tag=f"{pref}_atl{fc}_{st}",
                ch=f"atl{par}")
            for dt in range(ND):
                P.t(lambda e, o=cx.ps_mm, a=atb, w=cx.wo_sb, dt=dt, st_=(dt == 0),
                    sp=(dt == ND - 1 and bias_row is None): e.matmul(
                    o[:, 0:512], a[:, dt * 128:(dt + 1) * 128],
                    w[:, dt * 512:(dt + 1) * 512], start=st_, stop=sp),
                    reads=(katb, ("wo_sb",)), writes=(("ps_mm",),),
                    tag=f"{pref}_om{fc}_{st}_{dt}")
            if bias_row is not None:
                P.t(lambda e, o=cx.ps_mm, b=bias_row, fc=fc: e.matmul(
                    o[:, 0:512], cx.ones_bf[0:1, 0:128],
                    b[0:1, fc * 512:(fc + 1) * 512], start=False, stop=True),
                    reads=((bias_key or (bias_row.tensor.name,)), ("ones_bf",)),
                    writes=(("ps_mm",),), tag=f"{pref}_ob{fc}_{st}")
            rsb = cx.res_sb[par]
            krs = ("res_sb", par)
            P.v(lambda e, o=rsb, i_=cx.ps_mm: e.tensor_copy(o[:, :], i_[:, 0:512]),
                reads=(("ps_mm",),), writes=(krs,), tag=f"{pref}_oe{fc}_{st}")
            P.dma("p", lambda e, d=xres, s=rsb, st=st, fc=fc: e.dma_start(
                d[st * 128:(st + 1) * 128, fc * 512:(fc + 1) * 512], s[:, :],
                accum_op=ALU.add),
                reads=(krs, ("X", st)), writes=(("X", st),),
                tag=f"{pref}_oacc{fc}_{st}", ch=f"res{par}")


def emit_qkv_head_qtlayout(cx, pref, h, par, wq_src, wk_src, wv_src,
                           q_scale=1.0):
    P = cx.P
    xlt = cx.xlnT.rearrange("p (dt s) -> p dt s", dt=ND)
    for (src, col0), wbuf, wkey, tag in (
            (wq_src, cx.wq_sb[par], ("wq_sb", par), "wq"),
            (wk_src, cx.wk_sb[par], ("wk_sb", par), "wk"),
            (wv_src, cx.wv_sb[par], ("wv_sb", par), "wv")):
        load_w_colslice(cx, "p", src, col0, 128, wbuf, wkey, f"{pref}_{tag}{h}",
                        ch=f"{tag}{par}")
    for (wbuf, wkey, dst, dkey, sc) in (
            (cx.wq_sb[par], ("wq_sb", par), cx.qt[par], ("qt", par), q_scale),
            (cx.wk_sb[par], ("wk_sb", par), cx.kt[par], ("kt", par), 1.0)):
        for c in range(4):
            for dt in range(ND):
                P.t(lambda e, o=cx.ps_mm, w=wbuf, x=xlt, dt=dt, c=c,
                    st=(dt == 0), sp=(dt == ND - 1): e.matmul(
                    o[:, 0:512], w[:, dt * 128:(dt + 1) * 128],
                    x[:, dt, c * 512:(c + 1) * 512], start=st, stop=sp),
                    reads=(wkey,) + XLN_KEYS, writes=(("ps_mm",),),
                    tag=f"{pref}_qk{h}_{c}_{dt}")
            if sc == 1.0:
                P.s(lambda e, o=dst, i_=cx.ps_mm, c=c: e.copy(
                    out=o[:, c * 512:(c + 1) * 512], in_=i_[:, 0:512]),
                    reads=(("ps_mm",),), writes=(dkey,), tag=f"{pref}_qke{h}_{c}")
            else:
                P.s(lambda e, o=dst, i_=cx.ps_mm, c=c, sc=sc: e.mul(
                    o[:, c * 512:(c + 1) * 512], i_[:, 0:512], float(sc)),
                    reads=(("ps_mm",),), writes=(dkey,), tag=f"{pref}_qke{h}_{c}")
    for c in range(4):
        for dt in range(ND):
            P.t(lambda e, o=cx.ps_mm, w=cx.wv_sb[par], x=xlt, dt=dt, c=c,
                st=(dt == 0), sp=(dt == ND - 1): e.matmul(
                o[:, 0:512], w[:, dt * 128:(dt + 1) * 128],
                x[:, dt, c * 512:(c + 1) * 512], start=st, stop=sp),
                reads=(("wv_sb", par),) + XLN_KEYS, writes=(("ps_mm",),),
                tag=f"{pref}_v{h}_{c}_{dt}")
        P.s(lambda e, o=cx.vtt[par], i_=cx.ps_mm: e.copy(out=o[:, :], in_=i_[:, 0:512]),
            reads=(("ps_mm",),), writes=(("res_sb", par),), tag=f"{pref}_ve{h}_{c}")
        for j in range(4):
            P.t(lambda e, o=cx.ps_tr, i_=cx.vtt[par], j=j: e.transpose(
                o[:, j * 128:(j + 1) * 128], i_[:, j * 128:(j + 1) * 128],
                cx.ident[:, :]),
                reads=(("res_sb", par), ("ident",)), writes=(("ps_tr",),),
                tag=f"{pref}_vtr{h}_{c}_{j}")
        P.v(lambda e, o=cx.vt[par], i_=cx.ps_tr, c=c: e.tensor_copy(
            o[:, c * 512:(c + 1) * 512], i_[:, 0:512]),
            reads=(("ps_tr",),), writes=(("vt", par),), tag=f"{pref}_vte{h}_{c}")



def emit_l0_attn_b(cx, xres):
    P = cx.P
    pref = "b0"
    wqkv = cx.din("L0_attn_b_Wqkv", (D, 3 * D), F32)
    wo = cx.din("L0_attn_b_Wo", (D, D), F32)
    att = cx.dram["ATT"]
    emit_ln_phase(cx, xres, pref)
    att_keys = []
    for h in range(16):
        par = h % 2
        emit_qkv_head_qtlayout(cx, pref, h, par,
                               (wqkv, h * 128), (wqkv, 2048 + h * 128),
                               (wqkv, 4096 + h * 128), q_scale=128.0 ** -0.5)
        attn_core(cx, pref, h, par, causal=True, mask_sb=cx.maskinf)
        P.dma("s", lambda e, d=att, s=cx.ot[par], h=h: e.dma_start(
            d[h * 128:(h + 1) * 128, :], s[:, :]),
            reads=(("ot", par),), writes=(("ATT", h),), tag=f"{pref}_ost{h}",
            ch=f"att{par}")
        att_keys.append(("ATT", h))
    emit_out_proj(cx, pref, att, wo, xres, att_keys=tuple(att_keys))



SUBLAYERS = ("l0_attn_a", "l0_ffn_1", "l0_attn_b", "l0_ffn_2",
             "l1_attn_a", "l1_ffn_1", "l1_attn_b", "l1_ffn_2")


def build_program(sublayers, inputs_shapes=None):
    nc = bass.Bass("TRN2", target_bir_lowering=False)
    P = Prog()
    cx = Ctx(nc, P)
    alloc_shared(cx)
    xin = cx.din("x", (S, D), F32)
    xres = cx.dout("xout", (S, D), F32)
    cx.dtmp("ATT", (D, S), BF16)
    cx.dtmp("W_UPT", (64, 128, D), BF16)
    cx.dtmp("W_DNT", (64, 128, D), BF16)
    cx.dtmp("X2T", (8192, D), BF16)
    cx.dtmp("LGN", (S, 2048), BF16)
    cx.dtmp("DSPN", (S, 2048), BF16)
    cx.dtmp("CMBT", (2048, S), BF16)
    cx.dtmp("SLIN", (D, 2048), BF16)
    cx.dtmp("SLOT_OUT", (2048, D), BF16)
    cx.dtmp("XLN", (S, D), BF16)
    cx.dtmp("W4T", (4, D, D), BF16)
    load_constants(cx)
    for n in range(NT):
        P.dma("y", lambda e, d=xres, s=xin, n=n: e.dma_start(
            d[n * 128:(n + 1) * 128, :], s[n * 128:(n + 1) * 128, :]),
            reads=(), writes=(("X", n),), tag=f"init{n}", ch="init", batch=True)
    for sl in sublayers:
        if sl == "l0_attn_b":
            emit_l0_attn_b(cx, xres)
        elif sl == "l0_attn_a":
            emit_l0_attn_a(cx, xres)
        elif sl == "l1_attn_a":
            emit_l1_attn_a(cx, xres)
        elif sl in ("l1_ffn_1", "l1_ffn_2"):
            emit_l1_ffn(cx, xres, int(sl[-1]))
        elif sl == "l1_attn_b":
            emit_l1_attn_b(cx, xres)
        elif sl in ("l0_ffn_1", "l0_ffn_2"):
            emit_l0_ffn(cx, xres, int(sl[-1]))
        else:
            raise NotImplementedError(sl)
    ctxs = P.build(nc)
    cx.close()
    for cm in reversed(ctxs):
        cm.__exit__(None, None, None)
    return nc, cx


SCRATCH = ("x", "xout", "ATT", "W_UPT", "W_DNT", "X2T", "XLN", "LGN",
           "DSPN", "CMBT", "SLIN", "SLOT_OUT", "W4T")


def stage_inputs(cx, inputs, x):
    consts = host_constants()
    in_maps = []
    staged = {}
    for n in cx.dram:
        if n in SCRATCH:
            continue
        if n.startswith("c_"):
            staged[n] = consts[n]
        else:
            v = np.asarray(inputs[n])
            if v.dtype == np.int32:
                staged[n] = np.ascontiguousarray(v)
            else:
                staged[n] = np.ascontiguousarray(v.astype(np.float32))
    for core in range(8):
        m = dict(staged)
        m["x"] = np.ascontiguousarray(x[core])
        in_maps.append(m)
    return in_maps


FULL_CHAIN = ["l0_attn_a", "l0_ffn_1", "l0_attn_b", "l0_ffn_2",
              "l1_attn_a", "l1_ffn_1", "l1_attn_b", "l1_ffn_2"]


def kernel(**inputs):
    x = np.asarray(inputs["L0_attn_a_x"], dtype=np.float32)
    nc, cx = build_program(FULL_CHAIN)
    in_maps = stage_inputs(cx, inputs, x)
    res = run_bass_kernel_spmd(nc, in_maps, core_ids=list(range(8)))
    return np.stack([r["xout"] for r in res.results], axis=0)



def load_row(cx, dram_vec, n, dst, dstkey, tag, ch, stream="y", cast=False):
    cx.P.dma(stream, lambda e, d=dst, s=dram_vec, n=n: e.dma_start(
        d[0:1, 0:n], s.rearrange("(a n) -> a n", a=1) if s.ndim == 1 else s),
        reads=(), writes=(dstkey,), tag=tag, ch=ch)


def load_cols(cx, src2d, nt, col_sb, col0, colkey, tag, f32=True):
    P = cx.P
    P.dma("p", lambda e, d=cx.colstg, s=src2d, nt=nt: e.dma_start(
        d[0:nt, 0:128], s),
        reads=(), writes=(("colstg",),), tag=f"{tag}_ld", ch="colld")
    if f32:
        P.dma("y", lambda e, d=cx.colbf, s=cx.colstg, nt=nt: e.dma_start(
            d[:, 0:nt], s[0:nt, 0:128], transpose=True),
            reads=(("colstg",),), writes=(("colbf",),), tag=f"{tag}_tr",
            ch="xbar")
        P.v(lambda e, o=col_sb, i_=cx.colbf, col0=col0, nt=nt: e.tensor_copy(
            o[:, col0:col0 + nt], i_[:, 0:nt]),
            reads=(("colbf",),), writes=(colkey,), tag=f"{tag}_cst")
    else:
        P.dma("y", lambda e, d=col_sb, s=cx.colstg, col0=col0, nt=nt:
              e.dma_start(d[:, col0:col0 + nt], s[0:nt, 0:128], transpose=True),
              reads=(("colstg",),), writes=(colkey,), tag=f"{tag}_tr",
              ch="xbar")


def row_to_cols(cx, row_sb, rowkey, n, col_sb, col0, colkey, tag, scale=None):
    P = cx.P
    nt = n // 128
    for i in range(nt):
        P.t(lambda e, o=cx.ps_o, r=row_sb, i=i: e.matmul(
            o[:, 0:1], r[0:1, i * 128:(i + 1) * 128], cx.ones_f32[0:1, 0:1],
            start=True, stop=True),
            reads=(rowkey, ("ones_f32",)), writes=(("ps_o",),), tag=f"{tag}_m{i}")
        if scale is None:
            P.v(lambda e, o=col_sb, i_=cx.ps_o, i=i: e.tensor_copy(
                o[:, col0 + i:col0 + i + 1], i_[:, 0:1]),
                reads=(("ps_o",),), writes=(colkey,), tag=f"{tag}_e{i}")
        else:
            P.v(lambda e, o=col_sb, i_=cx.ps_o, i=i, sc=scale: e.tensor_scalar(
                o[:, col0 + i:col0 + i + 1], i_[:, 0:1], float(sc), None, ALU.mult),
                reads=(("ps_o",),), writes=(colkey,), tag=f"{tag}_e{i}")



def emit_l0_ffn_dequant(cx):
    P = cx.P
    wupq = cx.din("L0_ffn_w_up_q", (8192, D), I32)
    wups = cx.din("L0_ffn_w_up_scale", (8192, 32), F32)
    wdnq = cx.din("L0_ffn_w_down_q", (D, 8192), I32)
    wdns = cx.din("L0_ffn_w_down_scale", (D, 128), F32)
    wupt = cx.dram["W_UPT"]
    wdnt = cx.dram["W_DNT"]
    chunks = []
    for ht in range(64):
        chunks.append(("up", ht, 0))
    for dt in range(16):
        for c in range(4):
            chunks.append(("dn", dt, c))
    for idx, (kind, a, b) in enumerate(chunks):
        par = idx % 2
        dqi, dqo, dqs = cx.dqi[par], cx.dqo[par], cx.dqs[par]
        ki, ko, ks2 = ("lnbuf", par), ("xnat", par), ("stats", par)
        if kind == "up":
            qsrc = wupq[a * 128:(a + 1) * 128, :]
            ssrc = wups[a * 128:(a + 1) * 128, :]
        else:
            qsrc = wdnq[a * 128:(a + 1) * 128, b * 2048:(b + 1) * 2048]
            ssrc = wdns[a * 128:(a + 1) * 128, b * 32:(b + 1) * 32]
        P.dma("p", lambda e, d=dqi, s=qsrc: e.dma_start(d[:, :], s),
              reads=(), writes=(ki,), tag=f"dq_q{idx}", ch=f"dqq{par}")
        P.dma("y", lambda e, d=dqs, s=ssrc: e.dma_start(d[:, 0:32], s),
              reads=(), writes=(ks2,), tag=f"dq_s{idx}", ch=f"dqs{par}")
        sc_b = dqs[:, 0:32].unsqueeze(2).broadcast_to([128, 32, 64])
        qv = dqi.rearrange("p (b j) -> p b j", b=32)
        ov = dqo.rearrange("p (b j) -> p b j", b=32)
        P.v(lambda e, o=ov, q=qv, s=sc_b: e.scalar_tensor_tensor(
            o[:, :, :], q[:, :, :], 7.5, s[:, :, :], ALU.subtract, ALU.mult),
            reads=(ki, ks2), writes=(ko,), tag=f"dq_c{idx}")
        for k in range(ND):
            P.t(lambda e, o=cx.ps_tr, i_=dqo, k=k: e.transpose(
                o[:, 128 * k:128 * (k + 1)], i_[:, 128 * k:128 * (k + 1)],
                cx.ident[:, :]),
                reads=(ko, ("ident",)), writes=(("ps_tr",),), tag=f"dq_t{idx}_{k}")
        stg = cx.wtstage[par]
        kst = ("at_sb", par)
        P.v(lambda e, o=stg, i_=cx.ps_tr: e.tensor_copy(o[:, :], i_[:, :]),
            reads=(("ps_tr",),), writes=(kst,), tag=f"dq_e{idx}")
        if kind == "up":
            dst = wupt[a, :, :]
            P.dma("s", lambda e, d=dst, s=stg: e.dma_start(d, s[:, :]),
                  reads=(kst,), writes=(("W_UPT",),), tag=f"dq_w{idx}",
                  ch=f"dqw{par}")
        else:
            sv = stg.rearrange("p (k j) -> p k j", k=ND)
            dst = wdnt.rearrange("ht p c -> p ht c")[:, b * 16:(b + 1) * 16,
                                                     a * 128:(a + 1) * 128]
            P.dma("s", lambda e, d=dst, s=sv: e.dma_start(d, s[:, :, :]),
                  reads=(kst,), writes=(("W_DNT",),), tag=f"dq_w{idx}",
                  ch=f"dqw{par}")


def emit_l0_ffn(cx, xres, app):
    P = cx.P
    pref = f"f0{app}"
    alora = cx.din("L0_ffn_w_up_lora_a", (D, 16), F32)
    blora = cx.din("L0_ffn_w_up_lora_b", (16, 8192), F32)
    adlora = cx.din("L0_ffn_w_down_lora_a", (8192, 16), F32)
    bdlora = cx.din("L0_ffn_w_down_lora_b", (16, D), F32)
    bup = cx.din("L0_ffn_b_up", (8192,), F32)
    bdown = cx.din("L0_ffn_b_down", (D,), F32)
    wupt = cx.dram["W_UPT"]
    wdnt = cx.dram["W_DNT"]
    x2t = cx.dram["X2T"]

    emit_ln_phase(cx, xres, pref)
    if not getattr(cx, "ffn0_ready", False):
        cx.ffn0_ready = True
        emit_l0_ffn_dequant(cx)
        load_w_colslice(cx, "p", alora, 0, 16, cx.aup, ("aup",), "ld_aup",
                        ch="smw", batch=True)
        load_w_colslice(cx, "p", adlora, 0, 16, cx.adn, ("adn",), "ld_adn",
                        rows=8192, ch="smw", batch=True)
        load_cols(cx, bup.rearrange("(a f) -> a f", a=64), 64, cx.colbuf,
                  0, ("colbuf",), "bupc")
    P.dma("p", lambda e, d=cx.rowbf, s=bdown: e.dma_start(
        d, s.rearrange("(a n) -> a n", a=1)),
        reads=(), writes=(("ot", 0),), tag=f"{pref}_ld_bdown", ch="smw3")
    P.dma("p", lambda e, d=cx.bdn_sb, s=bdlora: e.dma_start(d, s[:, :]),
          reads=(), writes=(("vt", 0),), tag=f"{pref}_ld_bdn", ch="smw3")

    xlt = cx.xlnT.rearrange("p (dt s) -> p dt s", dt=ND)
    for cs in range(4):
        for dt in range(ND):
            P.t(lambda e, o=cx.ps_o, a=cx.aup, x=xlt, dt=dt, cs=cs,
                st=(dt == 0), sp=(dt == ND - 1): e.matmul(
                o[0:16, 0:512], a[:, dt * 16:(dt + 1) * 16],
                x[:, dt, cs * 512:(cs + 1) * 512], start=st, stop=sp),
                reads=(("aup",),) + XLN_KEYS, writes=(("ps_o",),),
                tag=f"{pref}_lt{cs}_{dt}")
        P.s(lambda e, o=cx.tT, i_=cx.ps_o, cs=cs: e.copy(
            out=o[:, cs * 512:(cs + 1) * 512], in_=i_[0:16, 0:512]),
            reads=(("ps_o",),), writes=(("ptb", 0),), tag=f"{pref}_lte{cs}")

    for ht in range(64):
        par = ht % 2
        wut, kw = cx.wut[par], ("wq_sb", par)
        P.dma("p", lambda e, d=wut, s=wupt, ht=ht: e.dma_start(d[:, :], s[ht, :, :]),
              reads=(("W_UPT",),), writes=(kw,), tag=f"{pref}_wu{ht}",
              ch=f"wu{par}")
        bsl = cx.bsl[par]
        P.dma("p", lambda e, d=bsl, s=blora, ht=ht: e.dma_start(
            d[:, :], s[:, ht * 128:(ht + 1) * 128]),
            reads=(), writes=(cx.bsl_keys[par],), tag=f"{pref}_bu{ht}", ch=f"bu{par}")
        x2s = cx.x2t_sb[par]
        kx2 = ("kt", par)
        for cs in range(4):
            for dt in range(ND):
                P.t(lambda e, o=cx.ps_mm, w=wut, x=xlt, dt=dt, cs=cs,
                    st=(dt == 0): e.matmul(
                    o[:, 0:512], w[:, dt * 128:(dt + 1) * 128],
                    x[:, dt, cs * 512:(cs + 1) * 512], start=st, stop=False),
                    reads=(kw,) + XLN_KEYS, writes=(("ps_mm",),),
                    tag=f"{pref}_up{ht}_{cs}_{dt}")
            P.t(lambda e, o=cx.ps_mm, b=bsl, t_=cx.tT, cs=cs: e.matmul(
                o[:, 0:512], b, t_[:, cs * 512:(cs + 1) * 512],
                start=False, stop=True),
                reads=(cx.bsl_keys[par], ("ptb", 0)), writes=(("ps_mm",),),
                tag=f"{pref}_upl{ht}_{cs}")
            P.s(lambda e, o=x2s, i_=cx.ps_mm, cs=cs, ht=ht: e.activation(
                o[:, cs * 512:(cs + 1) * 512], i_[:, 0:512], AF.Relu,
                bias=cx.colbuf[:, ht:ht + 1]),
                reads=(("ps_mm",), ("colbuf",)), writes=(kx2,),
                tag=f"{pref}_ue{ht}_{cs}")
        P.dma("s", lambda e, d=x2t, s=x2s, ht=ht: e.dma_start(
            d[ht * 128:(ht + 1) * 128, :], s[:, :]),
            reads=(kx2,), writes=(("X2T",),), tag=f"{pref}_x2s{ht}",
            ch=f"x2s{par}")

    for cs in range(4):
        pan = cx.xlnT.rearrange("p (ht c) -> p ht c", ht=64)
        src = x2t.rearrange("(ht p) c -> p ht c", p=128)[:, :, cs * 512:(cs + 1) * 512]
        P.dma("y", lambda e, d=pan, s=src: e.dma_start(d[:, :, :], s),
              reads=(("X2T",),), writes=XLN_KEYS, tag=f"{pref}_pan{cs}",
              ch="panel")
        for ht in range(64):
            P.t(lambda e, o=cx.ps_o, a=cx.adn, x=pan, ht=ht,
                st=(ht == 0), sp=(ht == 63): e.matmul(
                o[0:16, 0:512], a[:, ht * 16:(ht + 1) * 16], x[:, ht, :],
                start=st, stop=sp),
                reads=(("adn",),) + XLN_KEYS, writes=(("ps_o",),),
                tag=f"{pref}_l2{cs}_{ht}")
        P.s(lambda e, o=cx.t2T, i_=cx.ps_o: e.copy(
            out=o, in_=i_[0:16, 0:512]),
            reads=(("ps_o",),), writes=(("pbuf", 0),), tag=f"{pref}_l2e{cs}")
        for dc in range(4):
            for b in range(4):
                par = (dc * 4 + b) % 2
                wdb = cx.wdb[par]
                kwd = ("wo_sb",) if par == 0 else ("wdb1",)
                wsrc = wdnt.rearrange("ht p c -> p ht c")[
                    :, b * 16:(b + 1) * 16, dc * 512:(dc + 1) * 512]
                P.dma("p", lambda e, d=wdb, s=wsrc: e.dma_start(
                    d.rearrange("p (h c) -> p h c", h=ND)[:, :, :], s),
                    reads=(("W_DNT",),), writes=(kwd,),
                    tag=f"{pref}_wd{cs}_{dc}_{b}", ch=f"wd{par}")
                for sp_ in range(4):
                    for hb in range(16):
                        P.t(lambda e, o=cx.ps_sc, x=pan, w=wdb, sp_=sp_, hb=hb,
                            b=b, st=(b == 0 and hb == 0): e.matmul(
                            o[:, sp_ * 512:(sp_ + 1) * 512],
                            x[:, b * 16 + hb, sp_ * 128:(sp_ + 1) * 128],
                            w[:, hb * 512:(hb + 1) * 512], start=st, stop=False),
                            reads=XLN_KEYS + (kwd,), writes=(("ps_sc",),),
                            tag=f"{pref}_dn{cs}_{dc}_{b}_{sp_}_{hb}")
            for sp_ in range(4):
                P.t(lambda e, o=cx.ps_sc, t_=cx.t2T, bd=bdlora, sp_=sp_, dc=dc: e.matmul(
                    o[:, sp_ * 512:(sp_ + 1) * 512], t_[:, sp_ * 128:(sp_ + 1) * 128],
                    cx.bdn_sb[:, dc * 512:(dc + 1) * 512], start=False, stop=False),
                    reads=(("pbuf", 0), ("vt", 0)), writes=(("ps_sc",),),
                    tag=f"{pref}_dl{cs}_{dc}_{sp_}")
                P.t(lambda e, o=cx.ps_sc, sp_=sp_, dc=dc: e.matmul(
                    o[:, sp_ * 512:(sp_ + 1) * 512], cx.ones_bf[0:1, 0:128],
                    cx.rowbf[:, dc * 512:(dc + 1) * 512], start=False, stop=True),
                    reads=(("ones_bf",), ("ot", 0)), writes=(("ps_sc",),),
                    tag=f"{pref}_db{cs}_{dc}_{sp_}")
                par2 = sp_ % 2
                rsb = cx.res_sb[par2]
                krs = ("res_sb", par2)
                stt = cs * 4 + sp_
                P.v(lambda e, o=rsb, i_=cx.ps_sc, sp_=sp_: e.tensor_copy(
                    o[:, :], i_[:, sp_ * 512:(sp_ + 1) * 512]),
                    reads=(("ps_sc",),), writes=(krs,), tag=f"{pref}_de{cs}_{dc}_{sp_}")
                P.dma("p", lambda e, d=xres, s=rsb, stt=stt, dc=dc: e.dma_start(
                    d[stt * 128:(stt + 1) * 128, dc * 512:(dc + 1) * 512], s[:, :],
                    accum_op=ALU.add),
                    reads=(krs, ("X", stt)), writes=(("X", stt),),
                    tag=f"{pref}_da{cs}_{dc}_{sp_}", ch=f"res{par2}")



def _row_to_cols_bf(cx, row_sb, rowkey, n, col_sb, col0, colkey, tag):
    P = cx.P
    for i in range(n // 128):
        P.t(lambda e, o=cx.ps_o, r=row_sb, i=i: e.matmul(
            o[:, 0:1], r[0:1, i * 128:(i + 1) * 128], cx.ones_f32[0:1, 0:1],
            start=True, stop=True),
            reads=(rowkey, ("ones_f32",)), writes=(("ps_o",),), tag=f"{tag}_m{i}")
        P.v(lambda e, o=col_sb, i_=cx.ps_o, i=i: e.tensor_copy(
            o[:, col0 + i:col0 + i + 1], i_[:, 0:1]),
            reads=(("ps_o",),), writes=(colkey,), tag=f"{tag}_e{i}")


def emit_l0_attn_a(cx, xres):
    P = cx.P
    pref = "a0"
    wq = cx.din("L0_attn_a_Wq", (D, D), F32)
    wkv = cx.din("L0_attn_a_Wkv", (D, 2 * D), F32)
    wo = cx.din("L0_attn_a_Wo", (D, D), F32)
    lnqg = cx.din("L0_attn_a_ln_q_g", (D,), F32)
    lnqb = cx.din("L0_attn_a_ln_q_b", (D,), F32)
    lnkg = cx.din("L0_attn_a_ln_kv_g", (D,), F32)
    lnkb = cx.din("L0_attn_a_ln_kv_b", (D,), F32)
    lnhg = cx.din("L0_attn_a_ln_h_g", (128,), F32)
    lnhb = cx.din("L0_attn_a_ln_h_b", (128,), F32)
    bq_v = cx.din("L0_attn_a_bq", (D,), F32)
    bkv_v = cx.din("L0_attn_a_bkv", (2 * D,), F32)
    bo_v = cx.din("L0_attn_a_bo", (D,), F32)
    cosT = cx.din("c_l0a_cos", (128, NT * 64), F32)
    sinT = cx.din("c_l0a_sin", (128, NT * 64), F32)
    att = cx.dram["ATT"]

    emit_ln_phase(cx, xres, pref, want_mnorm=cx.mnorm, inner_ln=True)

    for vec, col0, tag in ((lnqg, 0, "gq"), (lnkg, 16, "gkv")):
        load_cols(cx, vec.rearrange("(a f) -> a f", a=16), 16, cx.acol,
                  col0, ("acol",), f"{pref}_{tag}c")
    for vec, col0, tag in ((lnhg, 32, "hg"), (lnhb, 33, "hb")):
        P.dma("y", lambda e, d=cx.rowbuf, s=vec: e.dma_start(
            d[0:1, 0:128], s.rearrange("(a n) -> a n", a=1)),
            reads=(), writes=(("res_sb", 0),), tag=f"{pref}_{tag}r", ch="rowld")
        row_to_cols(cx, cx.rowbuf, ("res_sb", 0), 128, cx.acol, col0,
                    ("acol",), f"{pref}_{tag}c")
    P.v(lambda e, o=cx.acol: e.tensor_scalar(
        o[:, 34:36], o[:, 32:34], 128.0 ** -0.25, None, ALU.mult),
        reads=(("acol",),), writes=(("acol",),), tag=f"{pref}_hsc")
    for vec, col0, tag in ((lnqb, 0, "bq"), (lnkb, 16, "bkv")):
        load_cols(cx, vec.rearrange("(a f) -> a f", a=16), 16, cx.bqcol,
                  col0, ("bqcol",), f"{pref}_{tag}c", f32=False)
    cs_sb = cx.xnat[0].bitcast(F32)
    sn_sb = cx.xnat[1].bitcast(F32)
    P.dma("y", lambda e, d=cs_sb, s=cosT: e.dma_start(d[:, :], s[:, :]),
          reads=(), writes=(("xnat", 0),), tag=f"{pref}_cosld", ch="lnbuf0")
    P.dma("y", lambda e, d=sn_sb, s=sinT: e.dma_start(d[:, :], s[:, :]),
          reads=(), writes=(("xnat", 1),), tag=f"{pref}_sinld", ch="lnbuf1")
    rr = cx.lnbuf[0][:, 0:1024].rearrange("p (t j) -> p t j", t=NT)
    ri = cx.lnbuf[0][:, 1024:2048].rearrange("p (t j) -> p t j", t=NT)
    mb = cx.mnorm.unsqueeze(2).broadcast_to([128, NT, 64])
    P.v(lambda e, o=rr, c=cs_sb.rearrange("p (t j) -> p t j", t=NT), m=mb:
        e.tensor_tensor(o[:, :, :], c[:, :, :], m[:, :, :], ALU.mult),
        reads=(("xnat", 0), ("mnorm",)), writes=(("lnbuf", 0),), tag=f"{pref}_rr")
    P.v(lambda e, o=ri, c=sn_sb.rearrange("p (t j) -> p t j", t=NT), m=mb:
        e.tensor_tensor(o[:, :, :], c[:, :, :], m[:, :, :], ALU.mult),
        reads=(("xnat", 1), ("mnorm",)), writes=(("lnbuf", 0),), tag=f"{pref}_ri")

    xlt = cx.xlnT.rearrange("p (dt s) -> p dt s", dt=ND)
    att_keys = []
    bqt = bq_v.rearrange("(a p b) -> a p b", a=16, b=1)
    bkvt = bkv_v.rearrange("(a p b) -> a p b", a=32, b=1)
    for h in range(16):
        par = h % 2
        specs = [("q", wq, h * 128, cx.wq_sb[par], ("wq_sb", par), 0, bqt, h, 0),
                 ("k", wkv, h * 128, cx.wk_sb[par], ("wk_sb", par), 16, bkvt,
                  h, 1),
                 ("v", wkv, 2048 + h * 128, cx.wv_sb[par], ("wv_sb", par), 16,
                  bkvt, 16 + h, 2)]
        for m_, wsrc, col0, wbuf, wkey, gcol, bt, bi, pc in specs:
            load_w_colslice(cx, "p", wsrc, col0, 128, wbuf, wkey,
                            f"{pref}_w{m_}{h}", ch=f"w{m_}{par}")
            for dt in range(ND):
                P.t(lambda e, o=cx.ps_o, b=cx.bqcol, w=wbuf, dt=dt, gcol=gcol,
                    pc=pc, st=(dt == 0), sp=(dt == ND - 1): e.matmul(
                    o[:, pc:pc + 1], w[:, dt * 128:(dt + 1) * 128],
                    b[:, gcol + dt:gcol + dt + 1], start=st, stop=sp),
                    reads=(("bqcol",), wkey), writes=(("ps_o",),),
                    tag=f"{pref}_bm{m_}{h}_{dt}")
            P.dma("p", lambda e, d=cx.acol, s=bt, bi=bi, pc=pc: e.dma_start(
                d[:, 36 + pc:37 + pc], s[bi]),
                reads=(), writes=(("acol",),), tag=f"{pref}_pb{m_}{h}",
                ch="arowld")
            for dt in range(ND):
                P.v(lambda e, w=wbuf, dt=dt, gcol=gcol: e.tensor_scalar(
                    w[:, dt * 128:(dt + 1) * 128], w[:, dt * 128:(dt + 1) * 128],
                    cx.acol[:, gcol + dt:gcol + dt + 1], None, ALU.mult),
                    reads=(wkey, ("acol",)), writes=(wkey,),
                    tag=f"{pref}_ws{m_}{h}_{dt}")
        P.v(lambda e, o=cx.acol, i_=cx.ps_o: e.tensor_tensor(
            o[:, 36:39], o[:, 36:39], i_[:, 0:3], ALU.add),
            reads=(("acol",), ("ps_o",)), writes=(("acol",),),
            tag=f"{pref}_bab{h}")
        qnat, knat = cx.ot[par], cx.at_sb[par]
        kqn, kkn = ("ot", par), ("at_sb", par)
        for (m_, wbuf, wkey, tdst, ktd, ndst, knd, pc) in (
                ("q", cx.wq_sb[par], ("wq_sb", par), cx.qt[par], ("qt", par),
                 qnat, kqn, 0),
                ("k", cx.wk_sb[par], ("wk_sb", par), cx.kt[par], ("kt", par),
                 knat, kkn, 1)):
            for c in range(4):
                for dt in range(ND):
                    P.t(lambda e, o=cx.ps_mm, w=wbuf, x=xlt, dt=dt, c=c,
                        st=(dt == 0), sp=(dt == ND - 1): e.matmul(
                        o[:, 0:512], w[:, dt * 128:(dt + 1) * 128],
                        x[:, dt, c * 512:(c + 1) * 512], start=st, stop=sp),
                        reads=(wkey,) + XLN_KEYS, writes=(("ps_mm",),),
                        tag=f"{pref}_{m_}n{h}_{c}_{dt}")
                P.v(lambda e, o=tdst, i_=cx.ps_mm, c=c, pc=pc: e.tensor_scalar(
                    o[:, c * 512:(c + 1) * 512], i_[:, 0:512],
                    cx.acol[:, 36 + pc:37 + pc], 128.0 ** -0.25,
                    ALU.add, ALU.mult),
                    reads=(("ps_mm",), ("acol",)), writes=(ktd,),
                    tag=f"{pref}_{m_}ne{h}_{c}")
            trans16(cx, tdst, ktd, ndst, knd, pref, f"{m_}nt{h}")
        t1 = cx.lnbuf[1][:, 0:1024].rearrange("p (t j) -> p t j", t=NT)
        t2 = cx.lnbuf[1][:, 1024:2048].rearrange("p (t j) -> p t j", t=NT)
        for dst, kdst in ((qnat, kqn), (knat, kkn)):
            dv = dst.rearrange("p (t f) -> p t f", t=NT)
            yr = dv[:, :, 0:128:2]
            yi = dv[:, :, 1:128:2]
            P.v(lambda e, o=t1, a=yr, b=ri: e.tensor_tensor(
                o[:, :, :], a[:, :, :], b[:, :, :], ALU.mult),
                reads=(kdst, ("lnbuf", 0)), writes=(("lnbuf", 1),),
                tag=f"{pref}_r1{h}_{kdst[0]}")
            P.v(lambda e, o=t2, a=yi, b=rr: e.tensor_tensor(
                o[:, :, :], a[:, :, :], b[:, :, :], ALU.mult),
                reads=(kdst, ("lnbuf", 0)), writes=(("lnbuf", 1),),
                tag=f"{pref}_r2{h}_{kdst[0]}")
            P.v(lambda e, o=yr, a=yr, b=rr: e.tensor_tensor(
                o[:, :, :], a[:, :, :], b[:, :, :], ALU.mult),
                reads=(kdst, ("lnbuf", 0)), writes=(kdst,),
                tag=f"{pref}_r3{h}_{kdst[0]}")
            P.v(lambda e, o=yi, a=yi, b=ri: e.tensor_tensor(
                o[:, :, :], a[:, :, :], b[:, :, :], ALU.mult),
                reads=(kdst, ("lnbuf", 0)), writes=(kdst,),
                tag=f"{pref}_r4{h}_{kdst[0]}")
            P.v(lambda e, o=yr, a=yr, b=yi: e.tensor_tensor(
                o[:, :, :], a[:, :, :], b[:, :, :], ALU.subtract),
                reads=(kdst,), writes=(kdst,), tag=f"{pref}_r5{h}_{kdst[0]}")
            P.v(lambda e, o=yi, a=t1, b=t2: e.tensor_tensor(
                o[:, :, :], a[:, :, :], b[:, :, :], ALU.add),
                reads=(kdst, ("lnbuf", 1)), writes=(kdst,),
                tag=f"{pref}_r6{h}_{kdst[0]}")
        for dst, kdst, tbuf, ktb in ((qnat, kqn, cx.qt[par], ("qt", par)),
                                     (knat, kkn, cx.kt[par], ("kt", par))):
            for st in range(NT):
                P.v(lambda e, o=cx.mvb, i_=dst, st=st: e.bn_stats(
                    o[:, 6 * (st % 8):6 * (st % 8) + 6],
                    i_[:, st * 128:(st + 1) * 128]),
                    reads=(kdst,), writes=(("mvb",),), tag=f"{pref}_hs{h}_{kdst[0]}_{st}")
                P.v(lambda e, o=cx.mvb, st=st: e.bn_aggr(
                    o[:, 48 + 2 * (st % 8):48 + 2 * (st % 8) + 2],
                    o[:, 6 * (st % 8):6 * (st % 8) + 6]),
                    reads=(("mvb",),), writes=(("mvb",),), tag=f"{pref}_ha{h}_{kdst[0]}_{st}")
                P.s(lambda e, o=cx.mvb, st=st: e.activation(
                    o[:, 32 + st % 8:32 + st % 8 + 1],
                    o[:, 48 + 2 * (st % 8) + 1:48 + 2 * (st % 8) + 2], AF.Ln,
                    bias=cx.cbias[:, 1:2]),
                    reads=(("mvb",), ("cbias",)), writes=(("mvb",),),
                    tag=f"{pref}_hl{h}_{kdst[0]}_{st}")
                P.s(lambda e, o=cx.mvb, st=st: e.activation(
                    o[:, 40 + st % 8:40 + st % 8 + 1], o[:, 32 + st % 8:32 + st % 8 + 1],
                    AF.Exp, scale=-0.5, bias=cx.cbias[:, 0:1]),
                    reads=(("mvb",), ("cbias",)), writes=(("mvb",),),
                    tag=f"{pref}_hr{h}_{kdst[0]}_{st}")
                P.v(lambda e, o=dst, st=st: e.tensor_scalar(
                    o[:, st * 128:(st + 1) * 128], o[:, st * 128:(st + 1) * 128],
                    cx.mvb[:, 48 + 2 * (st % 8):48 + 2 * (st % 8) + 1],
                    cx.mvb[:, 40 + st % 8:40 + st % 8 + 1], ALU.subtract, ALU.mult),
                    reads=(kdst, ("mvb",)), writes=(kdst,),
                    tag=f"{pref}_hn{h}_{kdst[0]}_{st}")
                P.t(lambda e, o=cx.ps_tr, i_=dst, st=st: e.transpose(
                    o[:, (st % 8) * 128:(st % 8 + 1) * 128],
                    i_[:, st * 128:(st + 1) * 128], cx.ident[:, :]),
                    reads=(kdst, ("ident",)), writes=(("ps_tr",),),
                    tag=f"{pref}_ht{h}_{kdst[0]}_{st}")
                P.v(lambda e, o=tbuf, i_=cx.ps_tr, st=st: e.tensor_scalar(
                    o[:, st * 128:(st + 1) * 128], i_[:, (st % 8) * 128:(st % 8 + 1) * 128],
                    cx.acol[:, 34:35], cx.acol[:, 35:36], ALU.mult, ALU.add),
                    reads=(("ps_tr",), ("acol",)), writes=(ktb,),
                    tag=f"{pref}_hte{h}_{kdst[0]}_{st}")
        for c in range(4):
            for dt in range(ND):
                P.t(lambda e, o=cx.ps_mm, w=cx.wv_sb[par], x=xlt, dt=dt, c=c,
                    st=(dt == 0), sp=(dt == ND - 1): e.matmul(
                    o[:, 0:512], w[:, dt * 128:(dt + 1) * 128],
                    x[:, dt, c * 512:(c + 1) * 512], start=st, stop=sp),
                    reads=(("wv_sb", par),) + XLN_KEYS, writes=(("ps_mm",),),
                    tag=f"{pref}_v{h}_{c}_{dt}")
            P.v(lambda e, o=cx.vtt[par], i_=cx.ps_mm: e.tensor_scalar(
                o[:, :], i_[:, 0:512], cx.acol[:, 38:39], None, ALU.add),
                reads=(("ps_mm",), ("acol",)), writes=(("res_sb", par),),
                tag=f"{pref}_ve{h}_{c}")
            for j in range(4):
                P.t(lambda e, o=cx.ps_tr, i_=cx.vtt[par], j=j: e.transpose(
                    o[:, j * 128:(j + 1) * 128], i_[:, j * 128:(j + 1) * 128],
                    cx.ident[:, :]),
                    reads=(("res_sb", par), ("ident",)), writes=(("ps_tr",),),
                    tag=f"{pref}_vtr{h}_{c}_{j}")
            P.v(lambda e, o=cx.vt[par], i_=cx.ps_tr, c=c: e.tensor_copy(
                o[:, c * 512:(c + 1) * 512], i_[:, 0:512]),
                reads=(("ps_tr",),), writes=(("vt", par),), tag=f"{pref}_vte{h}_{c}")
        attn_core(cx, pref, h, par, causal=False, mask_sb=None)
        P.dma("s", lambda e, d=att, s=cx.ot[par], h=h: e.dma_start(
            d[h * 128:(h + 1) * 128, :], s[:, :]),
            reads=(("ot", par),), writes=(("ATT", h),), tag=f"{pref}_ost{h}",
            ch=f"att{par}")
        att_keys.append(("ATT", h))
    P.dma("p", lambda e, d=cx.rowbf, s=bo_v: e.dma_start(
        d, s.rearrange("(a n) -> a n", a=1)),
        reads=(), writes=(("ot", 0),), tag=f"{pref}_bold", ch="smw2")
    emit_out_proj(cx, pref, att, wo, xres, bias_row=cx.rowbf,
                  att_keys=tuple(att_keys), bias_key=("ot", 0))



def rope64_nat(cx, buf, key, tab, tkey, tmps, tmpkey, pref, tag):
    P = cx.P
    v = buf.rearrange("p (t f) -> p t f", t=NT)
    A = v[:, :, 0:32]
    B = v[:, :, 32:64]
    ct, st_ = tab
    cv = ct.rearrange("p (t j) -> p t j", t=NT)
    sv = st_.rearrange("p (t j) -> p t j", t=NT)
    cA, cB = cv[:, :, 0:32], cv[:, :, 32:64]
    sA, sB = sv[:, :, 0:32], sv[:, :, 32:64]
    t1, t2 = tmps
    tks = tuple(tkey)
    P.v(lambda e: e.tensor_tensor(t1[:, :, :], A[:, :, :], sB[:, :, :], ALU.mult),
        reads=(key,) + tks, writes=(tmpkey,), tag=f"{pref}_{tag}p1")
    P.v(lambda e: e.tensor_tensor(A[:, :, :], A[:, :, :], cA[:, :, :], ALU.mult),
        reads=(key,) + tks, writes=(key,), tag=f"{pref}_{tag}p2")
    P.v(lambda e: e.tensor_tensor(t2[:, :, :], B[:, :, :], sA[:, :, :], ALU.mult),
        reads=(key,) + tks, writes=(tmpkey,), tag=f"{pref}_{tag}p3")
    P.v(lambda e: e.tensor_tensor(A[:, :, :], A[:, :, :], t2[:, :, :], ALU.subtract),
        reads=(key, tmpkey), writes=(key,), tag=f"{pref}_{tag}p4")
    P.v(lambda e: e.tensor_tensor(B[:, :, :], B[:, :, :], cB[:, :, :], ALU.mult),
        reads=(key,) + tks, writes=(key,), tag=f"{pref}_{tag}p5")
    P.v(lambda e: e.tensor_tensor(B[:, :, :], B[:, :, :], t1[:, :, :], ALU.add),
        reads=(key, tmpkey), writes=(key,), tag=f"{pref}_{tag}p6")


def trans16(cx, src, skey, dst, dkey, pref, tag, scale=None):
    P = cx.P
    for k in range(ND):
        P.t(lambda e, o=cx.ps_tr, i_=src, k=k: e.transpose(
            o[:, k * 128:(k + 1) * 128], i_[:, k * 128:(k + 1) * 128],
            cx.ident[:, :]),
            reads=(skey, ("ident",)), writes=(("ps_tr",),), tag=f"{pref}_{tag}t{k}")
    if scale is None:
        P.v(lambda e, o=dst, i_=cx.ps_tr: e.tensor_copy(o[:, :], i_[:, :]),
            reads=(("ps_tr",),), writes=(dkey,), tag=f"{pref}_{tag}e")
    else:
        P.v(lambda e, o=dst, i_=cx.ps_tr, sc=scale: e.tensor_scalar(
            o[:, :], i_[:, :], float(sc), None, ALU.mult),
            reads=(("ps_tr",),), writes=(dkey,), tag=f"{pref}_{tag}e")


def stage_wT_pe(cx, pref, w_dram, row0, stage, skey, dst, dkey, tag, ch,
                nrow=128, col0=0, ncols=D):
    P = cx.P
    nd = ncols // 128
    P.dma("p", lambda e, d=stage, s=w_dram: e.dma_start(
        d[0:nrow, 0:ncols], s[row0:row0 + nrow, col0:col0 + ncols]),
        reads=(), writes=(skey,), tag=f"{tag}_ld", ch=ch)
    for k in range(nd):
        P.t(lambda e, o=cx.ps_tr, i_=stage, k=k, nrow=nrow: e.transpose(
            o[:, (k % 8) * 128:(k % 8) * 128 + nrow],
            i_[0:nrow, 128 * k:128 * (k + 1)], cx.ident[:, :]),
            reads=(skey, ("ident",)), writes=(("ps_tr",),), tag=f"{tag}_t{k}")
        P.v(lambda e, o=dst, i_=cx.ps_tr, k=k, nrow=nrow: e.tensor_copy(
            o[:, k * nrow:(k + 1) * nrow], i_[:, (k % 8) * 128:(k % 8) * 128 + nrow]),
            reads=(("ps_tr",),), writes=(dkey,), tag=f"{tag}_e{k}")


def emit_l1_attn_a(cx, xres):
    P = cx.P
    pref = "a1"
    wq = cx.din("L1_attn_a_W_q", (D, D), F32)
    wkv = cx.din("L1_attn_a_W_kv", (256, D), F32)
    wkh = cx.din("L1_attn_a_W_k_heads", (16, 128, 128), F32)
    wvh = cx.din("L1_attn_a_W_v_heads", (16, 128, 128), F32)
    wo = cx.din("L1_attn_a_W_o", (D, D), F32)
    cosP = cx.din("c_l1a_cos", (128, NT * 64), F32)
    sinP = cx.din("c_l1a_sin", (128, NT * 64), F32)
    att = cx.dram["ATT"]

    emit_ln_phase(cx, xres, pref)
    xlt = cx.xlnT.rearrange("p (dt s) -> p dt s", dt=ND)

    cosN = cx.xnat[0].bitcast(F32)
    sinN = cx.xnat[1].bitcast(F32)
    P.dma("y", lambda e, d=cosN, s=cosP: e.dma_start(d[:, :], s[:, :]),
          reads=(), writes=(("xnat", 0),), tag=f"{pref}_cosN", ch="lnbuf0")
    P.dma("y", lambda e, d=sinN, s=sinP: e.dma_start(d[:, :], s[:, :]),
          reads=(), writes=(("xnat", 1),), tag=f"{pref}_sinN", ch="lnbuf1")
    tmps = (cx.lnbuf[1][:, 0:512].rearrange("p (t j) -> p t j", t=NT),
            cx.lnbuf[1][:, 512:1024].rearrange("p (t j) -> p t j", t=NT))

    for wsrc, dstc, tag in ((wkh, cx.wkcol, "wk"), (wvh, cx.wvcol, "wv")):
        for h in range(16):
            stg = cx.lnbuf[0][:, 0:128]
            P.dma("y", lambda e, d=stg, s=wsrc, h=h: e.dma_start(d, s[h, :, :]),
                  reads=(), writes=(("lnbuf", 0),), tag=f"{pref}_{tag}h{h}",
                  ch="lnbuf0")
            P.v(lambda e, o=dstc, i_=stg, h=h: e.reduce_sum(
                o[:, h:h + 1], i_, axis=AX.X),
                reads=(("lnbuf", 0),), writes=((tag + "col",),),
                tag=f"{pref}_{tag}s{h}")

    stage_wT_pe(cx, pref, wkv, 0, cx.qt[0], ("qt", 0), cx.wk_sb[0], ("wk_sb", 0),
                f"{pref}_kc", ch="stg0")
    stage_wT_pe(cx, pref, wkv, 128, cx.qt[1], ("qt", 1), cx.wv_sb[0], ("wv_sb", 0),
                f"{pref}_vc", ch="stg1")
    kcT = cx.at_sb[0]
    kkc = ("at_sb", 0)
    vcT = cx.at_sb[1]
    kvc = ("at_sb", 1)
    for (wbuf, wkey, dst, dkey, tg) in ((cx.wk_sb[0], ("wk_sb", 0), kcT, kkc, "kc"),
                                        (cx.wv_sb[0], ("wv_sb", 0), vcT, kvc, "vc")):
        for c in range(4):
            for dt in range(ND):
                P.t(lambda e, o=cx.ps_mm, w=wbuf, x=xlt, dt=dt, c=c,
                    st=(dt == 0), sp=(dt == ND - 1): e.matmul(
                    o[:, 0:512], w[:, dt * 128:(dt + 1) * 128],
                    x[:, dt, c * 512:(c + 1) * 512], start=st, stop=sp),
                    reads=(wkey,) + XLN_KEYS, writes=(("ps_mm",),),
                    tag=f"{pref}_{tg}{c}_{dt}")
            P.s(lambda e, o=dst, i_=cx.ps_mm, c=c: e.copy(
                out=o[:, c * 512:(c + 1) * 512], in_=i_[:, 0:512]),
                reads=(("ps_mm",),), writes=(dkey,), tag=f"{pref}_{tg}e{c}")
    for c in range(4):
        for j in range(4):
            P.t(lambda e, o=cx.ps_tr, i_=vcT, c=c, j=j: e.transpose(
                o[:, j * 128:(j + 1) * 128],
                i_[:, (c * 4 + j) * 128:(c * 4 + j + 1) * 128], cx.ident[:, :]),
                reads=(kvc, ("ident",)), writes=(("ps_tr",),),
                tag=f"{pref}_vn{c}_{j}")
        P.v(lambda e, o=cx.vt[0], i_=cx.ps_tr, c=c: e.tensor_copy(
            o[:, c * 512:(c + 1) * 512], i_[:, 0:512]),
            reads=(("ps_tr",),), writes=(("vt", 0),), tag=f"{pref}_vne{c}")

    att_keys = []
    for h in range(16):
        par = h % 2
        stage_wT_pe(cx, pref, wq, h * 128, cx.ot[par], ("ot", par),
                    cx.wq_sb[par], ("wq_sb", par), f"{pref}_wq{h}_{par}",
                    ch=f"stg{par}")
        for c in range(4):
            for dt in range(ND):
                P.t(lambda e, o=cx.ps_mm, w=cx.wq_sb[par], x=xlt, dt=dt, c=c,
                    st=(dt == 0), sp=(dt == ND - 1): e.matmul(
                    o[:, 0:512], w[:, dt * 128:(dt + 1) * 128],
                    x[:, dt, c * 512:(c + 1) * 512], start=st, stop=sp),
                    reads=(("wq_sb", par),) + XLN_KEYS, writes=(("ps_mm",),),
                    tag=f"{pref}_q{h}_{c}_{dt}")
            P.s(lambda e, o=cx.qt[par], i_=cx.ps_mm, c=c: e.copy(
                out=o[:, c * 512:(c + 1) * 512], in_=i_[:, 0:512]),
                reads=(("ps_mm",),), writes=(("qt", par),), tag=f"{pref}_qe{h}_{c}")
        trans16(cx, cx.qt[par], ("qt", par), cx.at_sb[1], ("at_sb", 1),
                pref, f"qn{h}")
        rope64_nat(cx, cx.at_sb[1], ("at_sb", 1), (cosN, sinN),
                   (("xnat", 0), ("xnat", 1)), tmps, ("lnbuf", 1),
                   pref, f"rq{h}")
        trans16(cx, cx.at_sb[1], ("at_sb", 1), cx.qt[par], ("qt", par),
                pref, f"qb{h}", scale=128.0 ** -0.5)
        P.v(lambda e, o=cx.kt[par], i_=kcT, h=h: e.tensor_scalar(
            o[:, :], i_[:, :], cx.wkcol[:, h:h + 1], None, ALU.mult),
            reads=(kkc, ("wkcol",)), writes=(("kt", par),), tag=f"{pref}_kw{h}")
        trans16(cx, cx.kt[par], ("kt", par), cx.at_sb[1], ("at_sb", 1),
                pref, f"kn{h}")
        rope64_nat(cx, cx.at_sb[1], ("at_sb", 1), (cosN, sinN),
                   (("xnat", 0), ("xnat", 1)), tmps, ("lnbuf", 1),
                   pref, f"rk{h}")
        trans16(cx, cx.at_sb[1], ("at_sb", 1), cx.kt[par], ("kt", par),
                pref, f"kb{h}")
        attn_core(cx, pref, h, par, causal=True, mask_sb=cx.maskinf,
                  bufs=((cx.qt[par], ("qt", par)), (cx.kt[par], ("kt", par)),
                        (cx.vt[0], ("vt", 0)), (cx.ot[par], ("ot", par))),
                  oscale=(cx.wvcol[:, h:h + 1], ("wvcol",)))
        P.dma("s", lambda e, d=att, s=cx.ot[par], h=h: e.dma_start(
            d[h * 128:(h + 1) * 128, :], s[:, :]),
            reads=(("ot", par),), writes=(("ATT", h),), tag=f"{pref}_ost{h}",
            ch=f"att{par}")
        att_keys.append(("ATT", h))
    emit_out_proj(cx, pref, att, wo, xres, wT="pe", att_keys=tuple(att_keys))



def softmax_free_tile(cx, pref, tag, buf, key, dst, dkey):
    P = cx.P
    sm = cx.sm[0]
    ksm = ("sm", 0)
    P.v(lambda e, o=sm, i_=buf: e.reduce_max(
        o[:, 0:1], i_[:, :], axis=AX.X, negate=True),
        reads=(key,), writes=(ksm,), tag=f"{pref}_{tag}mx")
    P.s(lambda e, o=dst, i_=buf, sm=sm: e.activation(
        o[:, :], i_[:, :], AF.Exp, bias=sm[:, 0:1], accum_out=sm[:, 2:3]),
        reads=(key, ksm), writes=(dkey, ksm), tag=f"{pref}_{tag}e1")
    P.s(lambda e, o=sm: e.activation(o[:, 3:4], o[:, 2:3], AF.Ln,
        bias=cx.cbias[:, 0:1]),
        reads=(ksm, ("cbias",)), writes=(ksm,), tag=f"{pref}_{tag}ln")
    P.v(lambda e, o=sm: e.tensor_tensor(o[:, 4:5], o[:, 0:1], o[:, 3:4],
                                        ALU.subtract),
        reads=(ksm,), writes=(ksm,), tag=f"{pref}_{tag}b2")
    P.s(lambda e, o=dst, i_=buf, sm=sm: e.activation(
        o[:, :], i_[:, :], AF.Exp, bias=sm[:, 4:5]),
        reads=(key, ksm), writes=(dkey,), tag=f"{pref}_{tag}e2")


def emit_l1_ffn(cx, xres, app):
    P = cx.P
    pref = f"m1{app}"
    gamma = cx.din("L1_ffn_gamma", (D,), F32)
    beta = cx.din("L1_ffn_beta", (D,), F32)
    mu = cx.din("L1_ffn_mu", (D, 16, 128), F32)
    scale_in = cx.din("L1_ffn_scale", (1,), F32)
    w1 = cx.din("L1_ffn_W1", (16, D, 512), F32)
    b1 = cx.din("L1_ffn_b1", (16, 512), F32)
    w2 = cx.din("L1_ffn_W2", (16, 512, D), F32)
    b2 = cx.din("L1_ffn_b2", (16, D), F32)
    mu2 = mu.rearrange("d e s -> d (e s)")
    lgn, dspn = cx.dram["LGN"], cx.dram["DSPN"]
    cmbt, slin, slout = cx.dram["CMBT"], cx.dram["SLIN"], cx.dram["SLOT_OUT"]
    xlnd = cx.dram["XLN"]

    emit_ln_phase(cx, xres, pref, inner_ln=True)
    if not getattr(cx, "b1col_ready", False):
        cx.b1col_ready = True
        load_cols(cx, b1.rearrange("e (a f) -> (e a) f", a=4), 64,
                  cx.b1col, 0, ("b1col",), f"{pref}_b1c")
    for vec, col0, tag in ((gamma, 0, "g"), (beta, 16, "b")):
        load_cols(cx, vec.rearrange("(a f) -> a f", a=16), 16, cx.acol,
                  col0, ("acol",), f"{pref}_{tag}c")
    P.dma("y", lambda e, d=cx.rowbuf, s=scale_in: e.dma_start(
        d[0:1, 0:1], s.rearrange("(a n) -> a n", a=1)),
        reads=(), writes=(("res_sb", 0),), tag=f"{pref}_scr", ch="rowld")
    P.t(lambda e, o=cx.ps_o: e.matmul(
        o[:, 0:1], cx.ones_f32[0:1, 0:128], cx.rowbuf[0:1, 0:1],
        start=True, stop=True),
        reads=(("ones_f32",), ("res_sb", 0)), writes=(("ps_o",),),
        tag=f"{pref}_scm")
    P.v(lambda e, o=cx.acol, i_=cx.ps_o: e.tensor_copy(o[:, 32:33], i_[:, 0:1]),
        reads=(("ps_o",),), writes=(("acol",),), tag=f"{pref}_sce")
    xlt = cx.xlnT.rearrange("p (dt s) -> p dt s", dt=ND)
    for dt in range(ND):
        P.v(lambda e, x=xlt, dt=dt: e.tensor_scalar(
            x[:, dt, :], x[:, dt, :], cx.acol[:, dt:dt + 1],
            cx.acol[:, 16 + dt:17 + dt], ALU.mult, ALU.add),
            reads=tuple(XLN_KEYS) + (("acol",),), writes=XLN_KEYS,
            tag=f"{pref}_gb{dt}")
    for n in range(NT):
        par = n % 2
        xnat = cx.xnat[par]
        kx = ("xnat", par)
        for k in range(ND):
            P.t(lambda e, o=cx.ps_tr, x=xlt, k=k, n=n: e.transpose(
                o[:, k * 128:(k + 1) * 128], x[:, k, n * 128:(n + 1) * 128],
                cx.ident[:, :]),
                reads=XLN_KEYS + (("ident",),), writes=(("ps_tr",),),
                tag=f"{pref}_nt{n}_{k}")
        P.v(lambda e, o=xnat, i_=cx.ps_tr: e.tensor_copy(o[:, :], i_[:, :]),
            reads=(("ps_tr",),), writes=(kx,), tag=f"{pref}_ne{n}")
        P.dma("s", lambda e, d=xlnd, s=xnat, n=n: e.dma_start(
            d[n * 128:(n + 1) * 128, :], s[:, :]),
            reads=(kx,), writes=(("XLN", n),), tag=f"{pref}_xs{n}",
            ch=f"spill{par}")
        lb = cx.lnbuf[par]
        kl = ("lnbuf", par)
        P.s(lambda e, o=lb, i_=xnat, n=n: e.activation(
            o[:, :], i_[:, :], AF.Square, accum_out=cx.mnorm[:, n:n + 1]),
            reads=(kx,), writes=(kl, ("mnorm",)), tag=f"{pref}_sq{n}")
        P.s(lambda e, o=cx.mvb, n=n, par=par: e.activation(
            o[:, 32 + par:33 + par], cx.mnorm[:, n:n + 1], AF.Ln,
            bias=cx.cbias[:, 2:3]),
            reads=(("mnorm",), ("cbias",)), writes=(("mvb",),),
            tag=f"{pref}_nl{n}")
        P.s(lambda e, o=cx.mvb, par=par: e.activation(
            o[:, 34 + par:35 + par], o[:, 32 + par:33 + par], AF.Exp, scale=0.5,
            bias=cx.cbias[:, 0:1]),
            reads=(("mvb",), ("cbias",)), writes=(("mvb",),),
            tag=f"{pref}_nsq{n}")
        P.v(lambda e, o=cx.mvb, par=par: e.tensor_scalar(
            o[:, 36 + par:37 + par], o[:, 34 + par:35 + par], 1e-12, None,
            ALU.max),
            reads=(("mvb",),), writes=(("mvb",),), tag=f"{pref}_nm{n}")
        P.v(lambda e, o=cx.mvb, par=par: e.reciprocal(
            o[:, 38 + par:39 + par], o[:, 36 + par:37 + par]),
            reads=(("mvb",),), writes=(("mvb",),), tag=f"{pref}_nr{n}")
        P.v(lambda e, o=cx.mvb, par=par: e.tensor_scalar(
            o[:, 40 + par:41 + par], o[:, 38 + par:39 + par],
            cx.acol[:, 32:33], None, ALU.mult),
            reads=(("mvb",), ("acol",)), writes=(("mvb",),), tag=f"{pref}_nf{n}")
        xnb = lb.bitcast(BF16)[:, 0:2048]
        P.v(lambda e, o=xnb, i_=xnat, par=par: e.tensor_scalar(
            o[:, :], i_[:, :], cx.mvb[:, 40 + par:41 + par], None, ALU.mult),
            reads=(kx, ("mvb",)), writes=(kl,), tag=f"{pref}_xn{n}")
        for k in range(ND):
            P.t(lambda e, o=cx.ps_tr, i_=xnb, k=k: e.transpose(
                o[:, k * 128:(k + 1) * 128], i_[:, k * 128:(k + 1) * 128],
                cx.ident[:, :]),
                reads=(kl, ("ident",)), writes=(("ps_tr",),),
                tag=f"{pref}_xt{n}_{k}")
        P.v(lambda e, o=xlt, i_=cx.ps_tr.rearrange("p (dt s) -> p dt s", dt=ND),
            n=n: e.tensor_copy(o[:, :, n * 128:(n + 1) * 128], i_[:, :, :]),
            reads=(("ps_tr",),), writes=XLN_KEYS, tag=f"{pref}_xte{n}")

    for dt in range(ND):
        par = dt % 2
        msb = cx.qt[par]
        km = ("qt", par)
        P.dma("p", lambda e, d=msb, s=mu2, dt=dt: e.dma_start(
            d[:, :], s[dt * 128:(dt + 1) * 128, :]),
            reads=(), writes=(km,), tag=f"{pref}_mu{dt}", ch=f"mul{par}")
        sq = cx.kt[par]
        ksq = ("kt", par)
        P.s(lambda e, o=sq, i_=msb: e.activation(o[:, :], i_[:, :], AF.Square),
            reads=(km,), writes=(ksq,), tag=f"{pref}_musq{dt}")
        for c in range(4):
            P.t(lambda e, o=cx.ps_sc, i_=sq, c=c, dt=dt, st=(dt == 0),
                sp=(dt == ND - 1): e.matmul(
                o[0:1, c * 512:(c + 1) * 512], cx.onescol[:, 0:1],
                i_[:, c * 512:(c + 1) * 512], start=st, stop=sp),
                reads=(ksq, ("onescol",)), writes=(("ps_sc",),),
                tag=f"{pref}_mus{dt}_{c}")
    for c in range(4):
        P.v(lambda e, o=cx.rowbuf, i_=cx.ps_sc, c=c: e.tensor_copy(
            o, i_[0:1, c * 512:(c + 1) * 512]),
            reads=(("ps_sc",),), writes=(("res_sb", 0),), tag=f"{pref}_mne{c}")
        row_to_cols(cx, cx.rowbuf, ("res_sb", 0), 512, cx.mvb, c * 4,
                    ("mvb",), f"{pref}_mnc{c}")
    P.s(lambda e, o=cx.mvb: e.activation(o[:, 16:32], o[:, 0:16], AF.Ln,
        bias=cx.cbias[:, 2:3]),
        reads=(("mvb",), ("cbias",)), writes=(("mvb",),), tag=f"{pref}_cl")
    P.s(lambda e, o=cx.mvb: e.activation(o[:, 16:32], o[:, 16:32], AF.Exp,
        scale=0.5, bias=cx.cbias[:, 0:1]),
        reads=(("mvb",), ("cbias",)), writes=(("mvb",),), tag=f"{pref}_ce")
    P.v(lambda e, o=cx.mvb: e.tensor_scalar(
        o[:, 16:32], o[:, 16:32], 1e-12, None, ALU.max),
        reads=(("mvb",),), writes=(("mvb",),), tag=f"{pref}_cm")
    P.v(lambda e, o=cx.mvb: e.reciprocal(o[:, 16:32], o[:, 16:32]),
        reads=(("mvb",),), writes=(("mvb",),), tag=f"{pref}_cr")

    for slt in range(16):
        par = slt % 2
        musl = cx.wq_sb[par]
        kmu = ("wq_sb", par)
        msrc = mu2.rearrange("(dt p) c -> p dt c", p=128)[
            :, :, slt * 128:(slt + 1) * 128]
        P.dma("p", lambda e, d=musl, s=msrc: e.dma_start(
            d.rearrange("p (dt c) -> p dt c", dt=ND)[:, :, :], s),
            reads=(), writes=(kmu,), tag=f"{pref}_ms{slt}", ch=f"wq{par}")
        lgt = cx.qt[par]
        klg = ("qt", par)
        for c in range(4):
            for dt in range(ND):
                P.t(lambda e, o=cx.ps_mm, w=musl, x=xlt, dt=dt, c=c,
                    st=(dt == 0), sp=(dt == ND - 1): e.matmul(
                    o[:, 0:512], w[:, dt * 128:(dt + 1) * 128],
                    x[:, dt, c * 512:(c + 1) * 512], start=st, stop=sp),
                    reads=(kmu,) + XLN_KEYS, writes=(("ps_mm",),),
                    tag=f"{pref}_lg{slt}_{c}_{dt}")
            P.v(lambda e, o=lgt, i_=cx.ps_mm, c=c, slt=slt: e.tensor_scalar(
                o[:, c * 512:(c + 1) * 512], i_[:, 0:512],
                cx.mvb[:, 16 + slt:17 + slt], None, ALU.mult),
                reads=(("ps_mm",), ("mvb",)), writes=(klg,),
                tag=f"{pref}_lge{slt}_{c}")
        dsp = cx.kt[par]
        kdsp = ("kt", par)
        softmax_free_tile(cx, pref, f"d{slt}", lgt, klg, dsp, kdsp)
        for (tile, tkey, stg, skey, dram, tg) in (
                (dsp, kdsp, cx.vt[0], ("vt", 0), dspn, "dt"),
                (lgt, klg, cx.vt[1], ("vt", 1), lgn, "lt")):
            for k in range(ND):
                P.t(lambda e, o=cx.ps_tr, i_=tile, k=k: e.transpose(
                    o[:, k * 128:(k + 1) * 128], i_[:, k * 128:(k + 1) * 128],
                    cx.ident[:, :]),
                    reads=(tkey, ("ident",)), writes=(("ps_tr",),),
                    tag=f"{pref}_{tg}t{slt}_{k}")
            P.v(lambda e, o=stg, i_=cx.ps_tr: e.tensor_copy(o[:, :], i_[:, :]),
                reads=(("ps_tr",),), writes=(skey,), tag=f"{pref}_{tg}e{slt}")
            dst = dram.rearrange("(nt p) c -> p nt c", p=128)[
                :, :, slt * 128:(slt + 1) * 128]
            P.dma("s", lambda e, d=dst, s=stg: e.dma_start(
                d, s.rearrange("p (nt c) -> p nt c", nt=NT)[:, :, :]),
                reads=(skey,), writes=((tg.upper(), slt),),
                tag=f"{pref}_{tg}s{slt}", ch=f"lst{tg}")
    dsp_keys = tuple(("DT", s_) for s_ in range(16))
    lgn_keys = tuple(("LT", s_) for s_ in range(16))

    for nt in range(NT):
        par = nt % 2
        lgrow = cx.qt[par]
        klr = ("qt", par)
        P.dma("y", lambda e, d=lgrow, s=lgn, nt=nt: e.dma_start(
            d[:, :], s[nt * 128:(nt + 1) * 128, :]),
            reads=lgn_keys, writes=(klr,), tag=f"{pref}_cbl{nt}", ch=f"cb{par}")
        cmb = cx.kt[par]
        kcm = ("kt", par)
        softmax_free_tile(cx, pref, f"c{nt}", lgrow, klr, cmb, kcm)
        for k in range(ND):
            P.t(lambda e, o=cx.ps_tr, i_=cmb, k=k: e.transpose(
                o[:, k * 128:(k + 1) * 128], i_[:, k * 128:(k + 1) * 128],
                cx.ident[:, :]),
                reads=(kcm, ("ident",)), writes=(("ps_tr",),),
                tag=f"{pref}_cbt{nt}_{k}")
        stg = cx.vt[0]
        P.v(lambda e, o=stg, i_=cx.ps_tr: e.tensor_copy(o[:, :], i_[:, :]),
            reads=(("ps_tr",),), writes=(("vt", 0),), tag=f"{pref}_cbe{nt}")
        dst = cmbt.rearrange("(st p) c -> p st c", p=128)[
            :, :, nt * 128:(nt + 1) * 128]
        P.dma("s", lambda e, d=dst, s=stg: e.dma_start(
            d, s.rearrange("p (st c) -> p st c", st=NT)[:, :, :]),
            reads=(("vt", 0),), writes=(("CMBT", nt),), tag=f"{pref}_cbs{nt}",
            ch="lstdt")
    cmb_keys = tuple(("CMBT", n_) for n_ in range(NT))

    xl_half = cx.xlnT.rearrange("p (h nt c) -> p h nt c", h=2, nt=NT)
    for dh in range(2):
        xsrc = xlnd.rearrange("(nt p) c -> p nt c", p=128)[
            :, :, dh * 1024:(dh + 1) * 1024]
        P.dma("y", lambda e, d=xl_half, s=xsrc, dh=dh: e.dma_start(
            d[:, 0, :, :], s),
            reads=tuple(("XLN", n_) for n_ in range(NT)),
            writes=(XLN_KEYS[0],), tag=f"{pref}_sxl{dh}", ch="panel")
        for sh in range(2):
            dsrc = dspn.rearrange("(nt p) c -> p nt c", p=128)[
                :, :, sh * 1024:(sh + 1) * 1024]
            P.dma("y", lambda e, d=xl_half, s=dsrc, sh=sh: e.dma_start(
                d[:, 1, :, :], s),
                reads=dsp_keys, writes=(XLN_KEYS[1],), tag=f"{pref}_sdl{dh}_{sh}",
                ch="panel2")
            for dt2 in range(8):
                for c2 in range(2):
                    for nt in range(NT):
                        P.t(lambda e, o=cx.ps_mm, x=xl_half, nt=nt, dt2=dt2,
                            c2=c2, st=(nt == 0), sp=(nt == NT - 1): e.matmul(
                            o[:, 0:512], x[:, 0, nt, dt2 * 128:(dt2 + 1) * 128],
                            x[:, 1, nt, c2 * 512:(c2 + 1) * 512],
                            start=st, stop=sp),
                            reads=(XLN_KEYS[0], XLN_KEYS[1]),
                            writes=(("ps_mm",),),
                            tag=f"{pref}_si{dh}_{sh}_{dt2}_{c2}_{nt}")
                    stg2 = cx.at_sb[0]
                    P.v(lambda e, o=stg2, i_=cx.ps_mm, c2=c2: e.tensor_copy(
                        o[:, c2 * 512:(c2 + 1) * 512], i_[:, 0:512]),
                        reads=(("ps_mm",),), writes=(("at_sb", 0),),
                        tag=f"{pref}_sie{dh}_{sh}_{dt2}_{c2}")
                P.dma("s", lambda e, d=slin, s=stg2, dh=dh, sh=sh, dt2=dt2:
                      e.dma_start(d[dh * 1024 + dt2 * 128:dh * 1024 + (dt2 + 1) * 128,
                                    sh * 1024:(sh + 1) * 1024], s[:, 0:1024]),
                      reads=(("at_sb", 0),), writes=(("SLIN",),),
                      tag=f"{pref}_sis{dh}_{sh}_{dt2}", ch="lstlt")

    for e_ in range(16):
        par = e_ % 2
        slsb = cx.qt[par]
        ksl = ("qt", par)
        ssrc = slin.rearrange("(dt p) c -> p dt c", p=128)[
            :, :, e_ * 128:(e_ + 1) * 128]
        P.dma("y", lambda e, d=slsb, s=ssrc: e.dma_start(
            d.rearrange("p (dt c) -> p dt c", dt=ND)[:, :, :], s),
            reads=(("SLIN",),), writes=(ksl,), tag=f"{pref}_esl{e_}",
            ch=f"cb{par}")
        P.dma("p", lambda e, d=cx.wo_sb, s=w1, e_=e_: e.dma_start(
            d.rearrange("p (dt c) -> p dt c", dt=ND)[:, :, :],
            s.rearrange("e (dt p) c -> p e dt c", p=128)[:, e_, :, :]),
            reads=(), writes=(("wo_sb",),), tag=f"{pref}_ew1{e_}", ch="wo")
        hbuf = cx.kt[par]
        kh = ("kt", par)
        for ht in range(4):
            for dt in range(ND):
                P.t(lambda e, o=cx.ps_mm, w=cx.wo_sb, x=slsb, dt=dt, ht=ht,
                    st=(dt == 0), sp=(dt == ND - 1): e.matmul(
                    o[:, 0:128], w[:, dt * 512 + ht * 128:dt * 512 + (ht + 1) * 128],
                    x[:, dt * 128:(dt + 1) * 128], start=st, stop=sp),
                    reads=(("wo_sb",), ksl), writes=(("ps_mm",),),
                    tag=f"{pref}_eh{e_}_{ht}_{dt}")
            P.s(lambda e, o=hbuf, i_=cx.ps_mm, ht=ht, e_=e_: e.activation(
                o[:, ht * 128:(ht + 1) * 128], i_[:, 0:128], AF.Gelu,
                bias=cx.b1col[:, e_ * 4 + ht:e_ * 4 + ht + 1]),
                reads=(("ps_mm",), ("b1col",)), writes=(kh,),
                tag=f"{pref}_ehe{e_}_{ht}")
        P.dma("p", lambda e, d=cx.wdb[1], s=w2, e_=e_: e.dma_start(
            d.rearrange("p (ht c) -> p ht c", ht=4)[:, :, :],
            s.rearrange("e (ht p) c -> p e ht c", p=128)[:, e_, :, :]),
            reads=(), writes=(("wdb1",),), tag=f"{pref}_ew2{e_}", ch="wd1")
        P.dma("p", lambda e, d=cx.rowbf, s=b2, e_=e_: e.dma_start(
            d, s[e_:e_ + 1, :]),
            reads=(), writes=(("ot", 0),), tag=f"{pref}_eb2{e_}", ch="wd0")
        so_sb = cx.at_sb[1]
        kso = ("at_sb", 1)
        for dc in range(4):
            for ht in range(4):
                P.t(lambda e, o=cx.ps_mm, hb=hbuf, w=cx.wdb[1], ht=ht, dc=dc,
                    st=(ht == 0): e.matmul(
                    o[:, 0:512], hb[:, ht * 128:(ht + 1) * 128],
                    w[:, ht * 2048 + dc * 512:ht * 2048 + (dc + 1) * 512],
                    start=st, stop=False),
                    reads=(kh, ("wdb1",)), writes=(("ps_mm",),),
                    tag=f"{pref}_eo{e_}_{dc}_{ht}")
            P.t(lambda e, o=cx.ps_mm, dc=dc: e.matmul(
                o[:, 0:512], cx.ones_bf[0:1, 0:128],
                cx.rowbf[:, dc * 512:(dc + 1) * 512], start=False, stop=True),
                reads=(("ones_bf",), ("ot", 0)), writes=(("ps_mm",),),
                tag=f"{pref}_eob{e_}_{dc}")
            P.v(lambda e, o=so_sb, i_=cx.ps_mm, dc=dc: e.tensor_copy(
                o[:, dc * 512:(dc + 1) * 512], i_[:, 0:512]),
                reads=(("ps_mm",),), writes=(kso,), tag=f"{pref}_eoe{e_}_{dc}")
        P.dma("s", lambda e, d=slout, s=so_sb, e_=e_: e.dma_start(
            d[e_ * 128:(e_ + 1) * 128, :], s[:, :]),
            reads=(kso,), writes=(("SLOUT", e_),), tag=f"{pref}_eos{e_}",
            ch="lstlt")
    so_keys = tuple(("SLOUT", e2) for e2 in range(16))

    for dc in range(4):
        P.dma("p", lambda e, d=cx.wo_sb, s=slout, dc=dc: e.dma_start(
            d.rearrange("p (st c) -> p st c", st=ND)[:, :, :],
            s.rearrange("(st p) c -> p st c", p=128)[:, :, dc * 512:(dc + 1) * 512]),
            reads=so_keys, writes=(("wo_sb",),), tag=f"{pref}_fso{dc}", ch="wo")
        for nt in range(NT):
            par = nt % 2
            cmsb = cx.at_sb[par]
            kcmb = ("at_sb", par)
            csrc = cmbt.rearrange("(st p) c -> p st c", p=128)[
                :, :, nt * 128:(nt + 1) * 128]
            P.dma("y", lambda e, d=cmsb, s=csrc: e.dma_start(
                d.rearrange("p (st c) -> p st c", st=ND)[:, :, :], s),
                reads=cmb_keys, writes=(kcmb,), tag=f"{pref}_fcl{dc}_{nt}",
                ch=f"atl{par}")
            for st_ in range(ND):
                P.t(lambda e, o=cx.ps_mm, cm=cmsb, w=cx.wo_sb, st_=st_,
                    s0=(st_ == 0), sp=(st_ == ND - 1): e.matmul(
                    o[:, 0:512], cm[:, st_ * 128:(st_ + 1) * 128],
                    w[:, st_ * 512:(st_ + 1) * 512], start=s0, stop=sp),
                    reads=(kcmb, ("wo_sb",)), writes=(("ps_mm",),),
                    tag=f"{pref}_fm{dc}_{nt}_{st_}")
            rsb = cx.res_sb[par]
            krs = ("res_sb", par)
            P.v(lambda e, o=rsb, i_=cx.ps_mm: e.tensor_copy(o[:, :], i_[:, 0:512]),
                reads=(("ps_mm",),), writes=(krs,), tag=f"{pref}_fe{dc}_{nt}")
            P.dma("p", lambda e, d=xres, s=rsb, nt=nt, dc=dc: e.dma_start(
                d[nt * 128:(nt + 1) * 128, dc * 512:(dc + 1) * 512], s[:, :],
                accum_op=ALU.add),
                reads=(krs, ("X", nt)), writes=(("X", nt),),
                tag=f"{pref}_fa{dc}_{nt}", ch=f"res{par}")



def fp4_quant_half(cx, sv_src, skey, dv_dst, dkey, pref, tag, eng="v",
                   bufs=None):
    P = cx.P
    K = 12582912.0
    sv = sv_src.rearrange("p (b j) -> p b j", b=64)
    if bufs is None:
        if eng == "v":
            qs, kq, c0, c1 = cx.qstat, ("qstat",), 0, 64
            yb, ky = cx.at_sb[0].bitcast(F32), ("at_sb", 0)
            pi, kp = cx.ps_sc[:, 0:1024], ("ps_sc",)
            tb, kt_ = cx.pbuf[0].bitcast(F32), ("pbuf", 0)
        else:
            qs, kq, c0, c1 = cx.colbuf, ("colbuf",), 0, 0
            yb, ky = cx.at_sb[1].bitcast(F32), ("at_sb", 1)
            pi, kp = cx.pbuf[1].bitcast(F32), ("pbuf", 1)
            tb, kt_ = cx.ptb[0].bitcast(F32), ("ptb", 0)
    else:
        (qs, kq, c0), (yb, ky), (pi, kp), (tb, kt_) = bufs
        c1 = c0 + 64
    if eng == "p":
        rcs, krc, rc0 = cx.mvb, ("mvb",), 0
    else:
        rcs, krc, rc0 = qs, kq, c1
    em = P.v if eng == "v" else P.p
    P.v(lambda e, o=qs, i_=sv: e.reduce_max(
        o[:, c0:c0 + 64], i_[:, :, :], axis=AX.X, apply_absolute_value=True),
        reads=(skey,), writes=(kq,), tag=f"{pref}_{tag}am")
    P.v(lambda e, o=rcs, i_=qs: e.tensor_scalar(
        o[:, rc0:rc0 + 64], i_[:, c0:c0 + 64], 1e-30, None, ALU.max),
        reads=(kq,), writes=(krc,), tag=f"{pref}_{tag}rm")
    P.v(lambda e, o=rcs: e.reciprocal(o[:, rc0:rc0 + 64], o[:, rc0:rc0 + 64]),
        reads=(krc,), writes=(krc,), tag=f"{pref}_{tag}rc")
    yv = yb.rearrange("p (b j) -> p b j", b=64)
    rb = rcs[:, rc0:rc0 + 64].unsqueeze(2).broadcast_to([128, 64, 16])
    P.v(lambda e, o=yv, i_=sv, r=rb: e.scalar_tensor_tensor(
        o[:, :, :], i_[:, :, :], 6.0, r[:, :, :], ALU.mult, ALU.mult),
        reads=(skey, krc), writes=(ky,), tag=f"{pref}_{tag}y")
    em(lambda e, o=pi.bitcast(I32), i_=yb.bitcast(I32): e.tensor_scalar(
        o[:, :], i_[:, :], 0x7F800000, None, ALU.bitwise_and),
        reads=(ky,), writes=(kp,), tag=f"{pref}_{tag}p")
    em(lambda e, o=pi: e.tensor_scalar(
        o[:, :], o[:, :], 0.5, 0.5, ALU.mult, ALU.max),
        reads=(kp,), writes=(kp,), tag=f"{pref}_{tag}h")
    em(lambda e, o=tb, i_=pi, y=yb: e.scalar_tensor_tensor(
        o[:, :], i_[:, :], K, y[:, :], ALU.mult, ALU.add),
        reads=(kp, ky), writes=(kt_,), tag=f"{pref}_{tag}t")
    em(lambda e, o=yb, i_=pi, t=tb: e.scalar_tensor_tensor(
        o[:, :], i_[:, :], -K, t[:, :], ALU.mult, ALU.add),
        reads=(kp, kt_), writes=(ky,), tag=f"{pref}_{tag}q")
    ab = qs[:, c0:c0 + 64].unsqueeze(2).broadcast_to([128, 64, 16])
    dv = dv_dst.rearrange("p (b j) -> p b j", b=64)
    P.v(lambda e, o=dv, i_=yv, a=ab: e.scalar_tensor_tensor(
        o[:, :, :], i_[:, :, :], 1.0 / 6.0, a[:, :, :], ALU.mult, ALU.mult),
        reads=(ky, kq, krc), writes=(dkey,), tag=f"{pref}_{tag}f")


def emit_l1_attn_b(cx, xres):
    P = cx.P
    pref = "b1"
    wq = cx.din("L1_attn_b_Wq", (D, D), F32)
    wk = cx.din("L1_attn_b_Wk", (D, D), F32)
    wv = cx.din("L1_attn_b_Wv", (D, D), F32)
    wo = cx.din("L1_attn_b_Wo", (D, D), F32)
    cosB = cx.din("c_l1b_cos", (128, NT * 64), F32)
    sinB = cx.din("c_l1b_sin", (128, NT * 64), F32)
    att = cx.dram["ATT"]

    emit_ln_phase(cx, xres, pref, quant=True)

    w4t = cx.dram["W4T"]
    for wi, wsrc in enumerate((wq, wk, wv, wo)):
        for r in range(16):
            par = r % 2
            stg = cx.lnbuf[par]
            skl = ("lnbuf", par)
            P.dma("y", lambda e, d=stg, s=wsrc, r=r: e.dma_start(
                d[:, :], s[r * 128:(r + 1) * 128, :]),
                reads=(), writes=(skl,), tag=f"{pref}_pw{wi}_{r}",
                ch=f"qwl{par}")
            qt_ = cx.xnat[par]
            kqt = ("xnat", par)
            for hf in range(2):
                fp4_quant_half(cx, stg[:, hf * 1024:(hf + 1) * 1024], skl,
                               qt_[:, hf * 1024:(hf + 1) * 1024], kqt,
                               pref, f"pq{wi}_{r}_{hf}")
            for k in range(ND):
                P.t(lambda e, o=cx.ps_tr, i_=qt_, k=k: e.transpose(
                    o[:, k * 128:(k + 1) * 128], i_[:, k * 128:(k + 1) * 128],
                    cx.ident[:, :]),
                    reads=(kqt, ("ident",)), writes=(("ps_tr",),),
                    tag=f"{pref}_pt{wi}_{r}_{k}")
            stg2 = cx.wq_sb[par]
            k2 = ("wq_sb", par)
            P.v(lambda e, o=stg2, i_=cx.ps_tr: e.tensor_copy(o[:, :], i_[:, :]),
                reads=(("ps_tr",),), writes=(k2,), tag=f"{pref}_pe{wi}_{r}")
            sv = stg2.rearrange("p (k j) -> p k j", k=ND)
            dst = w4t[wi].rearrange("(dt p) c -> p dt c", p=128)[
                :, :, r * 128:(r + 1) * 128]
            P.dma("s", lambda e, d=dst, s=sv: e.dma_start(d, s[:, :, :]),
                  reads=(k2,), writes=(("W4T", wi),), tag=f"{pref}_ps{wi}_{r}",
                  ch=f"qws{par}")

    cosN = cx.xnat[0].bitcast(F32)
    sinN = cx.xnat[1].bitcast(F32)
    P.dma("y", lambda e, d=cosN, s=cosB: e.dma_start(d[:, :], s[:, :]),
          reads=(), writes=(("xnat", 0),), tag=f"{pref}_cosN", ch="lnbuf0")
    P.dma("y", lambda e, d=sinN, s=sinB: e.dma_start(d[:, :], s[:, :]),
          reads=(), writes=(("xnat", 1),), tag=f"{pref}_sinN", ch="lnbuf1")
    xlt = cx.xlnT.rearrange("p (dt s) -> p dt s", dt=ND)
    att_keys = []
    for hp in range(16):
        par = hp % 2
        for (wi, wbuf, wkey, tg) in ((0, cx.wq_sb[par], ("wq_sb", par), "q"),
                                     (1, cx.wk_sb[par], ("wk_sb", par), "k"),
                                     (2, cx.wv_sb[par], ("wv_sb", par), "v")):
            load_w_colslice(cx, "p", w4t[wi], hp * 128, 128, wbuf, wkey,
                            f"{pref}_w{tg}{hp}", ch=f"w{tg}{par}",
                            reads=(("W4T", wi),))
        for (wbuf, wkey, dst, dkey, tg, sc) in (
                (cx.wq_sb[par], ("wq_sb", par), cx.qt[par], ("qt", par), "q",
                 64.0 ** -0.5),
                (cx.wk_sb[par], ("wk_sb", par), cx.kt[par], ("kt", par), "k",
                 1.0)):
            for c in range(4):
                for dt in range(ND):
                    P.t(lambda e, o=cx.ps_mm, w=wbuf, x=xlt, dt=dt, c=c,
                        st=(dt == 0), sp=(dt == ND - 1): e.matmul(
                        o[:, 0:512], w[:, dt * 128:(dt + 1) * 128],
                        x[:, dt, c * 512:(c + 1) * 512], start=st, stop=sp),
                        reads=(wkey,) + XLN_KEYS, writes=(("ps_mm",),),
                        tag=f"{pref}_{tg}p{hp}_{c}_{dt}")
                if sc == 1.0:
                    P.s(lambda e, o=dst, i_=cx.ps_mm, c=c: e.copy(
                        out=o[:, c * 512:(c + 1) * 512], in_=i_[:, 0:512]),
                        reads=(("ps_mm",),), writes=(dkey,),
                        tag=f"{pref}_{tg}pe{hp}_{c}")
                else:
                    P.s(lambda e, o=dst, i_=cx.ps_mm, c=c, sc=sc: e.mul(
                        o[:, c * 512:(c + 1) * 512], i_[:, 0:512], float(sc)),
                        reads=(("ps_mm",),), writes=(dkey,),
                        tag=f"{pref}_{tg}pe{hp}_{c}")
        for c in range(4):
            for dt in range(ND):
                P.t(lambda e, o=cx.ps_mm, w=cx.wv_sb[par], x=xlt, dt=dt, c=c,
                    st=(dt == 0), sp=(dt == ND - 1): e.matmul(
                    o[:, 0:512], w[:, dt * 128:(dt + 1) * 128],
                    x[:, dt, c * 512:(c + 1) * 512], start=st, stop=sp),
                    reads=(("wv_sb", par),) + XLN_KEYS, writes=(("ps_mm",),),
                    tag=f"{pref}_vp{hp}_{c}_{dt}")
            P.s(lambda e, o=cx.vtt[par], i_=cx.ps_mm: e.copy(
                out=o, in_=i_[:, 0:512]),
                reads=(("ps_mm",),), writes=(("res_sb", par),),
                tag=f"{pref}_vpe{hp}_{c}")
            for j in range(4):
                P.t(lambda e, o=cx.ps_tr, i_=cx.vtt[par], j=j: e.transpose(
                    o[:, j * 128:(j + 1) * 128], i_[:, j * 128:(j + 1) * 128],
                    cx.ident[:, :]),
                    reads=(("res_sb", par), ("ident",)), writes=(("ps_tr",),),
                    tag=f"{pref}_vt{hp}_{c}_{j}")
            P.v(lambda e, o=cx.vt[par], i_=cx.ps_tr, c=c: e.tensor_copy(
                o[:, c * 512:(c + 1) * 512], i_[:, 0:512]),
                reads=(("ps_tr",),), writes=(("vt", par),),
                tag=f"{pref}_vte{hp}_{c}")
        for (tbuf, tkey) in ((cx.qt[par], ("qt", par)), (cx.kt[par], ("kt", par))):
            trans16(cx, tbuf, tkey, cx.at_sb[1], ("at_sb", 1), pref,
                    f"rn{hp}_{tkey[0]}")
            nb = cx.at_sb[1].rearrange("p (t g f) -> p t g f", t=NT, g=2)
            A = nb[:, :, :, 0:32]
            B = nb[:, :, :, 32:64]
            cv = cosN.rearrange("p (t j) -> p t j", t=NT).unsqueeze(2)
            sv = sinN.rearrange("p (t j) -> p t j", t=NT).unsqueeze(2)
            cA = cv[:, :, :, 0:32].broadcast_to([128, NT, 2, 32])
            cB = cv[:, :, :, 32:64].broadcast_to([128, NT, 2, 32])
            sA = sv[:, :, :, 0:32].broadcast_to([128, NT, 2, 32])
            sB = sv[:, :, :, 32:64].broadcast_to([128, NT, 2, 32])
            t1 = cx.lnbuf[1][:, 0:1024].rearrange("p (t g j) -> p t g j",
                                                  t=NT, g=2)
            t2 = cx.lnbuf[1][:, 1024:2048].rearrange("p (t g j) -> p t g j",
                                                     t=NT, g=2)
            kn = ("at_sb", 1)
            tk2 = (("xnat", 0), ("xnat", 1))
            ktm = ("lnbuf", 1)
            P.v(lambda e: e.tensor_tensor(t1[:, :, :, :], A[:, :, :, :],
                                          sB[:, :, :, :], ALU.mult),
                reads=(kn,) + tk2, writes=(ktm,), tag=f"{pref}_r1{hp}_{tkey[0]}")
            P.v(lambda e: e.tensor_tensor(A[:, :, :, :], A[:, :, :, :],
                                          cA[:, :, :, :], ALU.mult),
                reads=(kn,) + tk2, writes=(kn,), tag=f"{pref}_r2{hp}_{tkey[0]}")
            P.v(lambda e: e.tensor_tensor(t2[:, :, :, :], B[:, :, :, :],
                                          sA[:, :, :, :], ALU.mult),
                reads=(kn,) + tk2, writes=(ktm,), tag=f"{pref}_r3{hp}_{tkey[0]}")
            P.v(lambda e: e.tensor_tensor(A[:, :, :, :], A[:, :, :, :],
                                          t2[:, :, :, :], ALU.subtract),
                reads=(kn, ktm), writes=(kn,), tag=f"{pref}_r4{hp}_{tkey[0]}")
            P.v(lambda e: e.tensor_tensor(B[:, :, :, :], B[:, :, :, :],
                                          cB[:, :, :, :], ALU.mult),
                reads=(kn,) + tk2, writes=(kn,), tag=f"{pref}_r5{hp}_{tkey[0]}")
            P.v(lambda e: e.tensor_tensor(B[:, :, :, :], B[:, :, :, :],
                                          t1[:, :, :, :], ALU.add),
                reads=(kn, ktm), writes=(kn,), tag=f"{pref}_r6{hp}_{tkey[0]}")
            trans16(cx, cx.at_sb[1], ("at_sb", 1), tbuf, tkey, pref,
                    f"rb{hp}_{tkey[0]}")
        for g in range(2):
            attn_core(cx, pref, hp * 2 + g, par, causal=True,
                      mask_sb=cx.maskinf, dh=64,
                      qoff=64 * g, obuf=(cx.lnbuf[par], ("lnbuf", par)))
        ol = cx.lnbuf[par]
        kol = ("lnbuf", par)
        for k in range(ND):
            P.t(lambda e, o=cx.ps_sc, i_=ol, k=k: e.transpose(
                o[:, k * 128:(k + 1) * 128], i_[:, k * 128:(k + 1) * 128],
                cx.identf[:, :]),
                reads=(kol, ("identf",)), writes=(("ps_sc",),),
                tag=f"{pref}_ont{hp}_{k}")
        P.v(lambda e, o=ol, i_=cx.ps_sc: e.tensor_copy(o[:, :], i_[:, :]),
            reads=(("ps_sc",),), writes=(kol,), tag=f"{pref}_one{hp}")
        for hf in range(2):
            fp4_quant_half(cx, ol[:, hf * 1024:(hf + 1) * 1024], kol,
                           cx.ot[par][:, hf * 1024:(hf + 1) * 1024],
                           ("ot", par), pref, f"oq{hp}_{hf}")
        for k in range(ND):
            P.t(lambda e, o=cx.ps_tr, i_=cx.ot[par], k=k: e.transpose(
                o[:, k * 128:(k + 1) * 128], i_[:, k * 128:(k + 1) * 128],
                cx.ident[:, :]),
                reads=(("ot", par), ("ident",)), writes=(("ps_tr",),),
                tag=f"{pref}_oqt{hp}_{k}")
        P.v(lambda e, o=cx.wq_sb[par], i_=cx.ps_tr: e.tensor_copy(
            o[:, :], i_[:, :]),
            reads=(("ps_tr",),), writes=(("wq_sb", par),), tag=f"{pref}_oqe{hp}")
        P.dma("s", lambda e, d=att, s=cx.wq_sb[par], hp=hp: e.dma_start(
            d[hp * 128:(hp + 1) * 128, :], s[:, :]),
            reads=(("wq_sb", par),), writes=(("ATT", hp),),
            tag=f"{pref}_oqs{hp}", ch=f"att{par}")
        att_keys.append(("ATT", hp))
    for fc in range(4):
        load_w_colslice(cx, "p", w4t[3], fc * 512, 512, cx.wo_sb, ("wo_sb",),
                        f"{pref}_wo{fc}", ch="wo", reads=(("W4T", 3),))
        for st in range(NT):
            par2 = st % 2
            atb = cx.at_sb[0]
            katb = ("at_sb", 0)
            asrc = att.rearrange("(dt p) s -> p dt s", p=128)[
                :, :, st * 128:(st + 1) * 128]
            P.dma("y", lambda e, d=atb, s=asrc: e.dma_start(
                d.rearrange("p (dt c) -> p dt c", dt=ND)[:, :, :], s),
                reads=tuple(att_keys), writes=(katb,),
                tag=f"{pref}_atl{fc}_{st}", ch="atl0")
            for dt in range(ND):
                P.t(lambda e, o=cx.ps_mm, x=atb, w=cx.wo_sb, dt=dt, st=st,
                    s0=(dt == 0), sp=(dt == ND - 1): e.matmul(
                    o[:, 0:512], x[:, dt * 128:(dt + 1) * 128],
                    w[:, dt * 512:(dt + 1) * 512], start=s0, stop=sp),
                    reads=(katb, ("wo_sb",)), writes=(("ps_mm",),),
                    tag=f"{pref}_om{fc}_{st}_{dt}")
            rsb = cx.res_sb[par2]
            krs = ("res_sb", par2)
            P.v(lambda e, o=rsb, i_=cx.ps_mm: e.tensor_copy(o[:, :], i_[:, 0:512]),
                reads=(("ps_mm",),), writes=(krs,), tag=f"{pref}_oe{fc}_{st}")
            P.dma("p", lambda e, d=xres, s=rsb, st=st, fc=fc: e.dma_start(
                d[st * 128:(st + 1) * 128, fc * 512:(fc + 1) * 512], s[:, :],
                accum_op=ALU.add),
                reads=(krs, ("X", st)), writes=(("X", st),),
                tag=f"{pref}_oacc{fc}_{st}", ch=f"res{par2}")


# version 10
# speedup vs baseline: 1.0784x; 1.0784x over previous
import numpy as np
import ml_dtypes
import concourse.bass as bass
import concourse.mybir as mybir
from concourse.bass_utils import run_bass_kernel_spmd
from concourse.alu_op_type import AluOpType as ALU

F32 = mybir.dt.float32
BF16 = mybir.dt.bfloat16
I32 = mybir.dt.int32
AF = mybir.ActivationFunctionType
AX = mybir.AxisListType


STREAMS = ("t", "v", "s", "p", "y")


class _Op:
    __slots__ = ("idx", "stream", "emit", "reads", "writes", "dma", "tag",
                 "deps", "need_inc", "cmp_seq", "dma_seq", "snapshot",
                 "ch", "batch")

    def __init__(self, idx, stream, emit, reads, writes, dma, tag):
        self.idx = idx
        self.stream = stream
        self.emit = emit
        self.reads = tuple(reads)
        self.writes = tuple(writes)
        self.dma = dma
        self.tag = tag
        self.deps = None
        self.need_inc = False
        self.cmp_seq = None
        self.dma_seq = None
        self.snapshot = None
        self.ch = None
        self.batch = False


class Prog:
    def __init__(self):
        self.ops = []
        self.last_write = {}
        self.readers = {}

    def op(self, stream, emit, reads=(), writes=(), dma=False, tag="",
           ch=None, batch=False):
        i = len(self.ops)
        o = _Op(i, stream, emit, reads, writes, dma, tag)
        o.ch = ch
        o.batch = batch
        deps = set()
        for k in o.reads:
            w = self.last_write.get(k)
            if w is not None:
                deps.add(w)
        for k in o.writes:
            w = self.last_write.get(k)
            if w is not None:
                deps.add(w)
            for r in self.readers.get(k, ()):
                deps.add(r)
        deps.discard(i)
        o.deps = deps
        self.ops.append(o)
        for k in o.reads:
            self.readers.setdefault(k, []).append(i)
        for k in o.writes:
            self.last_write[k] = i
            self.readers[k] = []
        return i

    def t(self, emit, reads=(), writes=(), tag=""):
        return self.op("t", emit, reads, writes, False, tag)

    def v(self, emit, reads=(), writes=(), tag=""):
        return self.op("v", emit, reads, writes, False, tag)

    def s(self, emit, reads=(), writes=(), tag=""):
        return self.op("s", emit, reads, writes, False, tag)

    def p(self, emit, reads=(), writes=(), tag=""):
        return self.op("p", emit, reads, writes, False, tag)

    def dma(self, stream, emit, reads=(), writes=(), tag="", ch=None,
            batch=False):
        assert ch is not None, f"dma {tag} needs a channel"
        return self.op(stream, emit, reads, writes, True, tag, ch=ch,
                       batch=batch)

    def build(self, nc, final_stream="y"):
        ops = self.ops

        def skip_edge(prod, cons):
            return (prod.stream == cons.stream == "t"
                    and not prod.dma and not cons.dma)

        for o in ops:
            for d in o.deps:
                p_ = ops[d]
                if not skip_edge(p_, o):
                    p_.need_inc = True

        cmp_count = {st: 0 for st in STREAMS}
        ch_count = {}
        ch_stream = {}
        ch_batch = {}
        for o in ops:
            if o.dma:
                ch_count[o.ch] = ch_count.get(o.ch, 0) + 1
                o.dma_seq = ch_count[o.ch]
                if o.ch in ch_stream:
                    assert ch_stream[o.ch] == o.stream, \
                        f"channel {o.ch} used from two queues"
                ch_stream[o.ch] = o.stream
                ch_batch[o.ch] = ch_batch.get(o.ch, False) or o.batch
            elif o.need_inc:
                cmp_count[o.stream] += 1
                o.cmp_seq = cmp_count[o.stream]

        def target(o):
            if o.dma:
                if ch_batch[o.ch]:
                    return ("ch", o.ch), 16 * ch_count[o.ch]
                return ("ch", o.ch), 16 * o.dma_seq
            if o.cmp_seq is not None:
                return (o.stream, "c"), o.cmp_seq
            return None, None

        watermark = {st: {} for st in STREAMS}
        waits = {o.idx: [] for o in ops}
        for o in ops:
            wm = watermark[o.stream]
            req = {}
            snaps = {}
            if o.dma and not ch_batch[o.ch] and o.dma_seq > 1:
                req[("ch", o.ch)] = 16 * (o.dma_seq - 1)
                snaps[("ch", o.ch)] = None
            for d in sorted(o.deps):
                p_ = ops[d]
                if skip_edge(p_, o):
                    continue
                key, cnt = target(p_)
                if key is None:
                    raise AssertionError(f"dep without inc: {p_.tag} -> {o.tag}")
                if req.get(key, 0) < cnt:
                    req[key] = cnt
                    snaps[key] = p_.snapshot
            for key, cnt in sorted(req.items()):
                if wm.get(key, 0) >= cnt:
                    continue
                waits[o.idx].append((key, cnt))
                wm[key] = cnt
                sn = snaps[key]
                if sn:
                    for k2, c2 in sn.items():
                        if wm.get(k2, 0) < c2:
                            wm[k2] = c2
            if o.need_inc or o.dma:
                key, cnt = target(o)
                sn = dict(wm)
                sn[key] = cnt
                o.snapshot = sn

        sems = {}
        ctxs = []
        for st in STREAMS:
            if cmp_count[st]:
                cm = nc.semaphore(f"sem_{st}c")
                sems[(st, "c")] = cm.__enter__()
                ctxs.append(cm)
        for ch in ch_count:
            cm = nc.semaphore(f"sem_ch_{ch}")
            sems[("ch", ch)] = cm.__enter__()
            ctxs.append(cm)

        stream_ops = {st: [o for o in ops if o.stream == st] for st in STREAMS}
        final_counts = []
        for st in STREAMS:
            if cmp_count[st]:
                final_counts.append(((st, "c"), cmp_count[st]))
        for ch, n in ch_count.items():
            final_counts.append((("ch", ch), 16 * n))

        block_cm = nc.Block()
        block = block_cm.__enter__()
        ctxs.append(block_cm)

        def emit_stream(eng, st):
            for o in stream_ops[st]:
                for key, cnt in waits[o.idx]:
                    eng.wait_ge(sems[key], cnt)
                inst = o.emit(eng)
                if o.dma:
                    inst.then_inc(sems[("ch", o.ch)], 16)
                elif o.need_inc:
                    if inst is None:
                        raise AssertionError(f"op {o.tag} returned no inst")
                    inst.then_inc(sems[(o.stream, "c")], 1)
            if st == final_stream:
                for key, cnt in final_counts:
                    eng.wait_ge(sems[key], cnt)

        if stream_ops["y"] or final_stream == "y":
            @block.sync
            def _(sy):
                emit_stream(sy, "y")
        if stream_ops["t"]:
            @block.tensor
            def _(t):
                emit_stream(t, "t")
        if stream_ops["v"]:
            @block.vector
            def _(v):
                emit_stream(v, "v")
        if stream_ops["s"]:
            @block.scalar
            def _(s):
                emit_stream(s, "s")
        if stream_ops["p"]:
            @block.gpsimd
            def _(g):
                emit_stream(g, "p")
        return ctxs


S = 2048
D = 2048
NT = S // 128
ND = D // 128
EPS = 1e-5


def _tiled_table(x):
    w = x.shape[1]
    return np.ascontiguousarray(
        x.reshape(NT, 128, w).transpose(1, 0, 2).reshape(128, NT * w)
    ).astype(np.float32)


def host_constants():
    c = {}
    c["c_ident"] = np.eye(128, dtype=np.float32)
    tril = np.tril(np.ones((128, 128), np.float32))
    c["c_mask9"] = ((1.0 - tril) * (-1e9)).astype(np.float32)
    c["c_maskinf"] = ((1.0 - tril) * (-1e30)).astype(np.float32)
    c["c_ones"] = np.ones((1, 512), np.float32)
    c["c_bias"] = np.tile(np.array([[0.0, EPS, 1e-38]], np.float32), (128, 1))
    c["c_onescol"] = np.ones((128, 4), np.float32)

    t = np.arange(S, dtype=np.float64)
    f = (8000.0 / 200.0) ** np.linspace(0.0, 1.0, 64) * 200.0 / 1000.0
    fr = 200.0 * f / 1000.0
    ang = np.outer(t, fr)
    c["c_l0a_cos"] = _tiled_table(np.cos(ang))
    c["c_l0a_sin"] = _tiled_table(np.sin(ang))
    freqs = 1.0 / 10000.0 ** (np.arange(32, dtype=np.float64) / 32.0)
    ang1 = np.outer(t, freqs)
    c["c_l1a_cos"] = _tiled_table(np.repeat(np.cos(ang1), 2, axis=-1))
    c["c_l1a_sin"] = _tiled_table(np.repeat(np.sin(ang1), 2, axis=-1))
    freqsP = np.repeat(np.outer(np.arange(S, dtype=np.float64), freqs), 2, axis=-1)
    c["c_l1a_cosP"] = np.ascontiguousarray(np.cos(0 * freqsP[:1]))
    c["c_l1a_cosP"] = np.ascontiguousarray(
        np.repeat(np.cos(ang1), 2, axis=-1).T).astype(np.float32)
    c["c_l1a_sinP"] = np.ascontiguousarray(
        np.repeat(np.sin(ang1), 2, axis=-1).T).astype(np.float32)
    inv = 1.0 / 10000.0 ** (np.arange(0, 64, 2, dtype=np.float64) / 64.0)
    angb = np.outer(t, inv)
    c["c_l1b_cos"] = _tiled_table(np.concatenate([np.cos(angb), np.cos(angb)], -1))
    c["c_l1b_sin"] = _tiled_table(np.concatenate([np.sin(angb), np.sin(angb)], -1))
    c["c_l1b_cosP"] = np.ascontiguousarray(
        np.concatenate([np.cos(angb), np.cos(angb)], -1).T).astype(np.float32)
    c["c_l1b_sinP"] = np.ascontiguousarray(
        np.concatenate([np.sin(angb), np.sin(angb)], -1).T).astype(np.float32)
    return c


class Ctx:
    def __init__(self, nc, prog):
        self.nc = nc
        self.P = prog
        self.ctxs = []
        self.dram = {}

    def _enter(self, cm):
        obj = cm.__enter__()
        self.ctxs.append(cm)
        return obj

    def din(self, name, shape, dtype):
        if name in self.dram:
            return self.dram[name]
        ap = self.nc.dram_tensor(name, list(shape), dtype, kind="ExternalInput").ap()
        self.dram[name] = ap
        return ap

    def dout(self, name, shape, dtype):
        ap = self.nc.dram_tensor(name, list(shape), dtype, kind="ExternalOutput").ap()
        self.dram[name] = ap
        return ap

    def dtmp(self, name, shape, dtype):
        ap = self.nc.dram_tensor(name, list(shape), dtype, kind="Internal").ap()
        self.dram[name] = ap
        return ap

    def sbuf(self, name, shape, dtype):
        return self._enter(self.nc.sbuf_tensor(name, list(shape), dtype)).ap()

    def psum(self, name, shape, dtype):
        return self._enter(self.nc.psum_tensor(name, list(shape), dtype)).ap()

    def close(self):
        for cm in reversed(self.ctxs):
            cm.__exit__(None, None, None)


def alloc_shared(cx):
    cx.xlnT = cx.sbuf("xlnT", [128, NT * S], BF16)
    cx.lnbuf = [cx.sbuf(f"lnbuf{i}", [128, D], F32) for i in range(2)]
    cx.xnat = [cx.sbuf(f"xnat{i}", [128, D], BF16) for i in range(2)]
    cx.stats = [cx.sbuf(f"stats{i}", [128, 32], F32) for i in range(2)]
    cx.wq_sb = [cx.sbuf(f"wq_sb{i}", [128, ND * 128], BF16) for i in range(2)]
    cx.wk_sb = [cx.sbuf(f"wk_sb{i}", [128, ND * 128], BF16) for i in range(2)]
    cx.wv_sb = [cx.sbuf(f"wv_sb{i}", [128, ND * 128], BF16) for i in range(2)]
    cx.qt = [cx.sbuf(f"qt{i}", [128, S], BF16) for i in range(2)]
    cx.kt = [cx.sbuf(f"kt{i}", [128, S], BF16) for i in range(2)]
    cx.vt = [cx.sbuf(f"vt{i}", [128, NT * 128], BF16) for i in range(2)]
    cx.ot = [cx.sbuf(f"ot{i}", [128, S], BF16) for i in range(2)]
    cx.pbuf = [cx.sbuf(f"pbuf{i}", [128, S], BF16) for i in range(2)]
    cx.ptb = [cx.sbuf("ptb0", [128, S], BF16)]
    cx.sm = [cx.sbuf(f"sm{i}", [128, 16], F32) for i in range(2)]
    cx.colstg = cx.sbuf("colstg", [128, 128], BF16)
    cx.colbf = cx.sbuf("colbf", [128, 64], BF16)
    cx.b1col = cx.sbuf("b1col", [128, 64], F32)
    cx.ident = cx.sbuf("ident", [128, 128], BF16)
    cx.maskinf = cx.sbuf("maskinf", [128, 128], F32)
    cx.ones_bf = cx.sbuf("ones_bf", [1, 512], BF16)
    cx.ones_f32 = cx.sbuf("ones_f32", [1, 128], F32)
    cx.cbias = cx.sbuf("cbias", [128, 3], F32)
    cx.wo_sb = cx.sbuf("wo_sb", [128, ND * 512], BF16)
    cx.at_sb = [cx.sbuf(f"at_sb{i}", [128, ND * 128], BF16) for i in range(2)]
    cx.res_sb = [cx.sbuf(f"res_sb{i}", [128, 512], F32) for i in range(2)]
    cx.vtt = [cx.res_sb[i].bitcast(BF16)[:, 0:512] for i in range(2)]
    cx.wdb = [cx.wo_sb, cx.sbuf("wdb1", [128, ND * 512], BF16)]
    cx.wut = cx.wq_sb
    cx.x2t_sb = cx.kt
    cx.dqi = [b.bitcast(I32) for b in cx.lnbuf]
    cx.dqo = cx.xnat
    cx.dqs = cx.stats
    cx.wtstage = cx.at_sb
    cx.aup = cx.sbuf("aup", [128, 16 * 16], BF16)
    cx.mnorm = cx.sbuf("mnorm", [128, 16], F32)
    cx.acol = cx.sbuf("acol", [128, 40], F32)
    cx.bqcol = cx.sbuf("bqcol", [128, 32], BF16)
    cx.mvb = cx.sbuf("mvb", [128, 64], F32)
    cx.wkcol = cx.sbuf("wkcol", [128, 16], F32)
    cx.onescol = cx.sbuf("onescol", [128, 4], BF16)
    cx.qstat = cx.sbuf("qstat", [128, 128], F32)
    cx.identf = cx.sbuf("identf", [128, 128], F32)
    cx.wvcol = cx.sbuf("wvcol", [128, 16], F32)
    cx.adn = cx.sbuf("adn", [128, 64 * 16], BF16)
    cx.colbuf = cx.sbuf("colbuf", [128, 64], F32)
    cx.bsl = [cx.vtt[0][0:16, 0:128], cx.vtt[1][0:16, 0:128]]
    cx.bsl_keys = [("res_sb", 0), ("res_sb", 1)]
    cx.bdn_sb = cx.vt[0][0:16, :]
    cx.tT = cx.ptb[0][0:16, :]
    cx.t2T = cx.pbuf[0][0:16, 0:512]
    cx.rowbuf = cx.res_sb[0][0:1, :]
    cx.rowbf = cx.ot[0][0:1, :]
    cx.ps_sc = cx.psum("ps_sc", [128, 2048], F32)
    cx.ps_tr = cx.psum("ps_tr", [128, 2048], BF16)
    cx.ps_o = cx.psum("ps_o", [128, 512], F32)
    cx.ps_mm = cx.psum("ps_mm", [128, 512], F32)
    cx.consts = host_constants()


def load_constants(cx):
    P = cx.P
    for name, dst, stream in (("c_ident", cx.ident, "p"),
                              ("c_maskinf", cx.maskinf, "y"),
                              ("c_ones", cx.ones_bf, "p"),
                              ("c_ones", cx.ones_f32, "y"),
                              ("c_bias", cx.cbias, "y"),
                              ("c_onescol", cx.onescol, "p"),
                              ("c_ident", cx.identf, "y")):
        src = cx.din(name, cx.consts[name].shape, F32)
        key = dst.tensor.name
        r, c = dst.shape
        P.dma(stream, lambda e, d=dst, s=src, r=r, c=c: e.dma_start(
            d[:, :], s[0:r, 0:c]),
            reads=(), writes=((key,),), tag=f"ld_{key}",
            ch=f"const_{stream}", batch=True)



def emit_ln_phase(cx, xres, pref, spill_xln=None, want_mnorm=None,
                  inner_ln=False, quant=False):
    P = cx.P
    for n in range(NT):
        par = n % 2
        lnb, xnat, st = cx.lnbuf[par], cx.xnat[par], cx.stats[par]
        kl, kx, ks = ("lnbuf", par), ("xnat", par), ("stats", par)
        P.dma("y", lambda e, d=lnb, s=xres, n=n: e.dma_start(
            d[:, :], s[n * 128:(n + 1) * 128, :]),
            reads=(("X", n),), writes=(kl,), tag=f"{pref}_lnld{n}",
            ch=f"lnbuf{par}")
        for i in range(4):
            P.v(lambda e, o=st, x=lnb, i=i: e.bn_stats(
                o[:, 6 * i:6 * i + 6], x[:, 512 * i:512 * (i + 1)]),
                reads=(kl,), writes=(ks,), tag=f"{pref}_bnst{n}_{i}")
        P.v(lambda e, o=st: e.bn_aggr(o[:, 24:26], o[:, 0:24]),
            reads=(ks,), writes=(ks,), tag=f"{pref}_bnag{n}")
        P.s(lambda e, o=st: e.activation(o[:, 26:27], o[:, 25:26], AF.Ln,
            bias=cx.cbias[:, 1:2]),
            reads=(ks, ("cbias",)), writes=(ks,), tag=f"{pref}_lnv{n}")
        P.s(lambda e, o=st: e.activation(o[:, 27:28], o[:, 26:27], AF.Exp, scale=-0.5, bias=cx.cbias[:, 0:1]),
            reads=(ks, ("cbias",)), writes=(ks,), tag=f"{pref}_rstd{n}")
        if want_mnorm is not None:
            P.s(lambda e, o=st: e.activation(o[:, 28:29], o[:, 25:26], AF.Ln,
                bias=cx.cbias[:, 2:3]),
                reads=(ks, ("cbias",)), writes=(ks,), tag=f"{pref}_lnv2{n}")
            P.s(lambda e, o=st: e.activation(o[:, 29:30], o[:, 28:29], AF.Exp, scale=0.5, bias=cx.cbias[:, 0:1]),
                reads=(ks, ("cbias",)), writes=(ks,), tag=f"{pref}_sqv{n}")
            P.v(lambda e, o=want_mnorm, st=st, n=n: e.tensor_scalar(
                o[:, n:n + 1], st[:, 29:30], st[:, 27:28], float(np.sqrt(D)),
                ALU.mult, ALU.mult),
                reads=(ks,), writes=(("mnorm",),), tag=f"{pref}_mn{n}")
        if inner_ln:
            P.v(lambda e, o=st: e.tensor_scalar(
                o[:, 30:31], o[:, 25:26], o[:, 27:28], None, ALU.mult),
                reads=(ks,), writes=(ks,), tag=f"{pref}_uf0{n}")
            P.v(lambda e, o=st: e.tensor_scalar(
                o[:, 31:32], o[:, 30:31], o[:, 27:28], None, ALU.mult),
                reads=(ks,), writes=(ks,), tag=f"{pref}_uf1{n}")
            P.s(lambda e, o=st: e.activation(o[:, 30:31], o[:, 31:32], AF.Ln,
                bias=cx.cbias[:, 1:2]),
                reads=(ks, ("cbias",)), writes=(ks,), tag=f"{pref}_uf2{n}")
            P.s(lambda e, o=st: e.activation(o[:, 31:32], o[:, 30:31], AF.Exp, scale=-0.5, bias=cx.cbias[:, 0:1]),
                reads=(ks, ("cbias",)), writes=(ks,), tag=f"{pref}_uf3{n}")
            P.v(lambda e, o=st: e.tensor_scalar(
                o[:, 29:30], o[:, 31:32], o[:, 27:28], None, ALU.mult),
                reads=(ks,), writes=(ks,), tag=f"{pref}_uf5{n}")
            rs = 29
        else:
            rs = 27
        P.v(lambda e, o=xnat, x=lnb, st=st, rs=rs: e.tensor_scalar(
            o[:, :], x[:, :], st[:, 24:25], st[:, rs:rs + 1], ALU.subtract, ALU.mult),
            reads=(kl, ks), writes=(kx,), tag=f"{pref}_xln{n}")
        if spill_xln is not None:
            P.dma("s", lambda e, d=spill_xln, s=xnat, n=n: e.dma_start(
                d[n * 128:(n + 1) * 128, :], s[:, :]),
                reads=(kx,), writes=(("XLN", n),), tag=f"{pref}_spill{n}",
                ch=f"spill{par}")
        if quant:
            P.v(lambda e, o=lnb, x=lnb, st=st, rs=rs: e.tensor_scalar(
                o[:, :], x[:, :], st[:, 24:25], st[:, rs:rs + 1],
                ALU.subtract, ALU.mult),
                reads=(kl, ks), writes=(kl,), tag=f"{pref}_xf32{n}")
            for hf in range(2):
                fp4_quant_half(cx, lnb[:, hf * 1024:(hf + 1) * 1024], kl,
                               xnat[:, hf * 1024:(hf + 1) * 1024], kx,
                               pref, f"xq{n}_{hf}")
        for k in range(ND):
            P.t(lambda e, o=cx.ps_tr, i_=xnat, k=k: e.transpose(
                o[:, 128 * k:128 * (k + 1)], i_[:, 128 * k:128 * (k + 1)],
                cx.ident[:, :]),
                reads=(kx, ("ident",)), writes=(("ps_tr",),), tag=f"{pref}_tr{n}_{k}")
        xlt = cx.xlnT.rearrange("p (dt s) -> p dt s", dt=ND)
        pst = cx.ps_tr.rearrange("p (dt s) -> p dt s", dt=ND)
        P.s(lambda e, o=xlt, i_=pst, n=n: e.copy(
            out=o[:, :, n * 128:(n + 1) * 128], in_=i_[:, :, :]),
            reads=(("ps_tr",),), writes=(("xlnT", n),), tag=f"{pref}_xlte{n}")


XLN_KEYS = tuple(("xlnT", n) for n in range(NT))
PS_SC_KEYS = tuple(("ps_sc", c) for c in range(4))



def load_w_colslice(cx, stream, w_dram, col0, ncol, dst, dstkey, tag, rows=D,
                    ch=None, batch=False, reads=()):
    nt = rows // 128
    src = w_dram.rearrange("(dt p) c -> p dt c", p=128)[:, :, col0:col0 + ncol]
    d = dst.rearrange("p (dt c) -> p dt c", dt=nt)
    cx.P.dma(stream, lambda e, d=d, s=src: e.dma_start(d[:, :, :], s),
             reads=reads, writes=(dstkey,), tag=tag, ch=ch or tag, batch=batch)


def attn_softmax_exp(cx, pref, h, qi, nk, par2, ncs, mid=None):
    P = cx.P
    nv = nk * 128
    sm = cx.sm[par2]
    ksm = ("sm", par2)
    kps = PS_SC_KEYS[:ncs]
    kpb = ("pbuf", par2)
    pb = cx.pbuf[par2]
    P.v(lambda e, o=sm, ncs=ncs: e.reduce_max(
        o[:, 0:1], o[:, 4:4 + ncs], axis=AX.X, negate=True),
        reads=(ksm,), writes=(ksm,), tag=f"{pref}_cmb{h}_{qi}")
    P.s(lambda e, o=pb, i_=cx.ps_sc, sm=sm, nv=nv: e.activation(
        o[:, 0:nv], i_[:, 0:nv], AF.Exp, bias=sm[:, 0:1],
        accum_out=sm[:, 2:3]),
        reads=kps + (ksm,), writes=(kpb, ksm), tag=f"{pref}_exp1_{h}_{qi}")
    if mid is not None:
        mid()
    P.v(lambda e, o=sm: e.reciprocal(o[:, 3:4], o[:, 2:3]),
        reads=(ksm,), writes=(ksm,), tag=f"{pref}_rcp{h}_{qi}")
    P.v(lambda e, o=pb, sm=sm, nv=nv: e.tensor_scalar(
        o[:, 0:nv], o[:, 0:nv], sm[:, 3:4], None, ALU.mult),
        reads=(kpb, ksm), writes=(kpb,), tag=f"{pref}_pnm{h}_{qi}")


def attn_core(cx, pref, h, par, causal, mask_sb, dh=128,
              bufs=None, oscale=None, qoff=0, obuf=None):
    P = cx.P
    if bufs is None:
        qt, kt, vt, ot = cx.qt[par], cx.kt[par], cx.vt[par], cx.ot[par]
        kq, kk, kv, ko = ("qt", par), ("kt", par), ("vt", par), ("ot", par)
    else:
        (qt, kq), (kt, kk), (vt, kv), (ot, ko) = bufs
    if obuf is not None:
        ot, ko = obuf

    def emit_ptr(qi):
        par2 = qi % 2
        nk = (qi + 1) if causal else NT
        kpb = ("pbuf", par2)
        for j in range(nk):
            P.t(lambda e, o=cx.ps_tr, i_=cx.pbuf[par2], j=j: e.transpose(
                o[:, j * 128:(j + 1) * 128], i_[:, j * 128:(j + 1) * 128],
                cx.ident[:, :]),
                reads=(kpb, ("ident",)), writes=(("ps_tr",),),
                tag=f"{pref}_ptr{h}_{qi}_{j}")

    def emit_pte_av(qi):
        nk = (qi + 1) if causal else NT
        nv = nk * 128
        kpt = ("ptb", 0)
        P.v(lambda e, o=cx.ptb[0], i_=cx.ps_tr, nv=nv: e.tensor_copy(
            o[:, 0:nv], i_[:, 0:nv]),
            reads=(("ps_tr",),), writes=(kpt,), tag=f"{pref}_pte{h}_{qi}")
        for j in range(nk):
            P.t(lambda e, o=cx.ps_o, v=vt, p_=cx.ptb[0], j=j,
                st=(j == 0), sp=(j == nk - 1): e.matmul(
                o[0:dh, 0:128], v[:, j * 128 + qoff:j * 128 + qoff + dh],
                p_[:, j * 128:(j + 1) * 128], start=st, stop=sp),
                reads=(kv, kpt), writes=(("ps_o",),),
                tag=f"{pref}_av{h}_{qi}_{j}")

    def emit_oev(qi):
        if oscale is None:
            P.s(lambda e, o=ot, i_=cx.ps_o, qi=qi: e.copy(
                out=o[qoff:qoff + dh, qi * 128:(qi + 1) * 128],
                in_=i_[0:dh, 0:128]),
                reads=(("ps_o",),), writes=(ko,), tag=f"{pref}_oev{h}_{qi}")
        else:
            osc, okey = oscale
            P.v(lambda e, o=ot, i_=cx.ps_o, qi=qi, osc=osc: e.tensor_scalar(
                o[0:dh, qi * 128:(qi + 1) * 128], i_[0:dh, 0:128], osc, None,
                ALU.mult),
                reads=(("ps_o",), okey), writes=(ko,), tag=f"{pref}_oev{h}_{qi}")

    for qi in range(NT):
        par2 = qi % 2
        nk = (qi + 1) if causal else NT
        nv = nk * 128
        sm = cx.sm[par2]
        ksm = ("sm", par2)
        ncs = (nv + 511) // 512
        for c in range(ncs):
            w = min(512, nv - c * 512)
            dg = causal and (qi // 4 == c)
            P.t(lambda e, o=cx.ps_sc, q=qt, k=kt, qi=qi, c=c, w=w, dg=dg:
                e.matmul(
                o[:, c * 512:c * 512 + w],
                q[qoff:qoff + dh, qi * 128:(qi + 1) * 128],
                k[qoff:qoff + dh, c * 512:c * 512 + w], start=True,
                stop=not dg),
                reads=(kq, kk), writes=(("ps_sc", c),),
                tag=f"{pref}_sc{h}_{qi}_{c}")
            if dg:
                P.t(lambda e, o=cx.ps_sc, qi=qi: e.matmul(
                    o[:, qi * 128:(qi + 1) * 128], cx.identf[:, :],
                    cx.maskinf[:, :], start=False, stop=True),
                    reads=(("identf",), ("maskinf",)), writes=(("ps_sc", c),),
                    tag=f"{pref}_msk{h}_{qi}")
            P.v(lambda e, o=sm, i_=cx.ps_sc, c=c, w=w: e.reduce_max(
                o[:, 4 + c:5 + c], i_[:, c * 512:c * 512 + w], axis=AX.X),
                reads=(("ps_sc", c), ksm), writes=(ksm,),
                tag=f"{pref}_mx{h}_{qi}_{c}")
        if qi > 0:
            emit_ptr(qi - 1)
        attn_softmax_exp(cx, pref, h, qi, nk, par2, ncs,
                         mid=(lambda: emit_pte_av(qi - 1)) if qi else None)
        if qi > 0:
            emit_oev(qi - 1)
    emit_ptr(NT - 1)
    emit_pte_av(NT - 1)
    emit_oev(NT - 1)


def emit_out_proj(cx, pref, att_dram, w_dram, xres, wT=False, bias_row=None,
                  att_keys=(), bias_key=None):
    P = cx.P
    for fc in range(4):
        if not wT:
            load_w_colslice(cx, "p", w_dram, fc * 512, 512, cx.wo_sb, ("wo_sb",),
                            f"{pref}_wo{fc}", ch="wo")
        elif wT == "pe":
            wsl = cx.wo_sb.rearrange("p (dt c) -> p dt c", dt=ND)
            for i in range(4):
                stg = cx.qt[i % 2]
                skey = ("qt", i % 2)
                P.dma("p", lambda e, d=stg, s=w_dram, fc=fc, i=i: e.dma_start(
                    d[:, :], s[fc * 512 + i * 128:fc * 512 + (i + 1) * 128, :]),
                    reads=(), writes=(skey,), tag=f"{pref}_wol{fc}_{i}",
                    ch=f"wol{i % 2}")
                for k in range(ND):
                    P.t(lambda e, o=cx.ps_tr, i_=stg, k=k: e.transpose(
                        o[:, (k % 8) * 128:(k % 8 + 1) * 128],
                        i_[:, 128 * k:128 * (k + 1)], cx.ident[:, :]),
                        reads=(skey, ("ident",)), writes=(("ps_tr",),),
                        tag=f"{pref}_wot{fc}_{i}_{k}")
                    P.v(lambda e, o=wsl, i_=cx.ps_tr, k=k, i=i: e.tensor_copy(
                        o[:, k, i * 128:(i + 1) * 128],
                        i_[:, (k % 8) * 128:(k % 8 + 1) * 128]),
                        reads=(("ps_tr",),), writes=(("wo_sb",),),
                        tag=f"{pref}_woe{fc}_{i}_{k}")
        else:
            wsl = cx.wo_sb.rearrange("p (dt c) -> p dt c", dt=ND)
            for dt in range(ND):
                P.dma("y", lambda e, o=wsl, s=w_dram, fc=fc, dt=dt: e.dma_start(
                    o[:, dt, :], s[fc * 512:(fc + 1) * 512, dt * 128:(dt + 1) * 128],
                    transpose=True),
                    reads=(), writes=(("wo_sb",),), tag=f"{pref}_woT{fc}_{dt}",
                    ch=f"woT{dt % 2}")
        for st in range(NT):
            par = st % 2
            atb = cx.at_sb[par]
            katb = ("at_sb", par)
            src = att_dram.rearrange("(dt p) s -> p dt s", p=128)[
                :, :, st * 128:(st + 1) * 128]
            P.dma("y", lambda e, d=atb, s=src: e.dma_start(
                d.rearrange("p (dt c) -> p dt c", dt=ND)[:, :, :], s),
                reads=att_keys, writes=(katb,), tag=f"{pref}_atl{fc}_{st}",
                ch=f"atl{par}")
            for dt in range(ND):
                P.t(lambda e, o=cx.ps_mm, a=atb, w=cx.wo_sb, dt=dt, st_=(dt == 0),
                    sp=(dt == ND - 1 and bias_row is None): e.matmul(
                    o[:, 0:512], a[:, dt * 128:(dt + 1) * 128],
                    w[:, dt * 512:(dt + 1) * 512], start=st_, stop=sp),
                    reads=(katb, ("wo_sb",)), writes=(("ps_mm",),),
                    tag=f"{pref}_om{fc}_{st}_{dt}")
            if bias_row is not None:
                P.t(lambda e, o=cx.ps_mm, b=bias_row, fc=fc: e.matmul(
                    o[:, 0:512], cx.ones_bf[0:1, 0:128],
                    b[0:1, fc * 512:(fc + 1) * 512], start=False, stop=True),
                    reads=((bias_key or (bias_row.tensor.name,)), ("ones_bf",)),
                    writes=(("ps_mm",),), tag=f"{pref}_ob{fc}_{st}")
            rsb = cx.res_sb[par]
            krs = ("res_sb", par)
            P.v(lambda e, o=rsb, i_=cx.ps_mm: e.tensor_copy(o[:, :], i_[:, 0:512]),
                reads=(("ps_mm",),), writes=(krs,), tag=f"{pref}_oe{fc}_{st}")
            P.dma("p", lambda e, d=xres, s=rsb, st=st, fc=fc: e.dma_start(
                d[st * 128:(st + 1) * 128, fc * 512:(fc + 1) * 512], s[:, :],
                accum_op=ALU.add),
                reads=(krs, ("X", st)), writes=(("X", st),),
                tag=f"{pref}_oacc{fc}_{st}", ch=f"res{par}")


def emit_qkv_head_qtlayout(cx, pref, h, par, wq_src, wk_src, wv_src,
                           q_scale=1.0):
    P = cx.P
    xlt = cx.xlnT.rearrange("p (dt s) -> p dt s", dt=ND)
    for (src, col0), wbuf, wkey, tag in (
            (wq_src, cx.wq_sb[par], ("wq_sb", par), "wq"),
            (wk_src, cx.wk_sb[par], ("wk_sb", par), "wk"),
            (wv_src, cx.wv_sb[par], ("wv_sb", par), "wv")):
        load_w_colslice(cx, "p", src, col0, 128, wbuf, wkey, f"{pref}_{tag}{h}",
                        ch=f"{tag}{par}")
    for (wbuf, wkey, dst, dkey, sc) in (
            (cx.wq_sb[par], ("wq_sb", par), cx.qt[par], ("qt", par), q_scale),
            (cx.wk_sb[par], ("wk_sb", par), cx.kt[par], ("kt", par), 1.0)):
        for c in range(4):
            for dt in range(ND):
                P.t(lambda e, o=cx.ps_mm, w=wbuf, x=xlt, dt=dt, c=c,
                    st=(dt == 0), sp=(dt == ND - 1): e.matmul(
                    o[:, 0:512], w[:, dt * 128:(dt + 1) * 128],
                    x[:, dt, c * 512:(c + 1) * 512], start=st, stop=sp),
                    reads=(wkey,) + XLN_KEYS, writes=(("ps_mm",),),
                    tag=f"{pref}_qk{h}_{c}_{dt}")
            if sc == 1.0:
                P.s(lambda e, o=dst, i_=cx.ps_mm, c=c: e.copy(
                    out=o[:, c * 512:(c + 1) * 512], in_=i_[:, 0:512]),
                    reads=(("ps_mm",),), writes=(dkey,), tag=f"{pref}_qke{h}_{c}")
            else:
                P.s(lambda e, o=dst, i_=cx.ps_mm, c=c, sc=sc: e.mul(
                    o[:, c * 512:(c + 1) * 512], i_[:, 0:512], float(sc)),
                    reads=(("ps_mm",),), writes=(dkey,), tag=f"{pref}_qke{h}_{c}")
    for c in range(4):
        for dt in range(ND):
            P.t(lambda e, o=cx.ps_mm, w=cx.wv_sb[par], x=xlt, dt=dt, c=c,
                st=(dt == 0), sp=(dt == ND - 1): e.matmul(
                o[:, 0:512], w[:, dt * 128:(dt + 1) * 128],
                x[:, dt, c * 512:(c + 1) * 512], start=st, stop=sp),
                reads=(("wv_sb", par),) + XLN_KEYS, writes=(("ps_mm",),),
                tag=f"{pref}_v{h}_{c}_{dt}")
        P.s(lambda e, o=cx.vtt[par], i_=cx.ps_mm: e.copy(out=o[:, :], in_=i_[:, 0:512]),
            reads=(("ps_mm",),), writes=(("res_sb", par),), tag=f"{pref}_ve{h}_{c}")
        for j in range(4):
            P.t(lambda e, o=cx.ps_tr, i_=cx.vtt[par], j=j: e.transpose(
                o[:, j * 128:(j + 1) * 128], i_[:, j * 128:(j + 1) * 128],
                cx.ident[:, :]),
                reads=(("res_sb", par), ("ident",)), writes=(("ps_tr",),),
                tag=f"{pref}_vtr{h}_{c}_{j}")
        P.v(lambda e, o=cx.vt[par], i_=cx.ps_tr, c=c: e.tensor_copy(
            o[:, c * 512:(c + 1) * 512], i_[:, 0:512]),
            reads=(("ps_tr",),), writes=(("vt", par),), tag=f"{pref}_vte{h}_{c}")



def emit_l0_attn_b(cx, xres):
    P = cx.P
    pref = "b0"
    wqkv = cx.din("L0_attn_b_Wqkv", (D, 3 * D), F32)
    wo = cx.din("L0_attn_b_Wo", (D, D), F32)
    att = cx.dram["ATT"]
    emit_ln_phase(cx, xres, pref)
    att_keys = []
    for h in range(16):
        par = h % 2
        emit_qkv_head_qtlayout(cx, pref, h, par,
                               (wqkv, h * 128), (wqkv, 2048 + h * 128),
                               (wqkv, 4096 + h * 128), q_scale=128.0 ** -0.5)
        attn_core(cx, pref, h, par, causal=True, mask_sb=cx.maskinf)
        P.dma("s", lambda e, d=att, s=cx.ot[par], h=h: e.dma_start(
            d[h * 128:(h + 1) * 128, :], s[:, :]),
            reads=(("ot", par),), writes=(("ATT", h),), tag=f"{pref}_ost{h}",
            ch=f"att{par}")
        att_keys.append(("ATT", h))
    emit_out_proj(cx, pref, att, wo, xres, att_keys=tuple(att_keys))



SUBLAYERS = ("l0_attn_a", "l0_ffn_1", "l0_attn_b", "l0_ffn_2",
             "l1_attn_a", "l1_ffn_1", "l1_attn_b", "l1_ffn_2")


def build_program(sublayers, inputs_shapes=None):
    nc = bass.Bass("TRN2", target_bir_lowering=False)
    P = Prog()
    cx = Ctx(nc, P)
    alloc_shared(cx)
    xin = cx.din("x", (S, D), F32)
    xres = cx.dout("xout", (S, D), F32)
    cx.dtmp("ATT", (D, S), BF16)
    cx.dtmp("W_UPT", (64, 128, D), BF16)
    cx.dtmp("W_DNT", (64, 128, D), BF16)
    cx.dtmp("X2T", (8192, D), BF16)
    cx.dtmp("LGN", (S, 2048), BF16)
    cx.dtmp("DSPN", (S, 2048), BF16)
    cx.dtmp("CMBT", (2048, S), BF16)
    cx.dtmp("SLIN", (D, 2048), BF16)
    cx.dtmp("SLOT_OUT", (2048, D), BF16)
    cx.dtmp("XLN", (S, D), BF16)
    cx.dtmp("W4T", (4, D, D), BF16)
    load_constants(cx)
    for n in range(NT):
        P.dma("y", lambda e, d=xres, s=xin, n=n: e.dma_start(
            d[n * 128:(n + 1) * 128, :], s[n * 128:(n + 1) * 128, :]),
            reads=(), writes=(("X", n),), tag=f"init{n}", ch="init", batch=True)
    for sl in sublayers:
        if sl == "l0_attn_b":
            emit_l0_attn_b(cx, xres)
        elif sl == "l0_attn_a":
            emit_l0_attn_a(cx, xres)
        elif sl == "l1_attn_a":
            emit_l1_attn_a(cx, xres)
        elif sl in ("l1_ffn_1", "l1_ffn_2"):
            emit_l1_ffn(cx, xres, int(sl[-1]))
        elif sl == "l1_attn_b":
            emit_l1_attn_b(cx, xres)
        elif sl in ("l0_ffn_1", "l0_ffn_2"):
            emit_l0_ffn(cx, xres, int(sl[-1]))
        else:
            raise NotImplementedError(sl)
    ctxs = P.build(nc)
    cx.close()
    for cm in reversed(ctxs):
        cm.__exit__(None, None, None)
    return nc, cx


SCRATCH = ("x", "xout", "ATT", "W_UPT", "W_DNT", "X2T", "XLN", "LGN",
           "DSPN", "CMBT", "SLIN", "SLOT_OUT", "W4T")


def stage_inputs(cx, inputs, x):
    consts = host_constants()
    in_maps = []
    staged = {}
    for n in cx.dram:
        if n in SCRATCH:
            continue
        if n.startswith("c_"):
            staged[n] = consts[n]
        else:
            v = np.asarray(inputs[n])
            if v.dtype == np.int32:
                staged[n] = np.ascontiguousarray(v)
            else:
                staged[n] = np.ascontiguousarray(v.astype(np.float32))
    for core in range(8):
        m = dict(staged)
        m["x"] = np.ascontiguousarray(x[core])
        in_maps.append(m)
    return in_maps


FULL_CHAIN = ["l0_attn_a", "l0_ffn_1", "l0_attn_b", "l0_ffn_2",
              "l1_attn_a", "l1_ffn_1", "l1_attn_b", "l1_ffn_2"]


def kernel(**inputs):
    x = np.asarray(inputs["L0_attn_a_x"], dtype=np.float32)
    nc, cx = build_program(FULL_CHAIN)
    in_maps = stage_inputs(cx, inputs, x)
    res = run_bass_kernel_spmd(nc, in_maps, core_ids=list(range(8)))
    return np.stack([r["xout"] for r in res.results], axis=0)



def load_row(cx, dram_vec, n, dst, dstkey, tag, ch, stream="y", cast=False):
    cx.P.dma(stream, lambda e, d=dst, s=dram_vec, n=n: e.dma_start(
        d[0:1, 0:n], s.rearrange("(a n) -> a n", a=1) if s.ndim == 1 else s),
        reads=(), writes=(dstkey,), tag=tag, ch=ch)


def load_cols(cx, src2d, nt, col_sb, col0, colkey, tag, f32=True):
    P = cx.P
    P.dma("p", lambda e, d=cx.colstg, s=src2d, nt=nt: e.dma_start(
        d[0:nt, 0:128], s),
        reads=(), writes=(("colstg",),), tag=f"{tag}_ld", ch="colld")
    if f32:
        P.dma("y", lambda e, d=cx.colbf, s=cx.colstg, nt=nt: e.dma_start(
            d[:, 0:nt], s[0:nt, 0:128], transpose=True),
            reads=(("colstg",),), writes=(("colbf",),), tag=f"{tag}_tr",
            ch="xbar")
        P.v(lambda e, o=col_sb, i_=cx.colbf, col0=col0, nt=nt: e.tensor_copy(
            o[:, col0:col0 + nt], i_[:, 0:nt]),
            reads=(("colbf",),), writes=(colkey,), tag=f"{tag}_cst")
    else:
        P.dma("y", lambda e, d=col_sb, s=cx.colstg, col0=col0, nt=nt:
              e.dma_start(d[:, col0:col0 + nt], s[0:nt, 0:128], transpose=True),
              reads=(("colstg",),), writes=(colkey,), tag=f"{tag}_tr",
              ch="xbar")


def row_to_cols(cx, row_sb, rowkey, n, col_sb, col0, colkey, tag, scale=None):
    P = cx.P
    nt = n // 128
    for i in range(nt):
        P.t(lambda e, o=cx.ps_o, r=row_sb, i=i: e.matmul(
            o[:, 0:1], r[0:1, i * 128:(i + 1) * 128], cx.ones_f32[0:1, 0:1],
            start=True, stop=True),
            reads=(rowkey, ("ones_f32",)), writes=(("ps_o",),), tag=f"{tag}_m{i}")
        if scale is None:
            P.v(lambda e, o=col_sb, i_=cx.ps_o, i=i: e.tensor_copy(
                o[:, col0 + i:col0 + i + 1], i_[:, 0:1]),
                reads=(("ps_o",),), writes=(colkey,), tag=f"{tag}_e{i}")
        else:
            P.v(lambda e, o=col_sb, i_=cx.ps_o, i=i, sc=scale: e.tensor_scalar(
                o[:, col0 + i:col0 + i + 1], i_[:, 0:1], float(sc), None, ALU.mult),
                reads=(("ps_o",),), writes=(colkey,), tag=f"{tag}_e{i}")



def emit_l0_ffn_dequant(cx):
    P = cx.P
    wupq = cx.din("L0_ffn_w_up_q", (8192, D), I32)
    wups = cx.din("L0_ffn_w_up_scale", (8192, 32), F32)
    wdnq = cx.din("L0_ffn_w_down_q", (D, 8192), I32)
    wdns = cx.din("L0_ffn_w_down_scale", (D, 128), F32)
    wupt = cx.dram["W_UPT"]
    wdnt = cx.dram["W_DNT"]
    chunks = []
    for ht in range(64):
        chunks.append(("up", ht, 0))
    for dt in range(16):
        for c in range(4):
            chunks.append(("dn", dt, c))
    for idx, (kind, a, b) in enumerate(chunks):
        par = idx % 2
        dqi, dqo, dqs = cx.dqi[par], cx.dqo[par], cx.dqs[par]
        ki, ko, ks2 = ("lnbuf", par), ("xnat", par), ("stats", par)
        if kind == "up":
            qsrc = wupq[a * 128:(a + 1) * 128, :]
            ssrc = wups[a * 128:(a + 1) * 128, :]
        else:
            qsrc = wdnq[a * 128:(a + 1) * 128, b * 2048:(b + 1) * 2048]
            ssrc = wdns[a * 128:(a + 1) * 128, b * 32:(b + 1) * 32]
        P.dma("p", lambda e, d=dqi, s=qsrc: e.dma_start(d[:, :], s),
              reads=(), writes=(ki,), tag=f"dq_q{idx}", ch=f"dqq{par}")
        P.dma("y", lambda e, d=dqs, s=ssrc: e.dma_start(d[:, 0:32], s),
              reads=(), writes=(ks2,), tag=f"dq_s{idx}", ch=f"dqs{par}")
        sc_b = dqs[:, 0:32].unsqueeze(2).broadcast_to([128, 32, 64])
        qv = dqi.rearrange("p (b j) -> p b j", b=32)
        ov = dqo.rearrange("p (b j) -> p b j", b=32)
        P.v(lambda e, o=ov, q=qv, s=sc_b: e.scalar_tensor_tensor(
            o[:, :, :], q[:, :, :], 7.5, s[:, :, :], ALU.subtract, ALU.mult),
            reads=(ki, ks2), writes=(ko,), tag=f"dq_c{idx}")
        for k in range(ND):
            P.t(lambda e, o=cx.ps_tr, i_=dqo, k=k: e.transpose(
                o[:, 128 * k:128 * (k + 1)], i_[:, 128 * k:128 * (k + 1)],
                cx.ident[:, :]),
                reads=(ko, ("ident",)), writes=(("ps_tr",),), tag=f"dq_t{idx}_{k}")
        stg = cx.wtstage[par]
        kst = ("at_sb", par)
        P.v(lambda e, o=stg, i_=cx.ps_tr: e.tensor_copy(o[:, :], i_[:, :]),
            reads=(("ps_tr",),), writes=(kst,), tag=f"dq_e{idx}")
        if kind == "up":
            dst = wupt[a, :, :]
            P.dma("s", lambda e, d=dst, s=stg: e.dma_start(d, s[:, :]),
                  reads=(kst,), writes=(("W_UPT",),), tag=f"dq_w{idx}",
                  ch=f"dqw{par}")
        else:
            sv = stg.rearrange("p (k j) -> p k j", k=ND)
            dst = wdnt.rearrange("ht p c -> p ht c")[:, b * 16:(b + 1) * 16,
                                                     a * 128:(a + 1) * 128]
            P.dma("s", lambda e, d=dst, s=sv: e.dma_start(d, s[:, :, :]),
                  reads=(kst,), writes=(("W_DNT",),), tag=f"dq_w{idx}",
                  ch=f"dqw{par}")


def emit_l0_ffn(cx, xres, app):
    P = cx.P
    pref = f"f0{app}"
    alora = cx.din("L0_ffn_w_up_lora_a", (D, 16), F32)
    blora = cx.din("L0_ffn_w_up_lora_b", (16, 8192), F32)
    adlora = cx.din("L0_ffn_w_down_lora_a", (8192, 16), F32)
    bdlora = cx.din("L0_ffn_w_down_lora_b", (16, D), F32)
    bup = cx.din("L0_ffn_b_up", (8192,), F32)
    bdown = cx.din("L0_ffn_b_down", (D,), F32)
    wupt = cx.dram["W_UPT"]
    wdnt = cx.dram["W_DNT"]
    x2t = cx.dram["X2T"]

    emit_ln_phase(cx, xres, pref)
    if not getattr(cx, "ffn0_ready", False):
        cx.ffn0_ready = True
        emit_l0_ffn_dequant(cx)
        load_w_colslice(cx, "p", alora, 0, 16, cx.aup, ("aup",), "ld_aup",
                        ch="smw", batch=True)
        load_w_colslice(cx, "p", adlora, 0, 16, cx.adn, ("adn",), "ld_adn",
                        rows=8192, ch="smw", batch=True)
        load_cols(cx, bup.rearrange("(a f) -> a f", a=64), 64, cx.colbuf,
                  0, ("colbuf",), "bupc")
    P.dma("p", lambda e, d=cx.rowbf, s=bdown: e.dma_start(
        d, s.rearrange("(a n) -> a n", a=1)),
        reads=(), writes=(("ot", 0),), tag=f"{pref}_ld_bdown", ch="smw3")
    P.dma("p", lambda e, d=cx.bdn_sb, s=bdlora: e.dma_start(d, s[:, :]),
          reads=(), writes=(("vt", 0),), tag=f"{pref}_ld_bdn", ch="smw3")

    xlt = cx.xlnT.rearrange("p (dt s) -> p dt s", dt=ND)
    for cs in range(4):
        for dt in range(ND):
            P.t(lambda e, o=cx.ps_o, a=cx.aup, x=xlt, dt=dt, cs=cs,
                st=(dt == 0), sp=(dt == ND - 1): e.matmul(
                o[0:16, 0:512], a[:, dt * 16:(dt + 1) * 16],
                x[:, dt, cs * 512:(cs + 1) * 512], start=st, stop=sp),
                reads=(("aup",),) + XLN_KEYS, writes=(("ps_o",),),
                tag=f"{pref}_lt{cs}_{dt}")
        P.s(lambda e, o=cx.tT, i_=cx.ps_o, cs=cs: e.copy(
            out=o[:, cs * 512:(cs + 1) * 512], in_=i_[0:16, 0:512]),
            reads=(("ps_o",),), writes=(("ptb", 0),), tag=f"{pref}_lte{cs}")

    for ht in range(64):
        par = ht % 2
        wut, kw = cx.wut[par], ("wq_sb", par)
        P.dma("p", lambda e, d=wut, s=wupt, ht=ht: e.dma_start(d[:, :], s[ht, :, :]),
              reads=(("W_UPT",),), writes=(kw,), tag=f"{pref}_wu{ht}",
              ch=f"wu{par}")
        bsl = cx.bsl[par]
        P.dma("p", lambda e, d=bsl, s=blora, ht=ht: e.dma_start(
            d[:, :], s[:, ht * 128:(ht + 1) * 128]),
            reads=(), writes=(cx.bsl_keys[par],), tag=f"{pref}_bu{ht}", ch=f"bu{par}")
        x2s = cx.x2t_sb[par]
        kx2 = ("kt", par)
        for cs in range(4):
            for dt in range(ND):
                P.t(lambda e, o=cx.ps_mm, w=wut, x=xlt, dt=dt, cs=cs,
                    st=(dt == 0): e.matmul(
                    o[:, 0:512], w[:, dt * 128:(dt + 1) * 128],
                    x[:, dt, cs * 512:(cs + 1) * 512], start=st, stop=False),
                    reads=(kw,) + XLN_KEYS, writes=(("ps_mm",),),
                    tag=f"{pref}_up{ht}_{cs}_{dt}")
            P.t(lambda e, o=cx.ps_mm, b=bsl, t_=cx.tT, cs=cs: e.matmul(
                o[:, 0:512], b, t_[:, cs * 512:(cs + 1) * 512],
                start=False, stop=True),
                reads=(cx.bsl_keys[par], ("ptb", 0)), writes=(("ps_mm",),),
                tag=f"{pref}_upl{ht}_{cs}")
            P.s(lambda e, o=x2s, i_=cx.ps_mm, cs=cs, ht=ht: e.activation(
                o[:, cs * 512:(cs + 1) * 512], i_[:, 0:512], AF.Relu,
                bias=cx.colbuf[:, ht:ht + 1]),
                reads=(("ps_mm",), ("colbuf",)), writes=(kx2,),
                tag=f"{pref}_ue{ht}_{cs}")
        P.dma("s", lambda e, d=x2t, s=x2s, ht=ht: e.dma_start(
            d[ht * 128:(ht + 1) * 128, :], s[:, :]),
            reads=(kx2,), writes=(("X2T",),), tag=f"{pref}_x2s{ht}",
            ch=f"x2s{par}")

    for cs in range(4):
        pan = cx.xlnT.rearrange("p (ht c) -> p ht c", ht=64)
        src = x2t.rearrange("(ht p) c -> p ht c", p=128)[:, :, cs * 512:(cs + 1) * 512]
        P.dma("y", lambda e, d=pan, s=src: e.dma_start(d[:, :, :], s),
              reads=(("X2T",),), writes=XLN_KEYS, tag=f"{pref}_pan{cs}",
              ch="panel")
        for ht in range(64):
            P.t(lambda e, o=cx.ps_o, a=cx.adn, x=pan, ht=ht,
                st=(ht == 0), sp=(ht == 63): e.matmul(
                o[0:16, 0:512], a[:, ht * 16:(ht + 1) * 16], x[:, ht, :],
                start=st, stop=sp),
                reads=(("adn",),) + XLN_KEYS, writes=(("ps_o",),),
                tag=f"{pref}_l2{cs}_{ht}")
        P.s(lambda e, o=cx.t2T, i_=cx.ps_o: e.copy(
            out=o, in_=i_[0:16, 0:512]),
            reads=(("ps_o",),), writes=(("pbuf", 0),), tag=f"{pref}_l2e{cs}")
        for dc in range(4):
            for b in range(4):
                par = (dc * 4 + b) % 2
                wdb = cx.wdb[par]
                kwd = ("wo_sb",) if par == 0 else ("wdb1",)
                wsrc = wdnt.rearrange("ht p c -> p ht c")[
                    :, b * 16:(b + 1) * 16, dc * 512:(dc + 1) * 512]
                P.dma("p", lambda e, d=wdb, s=wsrc: e.dma_start(
                    d.rearrange("p (h c) -> p h c", h=ND)[:, :, :], s),
                    reads=(("W_DNT",),), writes=(kwd,),
                    tag=f"{pref}_wd{cs}_{dc}_{b}", ch=f"wd{par}")
                for sp_ in range(4):
                    for hb in range(16):
                        P.t(lambda e, o=cx.ps_sc, x=pan, w=wdb, sp_=sp_, hb=hb,
                            b=b, st=(b == 0 and hb == 0): e.matmul(
                            o[:, sp_ * 512:(sp_ + 1) * 512],
                            x[:, b * 16 + hb, sp_ * 128:(sp_ + 1) * 128],
                            w[:, hb * 512:(hb + 1) * 512], start=st, stop=False),
                            reads=XLN_KEYS + (kwd,), writes=PS_SC_KEYS,
                            tag=f"{pref}_dn{cs}_{dc}_{b}_{sp_}_{hb}")
            for sp_ in range(4):
                P.t(lambda e, o=cx.ps_sc, t_=cx.t2T, bd=bdlora, sp_=sp_, dc=dc: e.matmul(
                    o[:, sp_ * 512:(sp_ + 1) * 512], t_[:, sp_ * 128:(sp_ + 1) * 128],
                    cx.bdn_sb[:, dc * 512:(dc + 1) * 512], start=False, stop=False),
                    reads=(("pbuf", 0), ("vt", 0)), writes=PS_SC_KEYS,
                    tag=f"{pref}_dl{cs}_{dc}_{sp_}")
                P.t(lambda e, o=cx.ps_sc, sp_=sp_, dc=dc: e.matmul(
                    o[:, sp_ * 512:(sp_ + 1) * 512], cx.ones_bf[0:1, 0:128],
                    cx.rowbf[:, dc * 512:(dc + 1) * 512], start=False, stop=True),
                    reads=(("ones_bf",), ("ot", 0)), writes=PS_SC_KEYS,
                    tag=f"{pref}_db{cs}_{dc}_{sp_}")
                par2 = sp_ % 2
                rsb = cx.res_sb[par2]
                krs = ("res_sb", par2)
                stt = cs * 4 + sp_
                P.v(lambda e, o=rsb, i_=cx.ps_sc, sp_=sp_: e.tensor_copy(
                    o[:, :], i_[:, sp_ * 512:(sp_ + 1) * 512]),
                    reads=PS_SC_KEYS, writes=(krs,), tag=f"{pref}_de{cs}_{dc}_{sp_}")
                P.dma("p", lambda e, d=xres, s=rsb, stt=stt, dc=dc: e.dma_start(
                    d[stt * 128:(stt + 1) * 128, dc * 512:(dc + 1) * 512], s[:, :],
                    accum_op=ALU.add),
                    reads=(krs, ("X", stt)), writes=(("X", stt),),
                    tag=f"{pref}_da{cs}_{dc}_{sp_}", ch=f"res{par2}")



def _row_to_cols_bf(cx, row_sb, rowkey, n, col_sb, col0, colkey, tag):
    P = cx.P
    for i in range(n // 128):
        P.t(lambda e, o=cx.ps_o, r=row_sb, i=i: e.matmul(
            o[:, 0:1], r[0:1, i * 128:(i + 1) * 128], cx.ones_f32[0:1, 0:1],
            start=True, stop=True),
            reads=(rowkey, ("ones_f32",)), writes=(("ps_o",),), tag=f"{tag}_m{i}")
        P.v(lambda e, o=col_sb, i_=cx.ps_o, i=i: e.tensor_copy(
            o[:, col0 + i:col0 + i + 1], i_[:, 0:1]),
            reads=(("ps_o",),), writes=(colkey,), tag=f"{tag}_e{i}")


def emit_l0_attn_a(cx, xres):
    P = cx.P
    pref = "a0"
    wq = cx.din("L0_attn_a_Wq", (D, D), F32)
    wkv = cx.din("L0_attn_a_Wkv", (D, 2 * D), F32)
    wo = cx.din("L0_attn_a_Wo", (D, D), F32)
    lnqg = cx.din("L0_attn_a_ln_q_g", (D,), F32)
    lnqb = cx.din("L0_attn_a_ln_q_b", (D,), F32)
    lnkg = cx.din("L0_attn_a_ln_kv_g", (D,), F32)
    lnkb = cx.din("L0_attn_a_ln_kv_b", (D,), F32)
    lnhg = cx.din("L0_attn_a_ln_h_g", (128,), F32)
    lnhb = cx.din("L0_attn_a_ln_h_b", (128,), F32)
    bq_v = cx.din("L0_attn_a_bq", (D,), F32)
    bkv_v = cx.din("L0_attn_a_bkv", (2 * D,), F32)
    bo_v = cx.din("L0_attn_a_bo", (D,), F32)
    cosT = cx.din("c_l0a_cos", (128, NT * 64), F32)
    sinT = cx.din("c_l0a_sin", (128, NT * 64), F32)
    att = cx.dram["ATT"]

    emit_ln_phase(cx, xres, pref, want_mnorm=cx.mnorm, inner_ln=True)

    for vec, col0, tag in ((lnqg, 0, "gq"), (lnkg, 16, "gkv")):
        load_cols(cx, vec.rearrange("(a f) -> a f", a=16), 16, cx.acol,
                  col0, ("acol",), f"{pref}_{tag}c")
    for vec, col0, tag in ((lnhg, 32, "hg"), (lnhb, 33, "hb")):
        P.dma("y", lambda e, d=cx.rowbuf, s=vec: e.dma_start(
            d[0:1, 0:128], s.rearrange("(a n) -> a n", a=1)),
            reads=(), writes=(("res_sb", 0),), tag=f"{pref}_{tag}r", ch="rowld")
        row_to_cols(cx, cx.rowbuf, ("res_sb", 0), 128, cx.acol, col0,
                    ("acol",), f"{pref}_{tag}c")
    P.v(lambda e, o=cx.acol: e.tensor_scalar(
        o[:, 34:36], o[:, 32:34], 128.0 ** -0.25, None, ALU.mult),
        reads=(("acol",),), writes=(("acol",),), tag=f"{pref}_hsc")
    for vec, col0, tag in ((lnqb, 0, "bq"), (lnkb, 16, "bkv")):
        load_cols(cx, vec.rearrange("(a f) -> a f", a=16), 16, cx.bqcol,
                  col0, ("bqcol",), f"{pref}_{tag}c", f32=False)
    cs_sb = cx.xnat[0].bitcast(F32)
    sn_sb = cx.xnat[1].bitcast(F32)
    P.dma("y", lambda e, d=cs_sb, s=cosT: e.dma_start(d[:, :], s[:, :]),
          reads=(), writes=(("xnat", 0),), tag=f"{pref}_cosld", ch="lnbuf0")
    P.dma("y", lambda e, d=sn_sb, s=sinT: e.dma_start(d[:, :], s[:, :]),
          reads=(), writes=(("xnat", 1),), tag=f"{pref}_sinld", ch="lnbuf1")
    rr = cx.lnbuf[0][:, 0:1024].rearrange("p (t j) -> p t j", t=NT)
    ri = cx.lnbuf[0][:, 1024:2048].rearrange("p (t j) -> p t j", t=NT)
    mb = cx.mnorm.unsqueeze(2).broadcast_to([128, NT, 64])
    P.v(lambda e, o=rr, c=cs_sb.rearrange("p (t j) -> p t j", t=NT), m=mb:
        e.tensor_tensor(o[:, :, :], c[:, :, :], m[:, :, :], ALU.mult),
        reads=(("xnat", 0), ("mnorm",)), writes=(("lnbuf", 0),), tag=f"{pref}_rr")
    P.v(lambda e, o=ri, c=sn_sb.rearrange("p (t j) -> p t j", t=NT), m=mb:
        e.tensor_tensor(o[:, :, :], c[:, :, :], m[:, :, :], ALU.mult),
        reads=(("xnat", 1), ("mnorm",)), writes=(("lnbuf", 0),), tag=f"{pref}_ri")

    xlt = cx.xlnT.rearrange("p (dt s) -> p dt s", dt=ND)
    att_keys = []
    bqt = bq_v.rearrange("(a p b) -> a p b", a=16, b=1)
    bkvt = bkv_v.rearrange("(a p b) -> a p b", a=32, b=1)
    for h in range(16):
        par = h % 2
        specs = [("q", wq, h * 128, cx.wq_sb[par], ("wq_sb", par), 0, bqt, h, 0),
                 ("k", wkv, h * 128, cx.wk_sb[par], ("wk_sb", par), 16, bkvt,
                  h, 1),
                 ("v", wkv, 2048 + h * 128, cx.wv_sb[par], ("wv_sb", par), 16,
                  bkvt, 16 + h, 2)]
        for m_, wsrc, col0, wbuf, wkey, gcol, bt, bi, pc in specs:
            load_w_colslice(cx, "p", wsrc, col0, 128, wbuf, wkey,
                            f"{pref}_w{m_}{h}", ch=f"w{m_}{par}")
            for dt in range(ND):
                P.t(lambda e, o=cx.ps_o, b=cx.bqcol, w=wbuf, dt=dt, gcol=gcol,
                    pc=pc, st=(dt == 0), sp=(dt == ND - 1): e.matmul(
                    o[:, pc:pc + 1], w[:, dt * 128:(dt + 1) * 128],
                    b[:, gcol + dt:gcol + dt + 1], start=st, stop=sp),
                    reads=(("bqcol",), wkey), writes=(("ps_o",),),
                    tag=f"{pref}_bm{m_}{h}_{dt}")
            P.dma("p", lambda e, d=cx.acol, s=bt, bi=bi, pc=pc: e.dma_start(
                d[:, 36 + pc:37 + pc], s[bi]),
                reads=(), writes=(("acol",),), tag=f"{pref}_pb{m_}{h}",
                ch="arowld")
            for dt in range(ND):
                P.v(lambda e, w=wbuf, dt=dt, gcol=gcol: e.tensor_scalar(
                    w[:, dt * 128:(dt + 1) * 128], w[:, dt * 128:(dt + 1) * 128],
                    cx.acol[:, gcol + dt:gcol + dt + 1], None, ALU.mult),
                    reads=(wkey, ("acol",)), writes=(wkey,),
                    tag=f"{pref}_ws{m_}{h}_{dt}")
        P.v(lambda e, o=cx.acol, i_=cx.ps_o: e.tensor_tensor(
            o[:, 36:39], o[:, 36:39], i_[:, 0:3], ALU.add),
            reads=(("acol",), ("ps_o",)), writes=(("acol",),),
            tag=f"{pref}_bab{h}")
        qnat, knat = cx.ot[par], cx.at_sb[par]
        kqn, kkn = ("ot", par), ("at_sb", par)
        for (m_, wbuf, wkey, tdst, ktd, ndst, knd, pc) in (
                ("q", cx.wq_sb[par], ("wq_sb", par), cx.qt[par], ("qt", par),
                 qnat, kqn, 0),
                ("k", cx.wk_sb[par], ("wk_sb", par), cx.kt[par], ("kt", par),
                 knat, kkn, 1)):
            for c in range(4):
                for dt in range(ND):
                    P.t(lambda e, o=cx.ps_mm, w=wbuf, x=xlt, dt=dt, c=c,
                        st=(dt == 0), sp=(dt == ND - 1): e.matmul(
                        o[:, 0:512], w[:, dt * 128:(dt + 1) * 128],
                        x[:, dt, c * 512:(c + 1) * 512], start=st, stop=sp),
                        reads=(wkey,) + XLN_KEYS, writes=(("ps_mm",),),
                        tag=f"{pref}_{m_}n{h}_{c}_{dt}")
                P.v(lambda e, o=tdst, i_=cx.ps_mm, c=c, pc=pc: e.tensor_scalar(
                    o[:, c * 512:(c + 1) * 512], i_[:, 0:512],
                    cx.acol[:, 36 + pc:37 + pc], 128.0 ** -0.25,
                    ALU.add, ALU.mult),
                    reads=(("ps_mm",), ("acol",)), writes=(ktd,),
                    tag=f"{pref}_{m_}ne{h}_{c}")
            trans16(cx, tdst, ktd, ndst, knd, pref, f"{m_}nt{h}")
        t1 = cx.lnbuf[1][:, 0:1024].rearrange("p (t j) -> p t j", t=NT)
        t2 = cx.lnbuf[1][:, 1024:2048].rearrange("p (t j) -> p t j", t=NT)
        for dst, kdst in ((qnat, kqn), (knat, kkn)):
            dv = dst.rearrange("p (t f) -> p t f", t=NT)
            yr = dv[:, :, 0:128:2]
            yi = dv[:, :, 1:128:2]
            P.v(lambda e, o=t1, a=yr, b=ri: e.tensor_tensor(
                o[:, :, :], a[:, :, :], b[:, :, :], ALU.mult),
                reads=(kdst, ("lnbuf", 0)), writes=(("lnbuf", 1),),
                tag=f"{pref}_r1{h}_{kdst[0]}")
            P.v(lambda e, o=t2, a=yi, b=rr: e.tensor_tensor(
                o[:, :, :], a[:, :, :], b[:, :, :], ALU.mult),
                reads=(kdst, ("lnbuf", 0)), writes=(("lnbuf", 1),),
                tag=f"{pref}_r2{h}_{kdst[0]}")
            P.v(lambda e, o=yr, a=yr, b=rr: e.tensor_tensor(
                o[:, :, :], a[:, :, :], b[:, :, :], ALU.mult),
                reads=(kdst, ("lnbuf", 0)), writes=(kdst,),
                tag=f"{pref}_r3{h}_{kdst[0]}")
            P.v(lambda e, o=yi, a=yi, b=ri: e.tensor_tensor(
                o[:, :, :], a[:, :, :], b[:, :, :], ALU.mult),
                reads=(kdst, ("lnbuf", 0)), writes=(kdst,),
                tag=f"{pref}_r4{h}_{kdst[0]}")
            P.v(lambda e, o=yr, a=yr, b=yi: e.tensor_tensor(
                o[:, :, :], a[:, :, :], b[:, :, :], ALU.subtract),
                reads=(kdst,), writes=(kdst,), tag=f"{pref}_r5{h}_{kdst[0]}")
            P.v(lambda e, o=yi, a=t1, b=t2: e.tensor_tensor(
                o[:, :, :], a[:, :, :], b[:, :, :], ALU.add),
                reads=(kdst, ("lnbuf", 1)), writes=(kdst,),
                tag=f"{pref}_r6{h}_{kdst[0]}")
        for dst, kdst, tbuf, ktb in ((qnat, kqn, cx.qt[par], ("qt", par)),
                                     (knat, kkn, cx.kt[par], ("kt", par))):
            for st in range(NT):
                P.v(lambda e, o=cx.mvb, i_=dst, st=st: e.bn_stats(
                    o[:, 6 * (st % 8):6 * (st % 8) + 6],
                    i_[:, st * 128:(st + 1) * 128]),
                    reads=(kdst,), writes=(("mvb",),), tag=f"{pref}_hs{h}_{kdst[0]}_{st}")
                P.v(lambda e, o=cx.mvb, st=st: e.bn_aggr(
                    o[:, 48 + 2 * (st % 8):48 + 2 * (st % 8) + 2],
                    o[:, 6 * (st % 8):6 * (st % 8) + 6]),
                    reads=(("mvb",),), writes=(("mvb",),), tag=f"{pref}_ha{h}_{kdst[0]}_{st}")
                P.s(lambda e, o=cx.mvb, st=st: e.activation(
                    o[:, 32 + st % 8:32 + st % 8 + 1],
                    o[:, 48 + 2 * (st % 8) + 1:48 + 2 * (st % 8) + 2], AF.Ln,
                    bias=cx.cbias[:, 1:2]),
                    reads=(("mvb",), ("cbias",)), writes=(("mvb",),),
                    tag=f"{pref}_hl{h}_{kdst[0]}_{st}")
                P.s(lambda e, o=cx.mvb, st=st: e.activation(
                    o[:, 40 + st % 8:40 + st % 8 + 1], o[:, 32 + st % 8:32 + st % 8 + 1],
                    AF.Exp, scale=-0.5, bias=cx.cbias[:, 0:1]),
                    reads=(("mvb",), ("cbias",)), writes=(("mvb",),),
                    tag=f"{pref}_hr{h}_{kdst[0]}_{st}")
                P.v(lambda e, o=dst, st=st: e.tensor_scalar(
                    o[:, st * 128:(st + 1) * 128], o[:, st * 128:(st + 1) * 128],
                    cx.mvb[:, 48 + 2 * (st % 8):48 + 2 * (st % 8) + 1],
                    cx.mvb[:, 40 + st % 8:40 + st % 8 + 1], ALU.subtract, ALU.mult),
                    reads=(kdst, ("mvb",)), writes=(kdst,),
                    tag=f"{pref}_hn{h}_{kdst[0]}_{st}")
                P.t(lambda e, o=cx.ps_tr, i_=dst, st=st: e.transpose(
                    o[:, (st % 8) * 128:(st % 8 + 1) * 128],
                    i_[:, st * 128:(st + 1) * 128], cx.ident[:, :]),
                    reads=(kdst, ("ident",)), writes=(("ps_tr",),),
                    tag=f"{pref}_ht{h}_{kdst[0]}_{st}")
                P.v(lambda e, o=tbuf, i_=cx.ps_tr, st=st: e.tensor_scalar(
                    o[:, st * 128:(st + 1) * 128], i_[:, (st % 8) * 128:(st % 8 + 1) * 128],
                    cx.acol[:, 34:35], cx.acol[:, 35:36], ALU.mult, ALU.add),
                    reads=(("ps_tr",), ("acol",)), writes=(ktb,),
                    tag=f"{pref}_hte{h}_{kdst[0]}_{st}")
        for c in range(4):
            for dt in range(ND):
                P.t(lambda e, o=cx.ps_mm, w=cx.wv_sb[par], x=xlt, dt=dt, c=c,
                    st=(dt == 0), sp=(dt == ND - 1): e.matmul(
                    o[:, 0:512], w[:, dt * 128:(dt + 1) * 128],
                    x[:, dt, c * 512:(c + 1) * 512], start=st, stop=sp),
                    reads=(("wv_sb", par),) + XLN_KEYS, writes=(("ps_mm",),),
                    tag=f"{pref}_v{h}_{c}_{dt}")
            P.v(lambda e, o=cx.vtt[par], i_=cx.ps_mm: e.tensor_scalar(
                o[:, :], i_[:, 0:512], cx.acol[:, 38:39], None, ALU.add),
                reads=(("ps_mm",), ("acol",)), writes=(("res_sb", par),),
                tag=f"{pref}_ve{h}_{c}")
            for j in range(4):
                P.t(lambda e, o=cx.ps_tr, i_=cx.vtt[par], j=j: e.transpose(
                    o[:, j * 128:(j + 1) * 128], i_[:, j * 128:(j + 1) * 128],
                    cx.ident[:, :]),
                    reads=(("res_sb", par), ("ident",)), writes=(("ps_tr",),),
                    tag=f"{pref}_vtr{h}_{c}_{j}")
            P.v(lambda e, o=cx.vt[par], i_=cx.ps_tr, c=c: e.tensor_copy(
                o[:, c * 512:(c + 1) * 512], i_[:, 0:512]),
                reads=(("ps_tr",),), writes=(("vt", par),), tag=f"{pref}_vte{h}_{c}")
        attn_core(cx, pref, h, par, causal=False, mask_sb=None)
        P.dma("s", lambda e, d=att, s=cx.ot[par], h=h: e.dma_start(
            d[h * 128:(h + 1) * 128, :], s[:, :]),
            reads=(("ot", par),), writes=(("ATT", h),), tag=f"{pref}_ost{h}",
            ch=f"att{par}")
        att_keys.append(("ATT", h))
    P.dma("p", lambda e, d=cx.rowbf, s=bo_v: e.dma_start(
        d, s.rearrange("(a n) -> a n", a=1)),
        reads=(), writes=(("ot", 0),), tag=f"{pref}_bold", ch="smw2")
    emit_out_proj(cx, pref, att, wo, xres, bias_row=cx.rowbf,
                  att_keys=tuple(att_keys), bias_key=("ot", 0))



def rope64_nat(cx, buf, key, tab, tkey, tmps, tmpkey, pref, tag):
    P = cx.P
    v = buf.rearrange("p (t f) -> p t f", t=NT)
    A = v[:, :, 0:32]
    B = v[:, :, 32:64]
    ct, st_ = tab
    cv = ct.rearrange("p (t j) -> p t j", t=NT)
    sv = st_.rearrange("p (t j) -> p t j", t=NT)
    cA, cB = cv[:, :, 0:32], cv[:, :, 32:64]
    sA, sB = sv[:, :, 0:32], sv[:, :, 32:64]
    t1, t2 = tmps
    tks = tuple(tkey)
    P.v(lambda e: e.tensor_tensor(t1[:, :, :], A[:, :, :], sB[:, :, :], ALU.mult),
        reads=(key,) + tks, writes=(tmpkey,), tag=f"{pref}_{tag}p1")
    P.v(lambda e: e.tensor_tensor(A[:, :, :], A[:, :, :], cA[:, :, :], ALU.mult),
        reads=(key,) + tks, writes=(key,), tag=f"{pref}_{tag}p2")
    P.v(lambda e: e.tensor_tensor(t2[:, :, :], B[:, :, :], sA[:, :, :], ALU.mult),
        reads=(key,) + tks, writes=(tmpkey,), tag=f"{pref}_{tag}p3")
    P.v(lambda e: e.tensor_tensor(A[:, :, :], A[:, :, :], t2[:, :, :], ALU.subtract),
        reads=(key, tmpkey), writes=(key,), tag=f"{pref}_{tag}p4")
    P.v(lambda e: e.tensor_tensor(B[:, :, :], B[:, :, :], cB[:, :, :], ALU.mult),
        reads=(key,) + tks, writes=(key,), tag=f"{pref}_{tag}p5")
    P.v(lambda e: e.tensor_tensor(B[:, :, :], B[:, :, :], t1[:, :, :], ALU.add),
        reads=(key, tmpkey), writes=(key,), tag=f"{pref}_{tag}p6")


def trans16(cx, src, skey, dst, dkey, pref, tag, scale=None):
    P = cx.P
    for k in range(ND):
        P.t(lambda e, o=cx.ps_tr, i_=src, k=k: e.transpose(
            o[:, k * 128:(k + 1) * 128], i_[:, k * 128:(k + 1) * 128],
            cx.ident[:, :]),
            reads=(skey, ("ident",)), writes=(("ps_tr",),), tag=f"{pref}_{tag}t{k}")
    if scale is None:
        P.v(lambda e, o=dst, i_=cx.ps_tr: e.tensor_copy(o[:, :], i_[:, :]),
            reads=(("ps_tr",),), writes=(dkey,), tag=f"{pref}_{tag}e")
    else:
        P.v(lambda e, o=dst, i_=cx.ps_tr, sc=scale: e.tensor_scalar(
            o[:, :], i_[:, :], float(sc), None, ALU.mult),
            reads=(("ps_tr",),), writes=(dkey,), tag=f"{pref}_{tag}e")


def stage_wT_pe(cx, pref, w_dram, row0, stage, skey, dst, dkey, tag, ch,
                nrow=128, col0=0, ncols=D):
    P = cx.P
    nd = ncols // 128
    P.dma("p", lambda e, d=stage, s=w_dram: e.dma_start(
        d[0:nrow, 0:ncols], s[row0:row0 + nrow, col0:col0 + ncols]),
        reads=(), writes=(skey,), tag=f"{tag}_ld", ch=ch)
    for k in range(nd):
        P.t(lambda e, o=cx.ps_tr, i_=stage, k=k, nrow=nrow: e.transpose(
            o[:, (k % 8) * 128:(k % 8) * 128 + nrow],
            i_[0:nrow, 128 * k:128 * (k + 1)], cx.ident[:, :]),
            reads=(skey, ("ident",)), writes=(("ps_tr",),), tag=f"{tag}_t{k}")
        P.v(lambda e, o=dst, i_=cx.ps_tr, k=k, nrow=nrow: e.tensor_copy(
            o[:, k * nrow:(k + 1) * nrow], i_[:, (k % 8) * 128:(k % 8) * 128 + nrow]),
            reads=(("ps_tr",),), writes=(dkey,), tag=f"{tag}_e{k}")


def emit_l1_attn_a(cx, xres):
    P = cx.P
    pref = "a1"
    wq = cx.din("L1_attn_a_W_q", (D, D), F32)
    wkv = cx.din("L1_attn_a_W_kv", (256, D), F32)
    wkh = cx.din("L1_attn_a_W_k_heads", (16, 128, 128), F32)
    wvh = cx.din("L1_attn_a_W_v_heads", (16, 128, 128), F32)
    wo = cx.din("L1_attn_a_W_o", (D, D), F32)
    cosP = cx.din("c_l1a_cos", (128, NT * 64), F32)
    sinP = cx.din("c_l1a_sin", (128, NT * 64), F32)
    att = cx.dram["ATT"]

    emit_ln_phase(cx, xres, pref)
    xlt = cx.xlnT.rearrange("p (dt s) -> p dt s", dt=ND)

    cosN = cx.xnat[0].bitcast(F32)
    sinN = cx.xnat[1].bitcast(F32)
    P.dma("y", lambda e, d=cosN, s=cosP: e.dma_start(d[:, :], s[:, :]),
          reads=(), writes=(("xnat", 0),), tag=f"{pref}_cosN", ch="lnbuf0")
    P.dma("y", lambda e, d=sinN, s=sinP: e.dma_start(d[:, :], s[:, :]),
          reads=(), writes=(("xnat", 1),), tag=f"{pref}_sinN", ch="lnbuf1")
    tmps = (cx.lnbuf[1][:, 0:512].rearrange("p (t j) -> p t j", t=NT),
            cx.lnbuf[1][:, 512:1024].rearrange("p (t j) -> p t j", t=NT))

    for wsrc, dstc, tag in ((wkh, cx.wkcol, "wk"), (wvh, cx.wvcol, "wv")):
        for h in range(16):
            stg = cx.lnbuf[0][:, 0:128]
            P.dma("y", lambda e, d=stg, s=wsrc, h=h: e.dma_start(d, s[h, :, :]),
                  reads=(), writes=(("lnbuf", 0),), tag=f"{pref}_{tag}h{h}",
                  ch="lnbuf0")
            P.v(lambda e, o=dstc, i_=stg, h=h: e.reduce_sum(
                o[:, h:h + 1], i_, axis=AX.X),
                reads=(("lnbuf", 0),), writes=((tag + "col",),),
                tag=f"{pref}_{tag}s{h}")

    stage_wT_pe(cx, pref, wkv, 0, cx.qt[0], ("qt", 0), cx.wk_sb[0], ("wk_sb", 0),
                f"{pref}_kc", ch="stg0")
    stage_wT_pe(cx, pref, wkv, 128, cx.qt[1], ("qt", 1), cx.wv_sb[0], ("wv_sb", 0),
                f"{pref}_vc", ch="stg1")
    kcT = cx.at_sb[0]
    kkc = ("at_sb", 0)
    vcT = cx.at_sb[1]
    kvc = ("at_sb", 1)
    for (wbuf, wkey, dst, dkey, tg) in ((cx.wk_sb[0], ("wk_sb", 0), kcT, kkc, "kc"),
                                        (cx.wv_sb[0], ("wv_sb", 0), vcT, kvc, "vc")):
        for c in range(4):
            for dt in range(ND):
                P.t(lambda e, o=cx.ps_mm, w=wbuf, x=xlt, dt=dt, c=c,
                    st=(dt == 0), sp=(dt == ND - 1): e.matmul(
                    o[:, 0:512], w[:, dt * 128:(dt + 1) * 128],
                    x[:, dt, c * 512:(c + 1) * 512], start=st, stop=sp),
                    reads=(wkey,) + XLN_KEYS, writes=(("ps_mm",),),
                    tag=f"{pref}_{tg}{c}_{dt}")
            P.s(lambda e, o=dst, i_=cx.ps_mm, c=c: e.copy(
                out=o[:, c * 512:(c + 1) * 512], in_=i_[:, 0:512]),
                reads=(("ps_mm",),), writes=(dkey,), tag=f"{pref}_{tg}e{c}")
    for c in range(4):
        for j in range(4):
            P.t(lambda e, o=cx.ps_tr, i_=vcT, c=c, j=j: e.transpose(
                o[:, j * 128:(j + 1) * 128],
                i_[:, (c * 4 + j) * 128:(c * 4 + j + 1) * 128], cx.ident[:, :]),
                reads=(kvc, ("ident",)), writes=(("ps_tr",),),
                tag=f"{pref}_vn{c}_{j}")
        P.v(lambda e, o=cx.vt[0], i_=cx.ps_tr, c=c: e.tensor_copy(
            o[:, c * 512:(c + 1) * 512], i_[:, 0:512]),
            reads=(("ps_tr",),), writes=(("vt", 0),), tag=f"{pref}_vne{c}")

    att_keys = []
    for h in range(16):
        par = h % 2
        stage_wT_pe(cx, pref, wq, h * 128, cx.ot[par], ("ot", par),
                    cx.wq_sb[par], ("wq_sb", par), f"{pref}_wq{h}_{par}",
                    ch=f"stg{par}")
        for c in range(4):
            for dt in range(ND):
                P.t(lambda e, o=cx.ps_mm, w=cx.wq_sb[par], x=xlt, dt=dt, c=c,
                    st=(dt == 0), sp=(dt == ND - 1): e.matmul(
                    o[:, 0:512], w[:, dt * 128:(dt + 1) * 128],
                    x[:, dt, c * 512:(c + 1) * 512], start=st, stop=sp),
                    reads=(("wq_sb", par),) + XLN_KEYS, writes=(("ps_mm",),),
                    tag=f"{pref}_q{h}_{c}_{dt}")
            P.s(lambda e, o=cx.qt[par], i_=cx.ps_mm, c=c: e.copy(
                out=o[:, c * 512:(c + 1) * 512], in_=i_[:, 0:512]),
                reads=(("ps_mm",),), writes=(("qt", par),), tag=f"{pref}_qe{h}_{c}")
        trans16(cx, cx.qt[par], ("qt", par), cx.at_sb[1], ("at_sb", 1),
                pref, f"qn{h}")
        rope64_nat(cx, cx.at_sb[1], ("at_sb", 1), (cosN, sinN),
                   (("xnat", 0), ("xnat", 1)), tmps, ("lnbuf", 1),
                   pref, f"rq{h}")
        trans16(cx, cx.at_sb[1], ("at_sb", 1), cx.qt[par], ("qt", par),
                pref, f"qb{h}", scale=128.0 ** -0.5)
        P.v(lambda e, o=cx.kt[par], i_=kcT, h=h: e.tensor_scalar(
            o[:, :], i_[:, :], cx.wkcol[:, h:h + 1], None, ALU.mult),
            reads=(kkc, ("wkcol",)), writes=(("kt", par),), tag=f"{pref}_kw{h}")
        trans16(cx, cx.kt[par], ("kt", par), cx.at_sb[1], ("at_sb", 1),
                pref, f"kn{h}")
        rope64_nat(cx, cx.at_sb[1], ("at_sb", 1), (cosN, sinN),
                   (("xnat", 0), ("xnat", 1)), tmps, ("lnbuf", 1),
                   pref, f"rk{h}")
        trans16(cx, cx.at_sb[1], ("at_sb", 1), cx.kt[par], ("kt", par),
                pref, f"kb{h}")
        attn_core(cx, pref, h, par, causal=True, mask_sb=cx.maskinf,
                  bufs=((cx.qt[par], ("qt", par)), (cx.kt[par], ("kt", par)),
                        (cx.vt[0], ("vt", 0)), (cx.ot[par], ("ot", par))),
                  oscale=(cx.wvcol[:, h:h + 1], ("wvcol",)))
        P.dma("s", lambda e, d=att, s=cx.ot[par], h=h: e.dma_start(
            d[h * 128:(h + 1) * 128, :], s[:, :]),
            reads=(("ot", par),), writes=(("ATT", h),), tag=f"{pref}_ost{h}",
            ch=f"att{par}")
        att_keys.append(("ATT", h))
    emit_out_proj(cx, pref, att, wo, xres, wT="pe", att_keys=tuple(att_keys))



def softmax_free_tile(cx, pref, tag, buf, key, dst, dkey):
    P = cx.P
    sm = cx.sm[0]
    ksm = ("sm", 0)
    P.v(lambda e, o=sm, i_=buf: e.reduce_max(
        o[:, 0:1], i_[:, :], axis=AX.X, negate=True),
        reads=(key,), writes=(ksm,), tag=f"{pref}_{tag}mx")
    P.s(lambda e, o=dst, i_=buf, sm=sm: e.activation(
        o[:, :], i_[:, :], AF.Exp, bias=sm[:, 0:1], accum_out=sm[:, 2:3]),
        reads=(key, ksm), writes=(dkey, ksm), tag=f"{pref}_{tag}e1")
    P.s(lambda e, o=sm: e.activation(o[:, 3:4], o[:, 2:3], AF.Ln,
        bias=cx.cbias[:, 0:1]),
        reads=(ksm, ("cbias",)), writes=(ksm,), tag=f"{pref}_{tag}ln")
    P.v(lambda e, o=sm: e.tensor_tensor(o[:, 4:5], o[:, 0:1], o[:, 3:4],
                                        ALU.subtract),
        reads=(ksm,), writes=(ksm,), tag=f"{pref}_{tag}b2")
    P.s(lambda e, o=dst, i_=buf, sm=sm: e.activation(
        o[:, :], i_[:, :], AF.Exp, bias=sm[:, 4:5]),
        reads=(key, ksm), writes=(dkey,), tag=f"{pref}_{tag}e2")


def emit_l1_ffn(cx, xres, app):
    P = cx.P
    pref = f"m1{app}"
    gamma = cx.din("L1_ffn_gamma", (D,), F32)
    beta = cx.din("L1_ffn_beta", (D,), F32)
    mu = cx.din("L1_ffn_mu", (D, 16, 128), F32)
    scale_in = cx.din("L1_ffn_scale", (1,), F32)
    w1 = cx.din("L1_ffn_W1", (16, D, 512), F32)
    b1 = cx.din("L1_ffn_b1", (16, 512), F32)
    w2 = cx.din("L1_ffn_W2", (16, 512, D), F32)
    b2 = cx.din("L1_ffn_b2", (16, D), F32)
    mu2 = mu.rearrange("d e s -> d (e s)")
    lgn, dspn = cx.dram["LGN"], cx.dram["DSPN"]
    cmbt, slin, slout = cx.dram["CMBT"], cx.dram["SLIN"], cx.dram["SLOT_OUT"]
    xlnd = cx.dram["XLN"]

    emit_ln_phase(cx, xres, pref, inner_ln=True)
    if not getattr(cx, "b1col_ready", False):
        cx.b1col_ready = True
        load_cols(cx, b1.rearrange("e (a f) -> (e a) f", a=4), 64,
                  cx.b1col, 0, ("b1col",), f"{pref}_b1c")
    for vec, col0, tag in ((gamma, 0, "g"), (beta, 16, "b")):
        load_cols(cx, vec.rearrange("(a f) -> a f", a=16), 16, cx.acol,
                  col0, ("acol",), f"{pref}_{tag}c")
    P.dma("y", lambda e, d=cx.rowbuf, s=scale_in: e.dma_start(
        d[0:1, 0:1], s.rearrange("(a n) -> a n", a=1)),
        reads=(), writes=(("res_sb", 0),), tag=f"{pref}_scr", ch="rowld")
    P.t(lambda e, o=cx.ps_o: e.matmul(
        o[:, 0:1], cx.ones_f32[0:1, 0:128], cx.rowbuf[0:1, 0:1],
        start=True, stop=True),
        reads=(("ones_f32",), ("res_sb", 0)), writes=(("ps_o",),),
        tag=f"{pref}_scm")
    P.v(lambda e, o=cx.acol, i_=cx.ps_o: e.tensor_copy(o[:, 32:33], i_[:, 0:1]),
        reads=(("ps_o",),), writes=(("acol",),), tag=f"{pref}_sce")
    xlt = cx.xlnT.rearrange("p (dt s) -> p dt s", dt=ND)
    for dt in range(ND):
        P.v(lambda e, x=xlt, dt=dt: e.tensor_scalar(
            x[:, dt, :], x[:, dt, :], cx.acol[:, dt:dt + 1],
            cx.acol[:, 16 + dt:17 + dt], ALU.mult, ALU.add),
            reads=tuple(XLN_KEYS) + (("acol",),), writes=XLN_KEYS,
            tag=f"{pref}_gb{dt}")
    for n in range(NT):
        par = n % 2
        xnat = cx.xnat[par]
        kx = ("xnat", par)
        for k in range(ND):
            P.t(lambda e, o=cx.ps_tr, x=xlt, k=k, n=n: e.transpose(
                o[:, k * 128:(k + 1) * 128], x[:, k, n * 128:(n + 1) * 128],
                cx.ident[:, :]),
                reads=XLN_KEYS + (("ident",),), writes=(("ps_tr",),),
                tag=f"{pref}_nt{n}_{k}")
        P.v(lambda e, o=xnat, i_=cx.ps_tr: e.tensor_copy(o[:, :], i_[:, :]),
            reads=(("ps_tr",),), writes=(kx,), tag=f"{pref}_ne{n}")
        P.dma("s", lambda e, d=xlnd, s=xnat, n=n: e.dma_start(
            d[n * 128:(n + 1) * 128, :], s[:, :]),
            reads=(kx,), writes=(("XLN", n),), tag=f"{pref}_xs{n}",
            ch=f"spill{par}")
        lb = cx.lnbuf[par]
        kl = ("lnbuf", par)
        P.s(lambda e, o=lb, i_=xnat, n=n: e.activation(
            o[:, :], i_[:, :], AF.Square, accum_out=cx.mnorm[:, n:n + 1]),
            reads=(kx,), writes=(kl, ("mnorm",)), tag=f"{pref}_sq{n}")
        P.s(lambda e, o=cx.mvb, n=n, par=par: e.activation(
            o[:, 32 + par:33 + par], cx.mnorm[:, n:n + 1], AF.Ln,
            bias=cx.cbias[:, 2:3]),
            reads=(("mnorm",), ("cbias",)), writes=(("mvb",),),
            tag=f"{pref}_nl{n}")
        P.s(lambda e, o=cx.mvb, par=par: e.activation(
            o[:, 34 + par:35 + par], o[:, 32 + par:33 + par], AF.Exp, scale=0.5,
            bias=cx.cbias[:, 0:1]),
            reads=(("mvb",), ("cbias",)), writes=(("mvb",),),
            tag=f"{pref}_nsq{n}")
        P.v(lambda e, o=cx.mvb, par=par: e.tensor_scalar(
            o[:, 36 + par:37 + par], o[:, 34 + par:35 + par], 1e-12, None,
            ALU.max),
            reads=(("mvb",),), writes=(("mvb",),), tag=f"{pref}_nm{n}")
        P.v(lambda e, o=cx.mvb, par=par: e.reciprocal(
            o[:, 38 + par:39 + par], o[:, 36 + par:37 + par]),
            reads=(("mvb",),), writes=(("mvb",),), tag=f"{pref}_nr{n}")
        P.v(lambda e, o=cx.mvb, par=par: e.tensor_scalar(
            o[:, 40 + par:41 + par], o[:, 38 + par:39 + par],
            cx.acol[:, 32:33], None, ALU.mult),
            reads=(("mvb",), ("acol",)), writes=(("mvb",),), tag=f"{pref}_nf{n}")
        xnb = lb.bitcast(BF16)[:, 0:2048]
        P.v(lambda e, o=xnb, i_=xnat, par=par: e.tensor_scalar(
            o[:, :], i_[:, :], cx.mvb[:, 40 + par:41 + par], None, ALU.mult),
            reads=(kx, ("mvb",)), writes=(kl,), tag=f"{pref}_xn{n}")
        for k in range(ND):
            P.t(lambda e, o=cx.ps_tr, i_=xnb, k=k: e.transpose(
                o[:, k * 128:(k + 1) * 128], i_[:, k * 128:(k + 1) * 128],
                cx.ident[:, :]),
                reads=(kl, ("ident",)), writes=(("ps_tr",),),
                tag=f"{pref}_xt{n}_{k}")
        P.v(lambda e, o=xlt, i_=cx.ps_tr.rearrange("p (dt s) -> p dt s", dt=ND),
            n=n: e.tensor_copy(o[:, :, n * 128:(n + 1) * 128], i_[:, :, :]),
            reads=(("ps_tr",),), writes=XLN_KEYS, tag=f"{pref}_xte{n}")

    for dt in range(ND):
        par = dt % 2
        msb = cx.qt[par]
        km = ("qt", par)
        P.dma("p", lambda e, d=msb, s=mu2, dt=dt: e.dma_start(
            d[:, :], s[dt * 128:(dt + 1) * 128, :]),
            reads=(), writes=(km,), tag=f"{pref}_mu{dt}", ch=f"mul{par}")
        sq = cx.kt[par]
        ksq = ("kt", par)
        P.s(lambda e, o=sq, i_=msb: e.activation(o[:, :], i_[:, :], AF.Square),
            reads=(km,), writes=(ksq,), tag=f"{pref}_musq{dt}")
        for c in range(4):
            P.t(lambda e, o=cx.ps_sc, i_=sq, c=c, dt=dt, st=(dt == 0),
                sp=(dt == ND - 1): e.matmul(
                o[0:1, c * 512:(c + 1) * 512], cx.onescol[:, 0:1],
                i_[:, c * 512:(c + 1) * 512], start=st, stop=sp),
                reads=(ksq, ("onescol",)), writes=PS_SC_KEYS,
                tag=f"{pref}_mus{dt}_{c}")
    for c in range(4):
        P.v(lambda e, o=cx.rowbuf, i_=cx.ps_sc, c=c: e.tensor_copy(
            o, i_[0:1, c * 512:(c + 1) * 512]),
            reads=PS_SC_KEYS, writes=(("res_sb", 0),), tag=f"{pref}_mne{c}")
        row_to_cols(cx, cx.rowbuf, ("res_sb", 0), 512, cx.mvb, c * 4,
                    ("mvb",), f"{pref}_mnc{c}")
    P.s(lambda e, o=cx.mvb: e.activation(o[:, 16:32], o[:, 0:16], AF.Ln,
        bias=cx.cbias[:, 2:3]),
        reads=(("mvb",), ("cbias",)), writes=(("mvb",),), tag=f"{pref}_cl")
    P.s(lambda e, o=cx.mvb: e.activation(o[:, 16:32], o[:, 16:32], AF.Exp,
        scale=0.5, bias=cx.cbias[:, 0:1]),
        reads=(("mvb",), ("cbias",)), writes=(("mvb",),), tag=f"{pref}_ce")
    P.v(lambda e, o=cx.mvb: e.tensor_scalar(
        o[:, 16:32], o[:, 16:32], 1e-12, None, ALU.max),
        reads=(("mvb",),), writes=(("mvb",),), tag=f"{pref}_cm")
    P.v(lambda e, o=cx.mvb: e.reciprocal(o[:, 16:32], o[:, 16:32]),
        reads=(("mvb",),), writes=(("mvb",),), tag=f"{pref}_cr")

    for slt in range(16):
        par = slt % 2
        musl = cx.wq_sb[par]
        kmu = ("wq_sb", par)
        msrc = mu2.rearrange("(dt p) c -> p dt c", p=128)[
            :, :, slt * 128:(slt + 1) * 128]
        P.dma("p", lambda e, d=musl, s=msrc: e.dma_start(
            d.rearrange("p (dt c) -> p dt c", dt=ND)[:, :, :], s),
            reads=(), writes=(kmu,), tag=f"{pref}_ms{slt}", ch=f"wq{par}")
        lgt = cx.qt[par]
        klg = ("qt", par)
        for c in range(4):
            for dt in range(ND):
                P.t(lambda e, o=cx.ps_mm, w=musl, x=xlt, dt=dt, c=c,
                    st=(dt == 0), sp=(dt == ND - 1): e.matmul(
                    o[:, 0:512], w[:, dt * 128:(dt + 1) * 128],
                    x[:, dt, c * 512:(c + 1) * 512], start=st, stop=sp),
                    reads=(kmu,) + XLN_KEYS, writes=(("ps_mm",),),
                    tag=f"{pref}_lg{slt}_{c}_{dt}")
            P.v(lambda e, o=lgt, i_=cx.ps_mm, c=c, slt=slt: e.tensor_scalar(
                o[:, c * 512:(c + 1) * 512], i_[:, 0:512],
                cx.mvb[:, 16 + slt:17 + slt], None, ALU.mult),
                reads=(("ps_mm",), ("mvb",)), writes=(klg,),
                tag=f"{pref}_lge{slt}_{c}")
        dsp = cx.kt[par]
        kdsp = ("kt", par)
        softmax_free_tile(cx, pref, f"d{slt}", lgt, klg, dsp, kdsp)
        for (tile, tkey, stg, skey, dram, tg) in (
                (dsp, kdsp, cx.vt[0], ("vt", 0), dspn, "dt"),
                (lgt, klg, cx.vt[1], ("vt", 1), lgn, "lt")):
            for k in range(ND):
                P.t(lambda e, o=cx.ps_tr, i_=tile, k=k: e.transpose(
                    o[:, k * 128:(k + 1) * 128], i_[:, k * 128:(k + 1) * 128],
                    cx.ident[:, :]),
                    reads=(tkey, ("ident",)), writes=(("ps_tr",),),
                    tag=f"{pref}_{tg}t{slt}_{k}")
            P.v(lambda e, o=stg, i_=cx.ps_tr: e.tensor_copy(o[:, :], i_[:, :]),
                reads=(("ps_tr",),), writes=(skey,), tag=f"{pref}_{tg}e{slt}")
            dst = dram.rearrange("(nt p) c -> p nt c", p=128)[
                :, :, slt * 128:(slt + 1) * 128]
            P.dma("s", lambda e, d=dst, s=stg: e.dma_start(
                d, s.rearrange("p (nt c) -> p nt c", nt=NT)[:, :, :]),
                reads=(skey,), writes=((tg.upper(), slt),),
                tag=f"{pref}_{tg}s{slt}", ch=f"lst{tg}")
    dsp_keys = tuple(("DT", s_) for s_ in range(16))
    lgn_keys = tuple(("LT", s_) for s_ in range(16))

    for nt in range(NT):
        par = nt % 2
        lgrow = cx.qt[par]
        klr = ("qt", par)
        P.dma("y", lambda e, d=lgrow, s=lgn, nt=nt: e.dma_start(
            d[:, :], s[nt * 128:(nt + 1) * 128, :]),
            reads=lgn_keys, writes=(klr,), tag=f"{pref}_cbl{nt}", ch=f"cb{par}")
        cmb = cx.kt[par]
        kcm = ("kt", par)
        softmax_free_tile(cx, pref, f"c{nt}", lgrow, klr, cmb, kcm)
        for k in range(ND):
            P.t(lambda e, o=cx.ps_tr, i_=cmb, k=k: e.transpose(
                o[:, k * 128:(k + 1) * 128], i_[:, k * 128:(k + 1) * 128],
                cx.ident[:, :]),
                reads=(kcm, ("ident",)), writes=(("ps_tr",),),
                tag=f"{pref}_cbt{nt}_{k}")
        stg = cx.vt[0]
        P.v(lambda e, o=stg, i_=cx.ps_tr: e.tensor_copy(o[:, :], i_[:, :]),
            reads=(("ps_tr",),), writes=(("vt", 0),), tag=f"{pref}_cbe{nt}")
        dst = cmbt.rearrange("(st p) c -> p st c", p=128)[
            :, :, nt * 128:(nt + 1) * 128]
        P.dma("s", lambda e, d=dst, s=stg: e.dma_start(
            d, s.rearrange("p (st c) -> p st c", st=NT)[:, :, :]),
            reads=(("vt", 0),), writes=(("CMBT", nt),), tag=f"{pref}_cbs{nt}",
            ch="lstdt")
    cmb_keys = tuple(("CMBT", n_) for n_ in range(NT))

    xl_half = cx.xlnT.rearrange("p (h nt c) -> p h nt c", h=2, nt=NT)
    for dh in range(2):
        xsrc = xlnd.rearrange("(nt p) c -> p nt c", p=128)[
            :, :, dh * 1024:(dh + 1) * 1024]
        P.dma("y", lambda e, d=xl_half, s=xsrc, dh=dh: e.dma_start(
            d[:, 0, :, :], s),
            reads=tuple(("XLN", n_) for n_ in range(NT)),
            writes=(XLN_KEYS[0],), tag=f"{pref}_sxl{dh}", ch="panel")
        for sh in range(2):
            dsrc = dspn.rearrange("(nt p) c -> p nt c", p=128)[
                :, :, sh * 1024:(sh + 1) * 1024]
            P.dma("y", lambda e, d=xl_half, s=dsrc, sh=sh: e.dma_start(
                d[:, 1, :, :], s),
                reads=dsp_keys, writes=(XLN_KEYS[1],), tag=f"{pref}_sdl{dh}_{sh}",
                ch="panel2")
            for dt2 in range(8):
                for c2 in range(2):
                    for nt in range(NT):
                        P.t(lambda e, o=cx.ps_mm, x=xl_half, nt=nt, dt2=dt2,
                            c2=c2, st=(nt == 0), sp=(nt == NT - 1): e.matmul(
                            o[:, 0:512], x[:, 0, nt, dt2 * 128:(dt2 + 1) * 128],
                            x[:, 1, nt, c2 * 512:(c2 + 1) * 512],
                            start=st, stop=sp),
                            reads=(XLN_KEYS[0], XLN_KEYS[1]),
                            writes=(("ps_mm",),),
                            tag=f"{pref}_si{dh}_{sh}_{dt2}_{c2}_{nt}")
                    stg2 = cx.at_sb[0]
                    P.v(lambda e, o=stg2, i_=cx.ps_mm, c2=c2: e.tensor_copy(
                        o[:, c2 * 512:(c2 + 1) * 512], i_[:, 0:512]),
                        reads=(("ps_mm",),), writes=(("at_sb", 0),),
                        tag=f"{pref}_sie{dh}_{sh}_{dt2}_{c2}")
                P.dma("s", lambda e, d=slin, s=stg2, dh=dh, sh=sh, dt2=dt2:
                      e.dma_start(d[dh * 1024 + dt2 * 128:dh * 1024 + (dt2 + 1) * 128,
                                    sh * 1024:(sh + 1) * 1024], s[:, 0:1024]),
                      reads=(("at_sb", 0),), writes=(("SLIN",),),
                      tag=f"{pref}_sis{dh}_{sh}_{dt2}", ch="lstlt")

    for e_ in range(16):
        par = e_ % 2
        slsb = cx.qt[par]
        ksl = ("qt", par)
        ssrc = slin.rearrange("(dt p) c -> p dt c", p=128)[
            :, :, e_ * 128:(e_ + 1) * 128]
        P.dma("y", lambda e, d=slsb, s=ssrc: e.dma_start(
            d.rearrange("p (dt c) -> p dt c", dt=ND)[:, :, :], s),
            reads=(("SLIN",),), writes=(ksl,), tag=f"{pref}_esl{e_}",
            ch=f"cb{par}")
        P.dma("p", lambda e, d=cx.wo_sb, s=w1, e_=e_: e.dma_start(
            d.rearrange("p (dt c) -> p dt c", dt=ND)[:, :, :],
            s.rearrange("e (dt p) c -> p e dt c", p=128)[:, e_, :, :]),
            reads=(), writes=(("wo_sb",),), tag=f"{pref}_ew1{e_}", ch="wo")
        hbuf = cx.kt[par]
        kh = ("kt", par)
        for ht in range(4):
            for dt in range(ND):
                P.t(lambda e, o=cx.ps_mm, w=cx.wo_sb, x=slsb, dt=dt, ht=ht,
                    st=(dt == 0), sp=(dt == ND - 1): e.matmul(
                    o[:, 0:128], w[:, dt * 512 + ht * 128:dt * 512 + (ht + 1) * 128],
                    x[:, dt * 128:(dt + 1) * 128], start=st, stop=sp),
                    reads=(("wo_sb",), ksl), writes=(("ps_mm",),),
                    tag=f"{pref}_eh{e_}_{ht}_{dt}")
            P.s(lambda e, o=hbuf, i_=cx.ps_mm, ht=ht, e_=e_: e.activation(
                o[:, ht * 128:(ht + 1) * 128], i_[:, 0:128], AF.Gelu,
                bias=cx.b1col[:, e_ * 4 + ht:e_ * 4 + ht + 1]),
                reads=(("ps_mm",), ("b1col",)), writes=(kh,),
                tag=f"{pref}_ehe{e_}_{ht}")
        P.dma("p", lambda e, d=cx.wdb[1], s=w2, e_=e_: e.dma_start(
            d.rearrange("p (ht c) -> p ht c", ht=4)[:, :, :],
            s.rearrange("e (ht p) c -> p e ht c", p=128)[:, e_, :, :]),
            reads=(), writes=(("wdb1",),), tag=f"{pref}_ew2{e_}", ch="wd1")
        P.dma("p", lambda e, d=cx.rowbf, s=b2, e_=e_: e.dma_start(
            d, s[e_:e_ + 1, :]),
            reads=(), writes=(("ot", 0),), tag=f"{pref}_eb2{e_}", ch="wd0")
        so_sb = cx.at_sb[1]
        kso = ("at_sb", 1)
        for dc in range(4):
            for ht in range(4):
                P.t(lambda e, o=cx.ps_mm, hb=hbuf, w=cx.wdb[1], ht=ht, dc=dc,
                    st=(ht == 0): e.matmul(
                    o[:, 0:512], hb[:, ht * 128:(ht + 1) * 128],
                    w[:, ht * 2048 + dc * 512:ht * 2048 + (dc + 1) * 512],
                    start=st, stop=False),
                    reads=(kh, ("wdb1",)), writes=(("ps_mm",),),
                    tag=f"{pref}_eo{e_}_{dc}_{ht}")
            P.t(lambda e, o=cx.ps_mm, dc=dc: e.matmul(
                o[:, 0:512], cx.ones_bf[0:1, 0:128],
                cx.rowbf[:, dc * 512:(dc + 1) * 512], start=False, stop=True),
                reads=(("ones_bf",), ("ot", 0)), writes=(("ps_mm",),),
                tag=f"{pref}_eob{e_}_{dc}")
            P.v(lambda e, o=so_sb, i_=cx.ps_mm, dc=dc: e.tensor_copy(
                o[:, dc * 512:(dc + 1) * 512], i_[:, 0:512]),
                reads=(("ps_mm",),), writes=(kso,), tag=f"{pref}_eoe{e_}_{dc}")
        P.dma("s", lambda e, d=slout, s=so_sb, e_=e_: e.dma_start(
            d[e_ * 128:(e_ + 1) * 128, :], s[:, :]),
            reads=(kso,), writes=(("SLOUT", e_),), tag=f"{pref}_eos{e_}",
            ch="lstlt")
    so_keys = tuple(("SLOUT", e2) for e2 in range(16))

    for dc in range(4):
        P.dma("p", lambda e, d=cx.wo_sb, s=slout, dc=dc: e.dma_start(
            d.rearrange("p (st c) -> p st c", st=ND)[:, :, :],
            s.rearrange("(st p) c -> p st c", p=128)[:, :, dc * 512:(dc + 1) * 512]),
            reads=so_keys, writes=(("wo_sb",),), tag=f"{pref}_fso{dc}", ch="wo")
        for nt in range(NT):
            par = nt % 2
            cmsb = cx.at_sb[par]
            kcmb = ("at_sb", par)
            csrc = cmbt.rearrange("(st p) c -> p st c", p=128)[
                :, :, nt * 128:(nt + 1) * 128]
            P.dma("y", lambda e, d=cmsb, s=csrc: e.dma_start(
                d.rearrange("p (st c) -> p st c", st=ND)[:, :, :], s),
                reads=cmb_keys, writes=(kcmb,), tag=f"{pref}_fcl{dc}_{nt}",
                ch=f"atl{par}")
            for st_ in range(ND):
                P.t(lambda e, o=cx.ps_mm, cm=cmsb, w=cx.wo_sb, st_=st_,
                    s0=(st_ == 0), sp=(st_ == ND - 1): e.matmul(
                    o[:, 0:512], cm[:, st_ * 128:(st_ + 1) * 128],
                    w[:, st_ * 512:(st_ + 1) * 512], start=s0, stop=sp),
                    reads=(kcmb, ("wo_sb",)), writes=(("ps_mm",),),
                    tag=f"{pref}_fm{dc}_{nt}_{st_}")
            rsb = cx.res_sb[par]
            krs = ("res_sb", par)
            P.v(lambda e, o=rsb, i_=cx.ps_mm: e.tensor_copy(o[:, :], i_[:, 0:512]),
                reads=(("ps_mm",),), writes=(krs,), tag=f"{pref}_fe{dc}_{nt}")
            P.dma("p", lambda e, d=xres, s=rsb, nt=nt, dc=dc: e.dma_start(
                d[nt * 128:(nt + 1) * 128, dc * 512:(dc + 1) * 512], s[:, :],
                accum_op=ALU.add),
                reads=(krs, ("X", nt)), writes=(("X", nt),),
                tag=f"{pref}_fa{dc}_{nt}", ch=f"res{par}")



def fp4_quant_half(cx, sv_src, skey, dv_dst, dkey, pref, tag, eng="v",
                   bufs=None):
    P = cx.P
    K = 12582912.0
    sv = sv_src.rearrange("p (b j) -> p b j", b=64)
    if bufs is None:
        if eng == "v":
            qs, kq, c0, c1 = cx.qstat, ("qstat",), 0, 64
            yb, ky = cx.at_sb[0].bitcast(F32), ("at_sb", 0)
            pi, kp = cx.ps_sc[:, 0:1024], PS_SC_KEYS[:2]
            tb, kt_ = cx.pbuf[0].bitcast(F32), ("pbuf", 0)
        else:
            qs, kq, c0, c1 = cx.colbuf, ("colbuf",), 0, 0
            yb, ky = cx.at_sb[1].bitcast(F32), ("at_sb", 1)
            pi, kp = cx.pbuf[1].bitcast(F32), (("pbuf", 1),)
            tb, kt_ = cx.ptb[0].bitcast(F32), ("ptb", 0)
    else:
        (qs, kq, c0), (yb, ky), (pi, kp), (tb, kt_) = bufs
        c1 = c0 + 64
    if not isinstance(kp, tuple) or not isinstance(kp[0], tuple):
        kp = (kp,)
    if eng == "p":
        rcs, krc, rc0 = cx.mvb, ("mvb",), 0
    else:
        rcs, krc, rc0 = qs, kq, c1
    em = P.v if eng == "v" else P.p
    P.v(lambda e, o=qs, i_=sv: e.reduce_max(
        o[:, c0:c0 + 64], i_[:, :, :], axis=AX.X, apply_absolute_value=True),
        reads=(skey,), writes=(kq,), tag=f"{pref}_{tag}am")
    P.v(lambda e, o=rcs, i_=qs: e.tensor_scalar(
        o[:, rc0:rc0 + 64], i_[:, c0:c0 + 64], 1e-30, None, ALU.max),
        reads=(kq,), writes=(krc,), tag=f"{pref}_{tag}rm")
    P.v(lambda e, o=rcs: e.reciprocal(o[:, rc0:rc0 + 64], o[:, rc0:rc0 + 64]),
        reads=(krc,), writes=(krc,), tag=f"{pref}_{tag}rc")
    yv = yb.rearrange("p (b j) -> p b j", b=64)
    rb = rcs[:, rc0:rc0 + 64].unsqueeze(2).broadcast_to([128, 64, 16])
    P.v(lambda e, o=yv, i_=sv, r=rb: e.scalar_tensor_tensor(
        o[:, :, :], i_[:, :, :], 6.0, r[:, :, :], ALU.mult, ALU.mult),
        reads=(skey, krc), writes=(ky,), tag=f"{pref}_{tag}y")
    em(lambda e, o=pi.bitcast(I32), i_=yb.bitcast(I32): e.tensor_scalar(
        o[:, :], i_[:, :], 0x7F800000, None, ALU.bitwise_and),
        reads=(ky,), writes=kp, tag=f"{pref}_{tag}p")
    em(lambda e, o=pi: e.tensor_scalar(
        o[:, :], o[:, :], 0.5, 0.5, ALU.mult, ALU.max),
        reads=kp, writes=kp, tag=f"{pref}_{tag}h")
    em(lambda e, o=tb, i_=pi, y=yb: e.scalar_tensor_tensor(
        o[:, :], i_[:, :], K, y[:, :], ALU.mult, ALU.add),
        reads=kp + (ky,), writes=(kt_,), tag=f"{pref}_{tag}t")
    em(lambda e, o=yb, i_=pi, t=tb: e.scalar_tensor_tensor(
        o[:, :], i_[:, :], -K, t[:, :], ALU.mult, ALU.add),
        reads=kp + (kt_,), writes=(ky,), tag=f"{pref}_{tag}q")
    ab = qs[:, c0:c0 + 64].unsqueeze(2).broadcast_to([128, 64, 16])
    dv = dv_dst.rearrange("p (b j) -> p b j", b=64)
    P.v(lambda e, o=dv, i_=yv, a=ab: e.scalar_tensor_tensor(
        o[:, :, :], i_[:, :, :], 1.0 / 6.0, a[:, :, :], ALU.mult, ALU.mult),
        reads=(ky, kq, krc), writes=(dkey,), tag=f"{pref}_{tag}f")


def emit_l1_attn_b(cx, xres):
    P = cx.P
    pref = "b1"
    wq = cx.din("L1_attn_b_Wq", (D, D), F32)
    wk = cx.din("L1_attn_b_Wk", (D, D), F32)
    wv = cx.din("L1_attn_b_Wv", (D, D), F32)
    wo = cx.din("L1_attn_b_Wo", (D, D), F32)
    cosB = cx.din("c_l1b_cos", (128, NT * 64), F32)
    sinB = cx.din("c_l1b_sin", (128, NT * 64), F32)
    att = cx.dram["ATT"]

    emit_ln_phase(cx, xres, pref, quant=True)

    w4t = cx.dram["W4T"]
    for wi, wsrc in enumerate((wq, wk, wv, wo)):
        for r in range(16):
            par = r % 2
            stg = cx.lnbuf[par]
            skl = ("lnbuf", par)
            P.dma("y", lambda e, d=stg, s=wsrc, r=r: e.dma_start(
                d[:, :], s[r * 128:(r + 1) * 128, :]),
                reads=(), writes=(skl,), tag=f"{pref}_pw{wi}_{r}",
                ch=f"qwl{par}")
            qt_ = cx.xnat[par]
            kqt = ("xnat", par)
            for hf in range(2):
                fp4_quant_half(cx, stg[:, hf * 1024:(hf + 1) * 1024], skl,
                               qt_[:, hf * 1024:(hf + 1) * 1024], kqt,
                               pref, f"pq{wi}_{r}_{hf}")
            for k in range(ND):
                P.t(lambda e, o=cx.ps_tr, i_=qt_, k=k: e.transpose(
                    o[:, k * 128:(k + 1) * 128], i_[:, k * 128:(k + 1) * 128],
                    cx.ident[:, :]),
                    reads=(kqt, ("ident",)), writes=(("ps_tr",),),
                    tag=f"{pref}_pt{wi}_{r}_{k}")
            stg2 = cx.wq_sb[par]
            k2 = ("wq_sb", par)
            P.v(lambda e, o=stg2, i_=cx.ps_tr: e.tensor_copy(o[:, :], i_[:, :]),
                reads=(("ps_tr",),), writes=(k2,), tag=f"{pref}_pe{wi}_{r}")
            sv = stg2.rearrange("p (k j) -> p k j", k=ND)
            dst = w4t[wi].rearrange("(dt p) c -> p dt c", p=128)[
                :, :, r * 128:(r + 1) * 128]
            P.dma("s", lambda e, d=dst, s=sv: e.dma_start(d, s[:, :, :]),
                  reads=(k2,), writes=(("W4T", wi),), tag=f"{pref}_ps{wi}_{r}",
                  ch=f"qws{par}")

    cosN = cx.xnat[0].bitcast(F32)
    sinN = cx.xnat[1].bitcast(F32)
    P.dma("y", lambda e, d=cosN, s=cosB: e.dma_start(d[:, :], s[:, :]),
          reads=(), writes=(("xnat", 0),), tag=f"{pref}_cosN", ch="lnbuf0")
    P.dma("y", lambda e, d=sinN, s=sinB: e.dma_start(d[:, :], s[:, :]),
          reads=(), writes=(("xnat", 1),), tag=f"{pref}_sinN", ch="lnbuf1")
    xlt = cx.xlnT.rearrange("p (dt s) -> p dt s", dt=ND)
    att_keys = []
    for hp in range(16):
        par = hp % 2
        for (wi, wbuf, wkey, tg) in ((0, cx.wq_sb[par], ("wq_sb", par), "q"),
                                     (1, cx.wk_sb[par], ("wk_sb", par), "k"),
                                     (2, cx.wv_sb[par], ("wv_sb", par), "v")):
            load_w_colslice(cx, "p", w4t[wi], hp * 128, 128, wbuf, wkey,
                            f"{pref}_w{tg}{hp}", ch=f"w{tg}{par}",
                            reads=(("W4T", wi),))
        for (wbuf, wkey, dst, dkey, tg, sc) in (
                (cx.wq_sb[par], ("wq_sb", par), cx.qt[par], ("qt", par), "q",
                 64.0 ** -0.5),
                (cx.wk_sb[par], ("wk_sb", par), cx.kt[par], ("kt", par), "k",
                 1.0)):
            for c in range(4):
                for dt in range(ND):
                    P.t(lambda e, o=cx.ps_mm, w=wbuf, x=xlt, dt=dt, c=c,
                        st=(dt == 0), sp=(dt == ND - 1): e.matmul(
                        o[:, 0:512], w[:, dt * 128:(dt + 1) * 128],
                        x[:, dt, c * 512:(c + 1) * 512], start=st, stop=sp),
                        reads=(wkey,) + XLN_KEYS, writes=(("ps_mm",),),
                        tag=f"{pref}_{tg}p{hp}_{c}_{dt}")
                if sc == 1.0:
                    P.s(lambda e, o=dst, i_=cx.ps_mm, c=c: e.copy(
                        out=o[:, c * 512:(c + 1) * 512], in_=i_[:, 0:512]),
                        reads=(("ps_mm",),), writes=(dkey,),
                        tag=f"{pref}_{tg}pe{hp}_{c}")
                else:
                    P.s(lambda e, o=dst, i_=cx.ps_mm, c=c, sc=sc: e.mul(
                        o[:, c * 512:(c + 1) * 512], i_[:, 0:512], float(sc)),
                        reads=(("ps_mm",),), writes=(dkey,),
                        tag=f"{pref}_{tg}pe{hp}_{c}")
        for c in range(4):
            for dt in range(ND):
                P.t(lambda e, o=cx.ps_mm, w=cx.wv_sb[par], x=xlt, dt=dt, c=c,
                    st=(dt == 0), sp=(dt == ND - 1): e.matmul(
                    o[:, 0:512], w[:, dt * 128:(dt + 1) * 128],
                    x[:, dt, c * 512:(c + 1) * 512], start=st, stop=sp),
                    reads=(("wv_sb", par),) + XLN_KEYS, writes=(("ps_mm",),),
                    tag=f"{pref}_vp{hp}_{c}_{dt}")
            P.s(lambda e, o=cx.vtt[par], i_=cx.ps_mm: e.copy(
                out=o, in_=i_[:, 0:512]),
                reads=(("ps_mm",),), writes=(("res_sb", par),),
                tag=f"{pref}_vpe{hp}_{c}")
            for j in range(4):
                P.t(lambda e, o=cx.ps_tr, i_=cx.vtt[par], j=j: e.transpose(
                    o[:, j * 128:(j + 1) * 128], i_[:, j * 128:(j + 1) * 128],
                    cx.ident[:, :]),
                    reads=(("res_sb", par), ("ident",)), writes=(("ps_tr",),),
                    tag=f"{pref}_vt{hp}_{c}_{j}")
            P.v(lambda e, o=cx.vt[par], i_=cx.ps_tr, c=c: e.tensor_copy(
                o[:, c * 512:(c + 1) * 512], i_[:, 0:512]),
                reads=(("ps_tr",),), writes=(("vt", par),),
                tag=f"{pref}_vte{hp}_{c}")
        for (tbuf, tkey) in ((cx.qt[par], ("qt", par)), (cx.kt[par], ("kt", par))):
            trans16(cx, tbuf, tkey, cx.at_sb[1], ("at_sb", 1), pref,
                    f"rn{hp}_{tkey[0]}")
            nb = cx.at_sb[1].rearrange("p (t g f) -> p t g f", t=NT, g=2)
            A = nb[:, :, :, 0:32]
            B = nb[:, :, :, 32:64]
            cv = cosN.rearrange("p (t j) -> p t j", t=NT).unsqueeze(2)
            sv = sinN.rearrange("p (t j) -> p t j", t=NT).unsqueeze(2)
            cA = cv[:, :, :, 0:32].broadcast_to([128, NT, 2, 32])
            cB = cv[:, :, :, 32:64].broadcast_to([128, NT, 2, 32])
            sA = sv[:, :, :, 0:32].broadcast_to([128, NT, 2, 32])
            sB = sv[:, :, :, 32:64].broadcast_to([128, NT, 2, 32])
            t1 = cx.lnbuf[1][:, 0:1024].rearrange("p (t g j) -> p t g j",
                                                  t=NT, g=2)
            t2 = cx.lnbuf[1][:, 1024:2048].rearrange("p (t g j) -> p t g j",
                                                     t=NT, g=2)
            kn = ("at_sb", 1)
            tk2 = (("xnat", 0), ("xnat", 1))
            ktm = ("lnbuf", 1)
            P.v(lambda e: e.tensor_tensor(t1[:, :, :, :], A[:, :, :, :],
                                          sB[:, :, :, :], ALU.mult),
                reads=(kn,) + tk2, writes=(ktm,), tag=f"{pref}_r1{hp}_{tkey[0]}")
            P.v(lambda e: e.tensor_tensor(A[:, :, :, :], A[:, :, :, :],
                                          cA[:, :, :, :], ALU.mult),
                reads=(kn,) + tk2, writes=(kn,), tag=f"{pref}_r2{hp}_{tkey[0]}")
            P.v(lambda e: e.tensor_tensor(t2[:, :, :, :], B[:, :, :, :],
                                          sA[:, :, :, :], ALU.mult),
                reads=(kn,) + tk2, writes=(ktm,), tag=f"{pref}_r3{hp}_{tkey[0]}")
            P.v(lambda e: e.tensor_tensor(A[:, :, :, :], A[:, :, :, :],
                                          t2[:, :, :, :], ALU.subtract),
                reads=(kn, ktm), writes=(kn,), tag=f"{pref}_r4{hp}_{tkey[0]}")
            P.v(lambda e: e.tensor_tensor(B[:, :, :, :], B[:, :, :, :],
                                          cB[:, :, :, :], ALU.mult),
                reads=(kn,) + tk2, writes=(kn,), tag=f"{pref}_r5{hp}_{tkey[0]}")
            P.v(lambda e: e.tensor_tensor(B[:, :, :, :], B[:, :, :, :],
                                          t1[:, :, :, :], ALU.add),
                reads=(kn, ktm), writes=(kn,), tag=f"{pref}_r6{hp}_{tkey[0]}")
            trans16(cx, cx.at_sb[1], ("at_sb", 1), tbuf, tkey, pref,
                    f"rb{hp}_{tkey[0]}")
        for g in range(2):
            attn_core(cx, pref, hp * 2 + g, par, causal=True,
                      mask_sb=cx.maskinf, dh=64,
                      qoff=64 * g, obuf=(cx.lnbuf[par], ("lnbuf", par)))
        ol = cx.lnbuf[par]
        kol = ("lnbuf", par)
        for k in range(ND):
            P.t(lambda e, o=cx.ps_sc, i_=ol, k=k: e.transpose(
                o[:, k * 128:(k + 1) * 128], i_[:, k * 128:(k + 1) * 128],
                cx.identf[:, :]),
                reads=(kol, ("identf",)), writes=PS_SC_KEYS,
                tag=f"{pref}_ont{hp}_{k}")
        P.v(lambda e, o=ol, i_=cx.ps_sc: e.tensor_copy(o[:, :], i_[:, :]),
            reads=PS_SC_KEYS, writes=(kol,), tag=f"{pref}_one{hp}")
        for hf in range(2):
            fp4_quant_half(cx, ol[:, hf * 1024:(hf + 1) * 1024], kol,
                           cx.ot[par][:, hf * 1024:(hf + 1) * 1024],
                           ("ot", par), pref, f"oq{hp}_{hf}")
        for k in range(ND):
            P.t(lambda e, o=cx.ps_tr, i_=cx.ot[par], k=k: e.transpose(
                o[:, k * 128:(k + 1) * 128], i_[:, k * 128:(k + 1) * 128],
                cx.ident[:, :]),
                reads=(("ot", par), ("ident",)), writes=(("ps_tr",),),
                tag=f"{pref}_oqt{hp}_{k}")
        P.v(lambda e, o=cx.wq_sb[par], i_=cx.ps_tr: e.tensor_copy(
            o[:, :], i_[:, :]),
            reads=(("ps_tr",),), writes=(("wq_sb", par),), tag=f"{pref}_oqe{hp}")
        P.dma("s", lambda e, d=att, s=cx.wq_sb[par], hp=hp: e.dma_start(
            d[hp * 128:(hp + 1) * 128, :], s[:, :]),
            reads=(("wq_sb", par),), writes=(("ATT", hp),),
            tag=f"{pref}_oqs{hp}", ch=f"att{par}")
        att_keys.append(("ATT", hp))
    for fc in range(4):
        load_w_colslice(cx, "p", w4t[3], fc * 512, 512, cx.wo_sb, ("wo_sb",),
                        f"{pref}_wo{fc}", ch="wo", reads=(("W4T", 3),))
        for st in range(NT):
            par2 = st % 2
            atb = cx.at_sb[0]
            katb = ("at_sb", 0)
            asrc = att.rearrange("(dt p) s -> p dt s", p=128)[
                :, :, st * 128:(st + 1) * 128]
            P.dma("y", lambda e, d=atb, s=asrc: e.dma_start(
                d.rearrange("p (dt c) -> p dt c", dt=ND)[:, :, :], s),
                reads=tuple(att_keys), writes=(katb,),
                tag=f"{pref}_atl{fc}_{st}", ch="atl0")
            for dt in range(ND):
                P.t(lambda e, o=cx.ps_mm, x=atb, w=cx.wo_sb, dt=dt, st=st,
                    s0=(dt == 0), sp=(dt == ND - 1): e.matmul(
                    o[:, 0:512], x[:, dt * 128:(dt + 1) * 128],
                    w[:, dt * 512:(dt + 1) * 512], start=s0, stop=sp),
                    reads=(katb, ("wo_sb",)), writes=(("ps_mm",),),
                    tag=f"{pref}_om{fc}_{st}_{dt}")
            rsb = cx.res_sb[par2]
            krs = ("res_sb", par2)
            P.v(lambda e, o=rsb, i_=cx.ps_mm: e.tensor_copy(o[:, :], i_[:, 0:512]),
                reads=(("ps_mm",),), writes=(krs,), tag=f"{pref}_oe{fc}_{st}")
            P.dma("p", lambda e, d=xres, s=rsb, st=st, fc=fc: e.dma_start(
                d[st * 128:(st + 1) * 128, fc * 512:(fc + 1) * 512], s[:, :],
                accum_op=ALU.add),
                reads=(krs, ("X", st)), writes=(("X", st),),
                tag=f"{pref}_oacc{fc}_{st}", ch=f"res{par2}")
